# version 03
# baseline (speedup 1.0000x reference)
import sys
sys.path.insert(0, "/opt/trn_rl_repo")

import contextlib
import numpy as np

import concourse.bass as bass
import concourse.mybir as mybir
from concourse.bass_utils import run_bass_kernel_spmd

F32 = mybir.dt.float32
BF16 = mybir.dt.bfloat16
AF = mybir.ActivationFunctionType
ALU = mybir.AluOpType
AXX = mybir.AxisListType.X

NCORES = 8
D = 2048
KB = 16
T = 8192
M = 1024
M1 = 1025
E = 8
HFF = 1408
NB = 11
EPS = 1e-5

CH_FULL = [(0, 342), (342, 342), (684, 341)]
CH_OWN = [(1, 512), (513, 512)]
MM_FULL = [(0, 512), (512, 512), (1024, 1)]
def group_list(c0, cN):
    out = []
    gi = 0
    gc = c0
    while gc < cN:
        out.append((gi, gc, min(128, cN - gc)))
        gc += 128
        gi += 1
    return out

_TAPS = []


class Sched:
    ENGINES = ("tensor", "vector", "scalar", "gpsimd", "sync")

    def __init__(self):
        self.prog = {e: [] for e in self.ENGINES}
        self.ecount = {e: 0 for e in self.ENGINES}
        self.qcount = {e: 0 for e in self.ENGINES}
        self.ccount = 0
        self.bufs = {}
        self.waited = {}
        self.deferred = {e: [] for e in self.ENGINES}

    def _buf(self, k):
        if k not in self.bufs:
            self.bufs[k] = {"w": None, "r": []}
        return self.bufs[k]

    def _need(self, eng, tag):
        kind, src, val = tag
        if kind == "e" and src == eng:
            if eng == "tensor":
                return False
            return True
        return True

    def _collect_waits(self, eng, reads, writes):
        waits = {}
        for k in reads:
            b = self._buf(k)
            if b["w"] is not None and self._need(eng, b["w"]):
                kind, src, val = b["w"]
                waits[(kind, src)] = max(waits.get((kind, src), 0), val)
        for k in writes:
            b = self._buf(k)
            if b["w"] is not None and self._need(eng, b["w"]):
                kind, src, val = b["w"]
                waits[(kind, src)] = max(waits.get((kind, src), 0), val)
            for tag in b["r"]:
                if self._need(eng, tag):
                    kind, src, val = tag
                    waits[(kind, src)] = max(waits.get((kind, src), 0), val)
        out = []
        for key, val in waits.items():
            if val > self.waited.get((eng, key), 0):
                self.waited[(eng, key)] = val
                out.append((key, val))
        return out

    def _register(self, tag, reads, writes):
        for k in writes:
            b = self._buf(k)
            b["w"] = tag
            b["r"] = []
        for k in reads:
            self._buf(k)["r"].append(tag)

    def op(self, eng, fn, reads=(), writes=(), inc=True):
        reads, writes = list(reads), list(writes)
        for key, val in self._collect_waits(eng, reads, writes):
            self.prog[eng].append(("wait", key, val))
        if inc:
            self.ecount[eng] += 1
            tag = ("e", eng, self.ecount[eng])
            self.prog[eng].append(("op", fn, True))
            for r, w in self.deferred[eng]:
                self._register(tag, r, w)
            self.deferred[eng] = []
            self._register(tag, reads, writes)
        else:
            self.prog[eng].append(("op", fn, False))
            self.deferred[eng].append((reads, writes))

    def dma(self, qeng, fn, reads=(), writes=()):
        reads, writes = list(reads), list(writes)
        for key, val in self._collect_waits(qeng, reads, writes):
            self.prog[qeng].append(("wait", key, val))
        self.qcount[qeng] += 1
        val = 16 * self.qcount[qeng]
        tag = ("q", qeng, val)
        self.prog[qeng].append(("dma", fn, val))
        self._register(tag, reads, writes)

    def collective(self, fn, reads=(), writes=()):
        eng = "gpsimd"
        for key, val in self._collect_waits(eng, list(reads), list(writes)):
            self.prog[eng].append(("wait", key, val))
        self.ccount += 1
        tag = ("c", "cc", self.ccount)
        self.prog[eng].append(("cc", fn, self.ccount))
        self._register(tag, list(reads), list(writes))

    def final_waits(self, eng):
        for q in self.ENGINES:
            if self.qcount[q]:
                key, val = ("q", q), 16 * self.qcount[q]
                if self.waited.get((eng, key), 0) < val:
                    self.waited[(eng, key)] = val
                    self.prog[eng].append(("wait", key, val))

    def replay(self, block, esems, qsems, ccsem):
        def run_engine(eng, engobj):
            for entry in self.prog[eng]:
                if entry[0] == "wait":
                    (kind, src), val = entry[1], entry[2]
                    sem = esems[src] if kind == "e" else (
                        qsems[src] if kind == "q" else ccsem)
                    engobj.wait_ge(sem, val)
                elif entry[0] == "op":
                    instr = entry[1]()
                    if entry[2]:
                        instr.then_inc(esems[eng], 1)
                elif entry[0] == "dma":
                    entry[1]().then_inc(qsems[eng], 16)
                else:
                    entry[1]().then_inc(ccsem, 1)

        @block.tensor
        def _(t):
            run_engine("tensor", t)

        @block.vector
        def _(v):
            run_engine("vector", v)

        @block.scalar
        def _(s):
            run_engine("scalar", s)

        @block.gpsimd
        def _(g):
            run_engine("gpsimd", g)

        @block.sync
        def _(s):
            run_engine("sync", s)


def strips(region, b0, b1, strip_bytes):
    lo = b0 // strip_bytes
    hi = (b1 - 1) // strip_bytes
    return [f"{region}:{i}" for i in range(lo, hi + 1)]


def build_program():
    nc = bass.Bass("TRN2", target_bir_lowering=False)

    d_x = nc.dram_tensor("xT", [D, M1], F32, kind="ExternalInput").ap()
    d_w1T = nc.dram_tensor("w1T", [E, D, HFF], F32, kind="ExternalInput").ap()
    d_w2T = nc.dram_tensor("w2T", [E, HFF, D], F32, kind="ExternalInput").ap()
    d_wk = nc.dram_tensor("Wk", [D, D], F32, kind="ExternalInput").ap()
    d_wv = nc.dram_tensor("Wv", [D, D], F32, kind="ExternalInput").ap()
    d_wr = nc.dram_tensor("Wr", [D, D], F32, kind="ExternalInput").ap()
    d_wo = nc.dram_tensor("Wo", [D, D], F32, kind="ExternalInput").ap()
    d_gwT = nc.dram_tensor("gwT", [D, E], F32, kind="ExternalInput").ap()
    d_gb = nc.dram_tensor("gb2", [E, 2], F32, kind="ExternalInput").ap()
    d_gbr = nc.dram_tensor("gbr", [1, 2 * E], F32, kind="ExternalInput").ap()
    d_vecs = nc.dram_tensor("vecs", [128, 7 * KB], F32, kind="ExternalInput").ap()
    d_b1r = nc.dram_tensor("b1r", [128, E * NB], F32, kind="ExternalInput").ap()
    d_b2 = nc.dram_tensor("b2", [E, D], F32, kind="ExternalInput").ap()
    d_id = nc.dram_tensor("id128", [128, 128], F32, kind="ExternalInput").ap()
    d_sel = nc.dram_tensor("sel", [128, NCORES], F32, kind="ExternalInput").ap()
    d_pv = nc.dram_tensor("pv", [128, 1], F32, kind="ExternalInput").ap()

    d_y = nc.dram_tensor("y", [D, M], F32, kind="ExternalOutput").ap()
    taps = {}
    TAP_SHAPES = {"x0": ([D, M1], F32), "x1": ([D, M1], F32),
                  "x2": ([D, M1], F32),
                  "xn1": ([128, KB * M1], BF16), "comb": ([8, M1], BF16),
                  "hs0": ([128, NB * M1], BF16), "sc1": ([128, 9 * 8], F32),
                  "sc0": ([128, 9 * 8], F32),
                  "xn2": ([128, KB * M1], BF16),
                  "xh": ([128, KB * M1], BF16),
                  "ekz": ([128, 2 * KB * M], BF16),
                  "wkv": ([128, 2 * KB * M], BF16),
                  "scn": ([128, 2 * KB * M1], BF16)}
    for tname in _TAPS:
        shp, dt = TAP_SHAPES[tname]
        taps[tname] = nc.dram_tensor("tap_" + tname, shp, dt,
                                     kind="ExternalOutput").ap()

    d_xsp = nc.dram_tensor("xspill", [128, KB * M1], F32)
    d_agi = nc.dram_tensor("agi", [128, 2 * KB], F32)
    d_ago = nc.dram_tensor("ago", [NCORES * 128, 2 * KB], F32)

    with contextlib.ExitStack() as ctx:
        sb = lambda name, shape, dt: ctx.enter_context(nc.sbuf_tensor(name, shape, dt))
        pst_ = lambda name, shape, dt: ctx.enter_context(nc.psum_tensor(name, shape, dt))

        XF = sb("XF", [128, KB * M1], F32)
        XN = sb("XN", [128, KB * M1], BF16)
        B1 = sb("B1", [128, KB * M1], BF16)
        B2 = sb("B2", [128, KB * M1], BF16)
        W2X = sb("W2X", [128, 3 * D], BF16)
        WG0 = sb("WG0", [128, 8 * 128], BF16)
        WG1 = sb("WG1", [128, 8 * 128], BF16)
        SMEAN = sb("SMEAN", [1, 512], F32)
        SRSTD = sb("SRSTD", [1, 512], F32)
        STMP = sb("STMP", [1, 512], F32)
        TH0 = sb("TH0", [128, 512], F32)
        TH1 = sb("TH1", [128, 512], F32)
        TSQ0 = sb("TSQ0", [128, 512], BF16)
        TSQ1 = sb("TSQ1", [128, 512], BF16)
        VEC = sb("VEC", [128, 7 * KB], F32)
        DD = sb("DD", [128, KB], F32)
        EU = sb("EU", [128, KB], F32)
        B1R = sb("B1R", [128, E * NB], F32)
        B2B = sb("B2B", [8, D], BF16)
        GWB = sb("GWB", [128, KB * E], BF16)
        GB8 = sb("GB8", [8, 1], F32)
        GBB2 = sb("GBB2", [128, 16], F32)
        GBB = sb("GBB", [128, 8], F32)
        GB2 = sb("GB2", [8, 2], F32)
        IDB = sb("IDB", [128, 128], BF16)
        ONE1B = sb("ONE1B", [1, 128], BF16)
        ONE1F = sb("ONE1F", [1, 128], F32)
        ONEKB = sb("ONEKB", [128, 1], BF16)
        EPSB = sb("EPSB", [1, 1], F32)
        SEL = sb("SEL", [128, NCORES], F32)
        PV = sb("PV", [128, 1], F32)
        AGP = sb("AGP", [128, 2 * KB], F32)
        AGALL = sb("AGALL", [128, NCORES * 2 * KB], BF16)
        AIN = sb("AIN", [128, 2 * KB], F32)
        IDF = sb("IDF", [128, 16], F32)
        SCR = sb("SCR", [128, 9 * 8], F32)
        CRB = sb("CRB", [128, 9 * 8], BF16)
        EQ1S = sb("EQ1S", [128, 9 * 8], BF16)
        EQ2S = sb("EQ2S", [128, 9 * 8], BF16)
        MSK8 = sb("MSK8", [128, 8], F32)
        TMP8 = sb("TMP8", [128, 8], F32)
        M1T = sb("M1T", [128, 9], F32)
        M2T = sb("M2T", [128, 9], F32)
        G1T = sb("G1T", [128, 9], F32)
        G2T = sb("G2T", [128, 9], F32)
        COMB05 = sb("COMB05", [8, M1], BF16)
        CROW = sb("CROW", [1, 512], F32)
        COMBB = sb("COMBB", [128, M1], BF16)

        PSB = [pst_(f"psb{i}", [128, 512], F32) for i in range(7)]
        PST = pst_("pst", [128, 1024], BF16)
        PS7F = PST[:].bitcast(F32)

        def psum(i):
            return PSB[i][:] if i < 7 else PS7F

        esems = {e: ctx.enter_context(nc.semaphore("se_" + e))
                 for e in Sched.ENGINES}
        qsems = {e: ctx.enter_context(nc.semaphore("sq_" + e))
                 for e in ("gpsimd", "sync", "scalar")}
        ccsem = ctx.enter_context(nc.semaphore("scc"))
        block = ctx.enter_context(nc.Block())

        sch = Sched()

        def kXF(blk, c0=0, w=M1):
            return strips("XF", (blk * M1 + c0) * 4, (blk * M1 + c0 + w) * 4, M1 * 4)

        def kXFbf(bc0, bc1):
            return strips("XF", bc0 * 2, bc1 * 2, M1 * 4)

        def kEK(blk, c0=0, w=M):
            return kXFbf(blk * M + c0, blk * M + c0 + w)

        def kZ(blk, c0=0, w=M):
            return kXFbf(KB * M + blk * M + c0, KB * M + blk * M + c0 + w)

        def kXN(blk, c0=0, w=M1):
            return strips("XN", (blk * M1 + c0) * 2, (blk * M1 + c0 + w) * 2, M1 * 2)

        def kB1c(bc0, bc1):
            return strips("B1", bc0 * 2, bc1 * 2, M1 * 2)

        def kB2c(bc0, bc1):
            return strips("B2", bc0 * 2, bc1 * 2, M1 * 2)

        def kW2X(bc0, bc1):
            return strips("W2X", bc0 * 2, bc1 * 2, 768)

        XFbf = XF[:].bitcast(BF16)

        def vXF(blk, c0, w):
            return XF[:, blk * M1 + c0: blk * M1 + c0 + w]

        def vEK(blk, c0, w):
            return XFbf[:, blk * M + c0: blk * M + c0 + w]

        def vZ(blk, c0, w):
            return XFbf[:, KB * M + blk * M + c0: KB * M + blk * M + c0 + w]

        def vXN(blk, c0, w):
            return XN[:, blk * M1 + c0: blk * M1 + c0 + w]

        sch.dma("sync", lambda: nc.sync.dma_start(
            XF[:].rearrange("p (b t) -> p b t", b=KB),
            d_x.rearrange("(b p) t -> p b t", p=128)),
            writes=[f"XF:{i}" for i in range(KB)])
        sch.dma("sync", lambda: nc.sync.dma_start(VEC[:], d_vecs), writes=["VEC"])
        sch.dma("sync", lambda: nc.sync.dma_start(B1R[:], d_b1r), writes=["B1R"])
        sch.dma("sync", lambda: nc.sync.dma_start(SEL[:], d_sel), writes=["SEL"])
        sch.dma("sync", lambda: nc.sync.dma_start(PV[:], d_pv), writes=["PV"])
        sch.dma("sync", lambda: nc.sync.dma_start(GB2[:], d_gb), writes=["GB2"])
        sch.dma("gpsimd", lambda: nc.gpsimd.dma_start(
            GBB2[:], d_gbr.broadcast_to([128, 2 * E])), writes=["GBB2"])
        sch.dma("gpsimd", lambda: nc.gpsimd.dma_start(B2B[:], d_b2), writes=["B2B"])
        sch.dma("gpsimd", lambda: nc.gpsimd.dma_start(IDB[:], d_id), writes=["IDB"])
        sch.dma("sync", lambda: nc.sync.dma_start(IDF[:], d_id[:, 0:16]), writes=["IDF"])
        sch.dma("gpsimd", lambda: nc.gpsimd.dma_start(
            GWB[:].rearrange("p (b e) -> p b e", b=KB),
            d_gwT.rearrange("(b p) e -> p b e", p=128)), writes=["GWB"])

        if "x0" in _TAPS:
            sch.dma("sync", lambda: nc.sync.dma_start(
                taps["x0"].rearrange("(b p) t -> p b t", p=128),
                XF[:].rearrange("p (b t) -> p b t", b=KB)),
                reads=[f"XF:{i}" for i in range(KB)])
        sch.op("vector", lambda: nc.vector.memset(ONE1B[:], 1.0), writes=["ONE1B"])
        sch.op("vector", lambda: nc.vector.memset(ONE1F[:], 1.0), writes=["ONE1F"])
        sch.op("vector", lambda: nc.vector.memset(ONEKB[:], 1.0), writes=["ONEKB"])
        sch.op("vector", lambda: nc.vector.memset(EPSB[:], EPS), writes=["EPSB"])
        sch.op("vector", lambda: nc.vector.tensor_add(GB8[:], GB2[:, 0:1], GB2[:, 1:2]),
               reads=["GB2"], writes=["GB8"])
        sch.op("vector", lambda: nc.vector.tensor_add(
            GBB[:, 0:8], GBB2[:, 0:8], GBB2[:, 8:16]),
            reads=["GBB2"], writes=["GBB"])
        sch.op("scalar", lambda: nc.scalar.activation(
            EU[:], VEC[:, 5 * KB:6 * KB], AF.Exp), reads=["VEC"], writes=["EU"])
        sch.op("scalar", lambda: nc.scalar.activation(
            DD[:], VEC[:, 6 * KB:7 * KB], AF.Exp), reads=["VEC"], writes=["DD"])
        sch.op("scalar", lambda: nc.scalar.activation(
            DD[:], DD[:], AF.Exp, scale=-1.0), reads=["DD"], writes=["DD"])

        def layer_norm(chunks, src_xn, apply_fn):
            for (cc, cw) in chunks:
                for blk in range(KB):
                    p = blk % 2
                    tile = TSQ0 if p == 0 else TSQ1
                    if not src_xn:
                        sch.op("scalar",
                               (lambda b=blk, t=tile, c=cc, w=cw:
                                nc.scalar.copy(t[:, 0:w], vXF(b, c, w))),
                               reads=kXF(blk, cc, cw), writes=[f"TSQ{p}"])
                    else:
                        sch.op("scalar",
                               (lambda b=blk, t=tile, c=cc, w=cw:
                                nc.scalar.copy(t[:, 0:w], vXN(b, c, w))),
                               reads=kXN(blk, cc, cw), writes=[f"TSQ{p}"])
                    mmk = [f"TSQ{p}"]
                    mm_in = (lambda t=tile, w=cw: t[:, 0:w])
                    tq = TSQ1 if p == 0 else TSQ0
                    tqk = f"TSQ{1 - p}"
                    sch.op("vector",
                           (lambda f=mm_in, t=tq, w=cw:
                            nc.vector.tensor_mul(t[:, 0:w], f(), f())),
                           reads=mmk, writes=[tqk])
                    sch.op("tensor",
                           (lambda f=mm_in, w=cw, s=(blk == 0),
                                   e_=(blk == KB - 1):
                            nc.tensor.matmul(PSB[5][0:1, 0:w], ONEKB[:], f(),
                                             start=s, stop=e_)),
                           reads=mmk + ["ONEKB"], writes=["PS5"],
                           inc=(blk == KB - 1))
                    sch.op("tensor",
                           (lambda t=tq, w=cw, s=(blk == 0), e_=(blk == KB - 1):
                            nc.tensor.matmul(PSB[6][0:1, 0:w], ONEKB[:],
                                             t[:, 0:w], start=s, stop=e_)),
                           reads=[tqk, "ONEKB"], writes=["PS6"],
                           inc=(blk == KB - 1))
                sch.op("scalar",
                       (lambda w=cw: nc.scalar.activation(
                           SMEAN[:, 0:w], PSB[5][0:1, 0:w], AF.Copy,
                           scale=1.0 / D)),
                       reads=["PS5"], writes=["SMEAN"])
                sch.op("scalar",
                       (lambda w=cw: nc.scalar.activation(
                           STMP[:, 0:w], PSB[6][0:1, 0:w], AF.Copy,
                           scale=1.0 / D)),
                       reads=["PS6"], writes=["STMP"])
                sch.op("vector",
                       (lambda w=cw: nc.vector.tensor_mul(
                           SRSTD[:, 0:w], SMEAN[:, 0:w], SMEAN[:, 0:w])),
                       reads=["SMEAN"], writes=["SRSTD"])
                sch.op("vector",
                       (lambda w=cw: nc.vector.tensor_sub(
                           SRSTD[:, 0:w], STMP[:, 0:w], SRSTD[:, 0:w])),
                       reads=["STMP", "SRSTD"], writes=["SRSTD"])
                sch.op("scalar",
                       (lambda w=cw: nc.scalar.activation(
                           SRSTD[:, 0:w], SRSTD[:, 0:w], AF.Ln,
                           bias=EPSB[0:1, 0:1])),
                       reads=["SRSTD", "EPSB"], writes=["SRSTD"])
                sch.op("scalar",
                       (lambda w=cw: nc.scalar.activation(
                           SRSTD[:, 0:w], SRSTD[:, 0:w], AF.Exp, scale=-0.5)),
                       reads=["SRSTD"], writes=["SRSTD"])
                sch.op("tensor",
                       (lambda w=cw: nc.tensor.matmul(
                           PSB[5][:, 0:w], ONE1F[:], SRSTD[:, 0:w],
                           start=True, stop=True)),
                       reads=["SRSTD", "ONE1F"], writes=["PS5"])
                sch.op("tensor",
                       (lambda w=cw: nc.tensor.matmul(
                           PSB[6][:, 0:w], ONE1F[:], SMEAN[:, 0:w],
                           start=True, stop=True)),
                       reads=["SMEAN", "ONE1F"], writes=["PS6"])
                for blk in range(KB):
                    apply_fn(blk, cc, cw)

        def apply_plain(blk, cc, cw):
            sch.op("vector",
                   (lambda b=blk, c=cc, w=cw: nc.vector.tensor_sub(
                       TH0[:, 0:w], vXF(b, c, w), PSB[6][:, 0:w])),
                   reads=kXF(blk, cc, cw) + ["PS6"], writes=["TH0"])
            sch.op("vector",
                   (lambda b=blk, c=cc, w=cw: nc.vector.tensor_mul(
                       vXN(b, c, w), TH0[:, 0:w], PSB[5][:, 0:w])),
                   reads=["TH0", "PS5"], writes=kXN(blk, cc, cw))

        def apply_plain_s(blk, cc, cw):
            sch.op("vector",
                   (lambda b=blk, c=cc, w=cw: nc.vector.tensor_sub(
                       TH0[:, 0:w], vXF(b, c, w), PSB[6][:, 0:w])),
                   reads=kXF(blk, cc, cw) + ["PS6"], writes=["TH0"])
            sch.op("vector",
                   (lambda b=blk, c=cc, w=cw: nc.vector.tensor_mul(
                       TH1[:, 0:w], TH0[:, 0:w], PSB[5][:, 0:w])),
                   reads=["TH0", "PS5"], writes=["TH1"])
            sch.op("vector",
                   (lambda b=blk, c=cc, w=cw: nc.vector.tensor_copy(
                       vXN(b, c, w), TH1[:, 0:w])),
                   reads=["TH1"], writes=kXN(blk, cc, cw))
            sch.op("vector",
                   (lambda b=blk, c=cc, w=cw: nc.vector.tensor_add(
                       vXF(b, c, w), vXF(b, c, w), TH1[:, 0:w])),
                   reads=["TH1"] + kXF(blk, cc, cw), writes=kXF(blk, cc, cw))

        def apply_internal(blk, cc, cw):
            sch.op("vector",
                   (lambda b=blk, c=cc, w=cw: nc.vector.tensor_sub(
                       TH0[:, 0:w], vXN(b, c, w), PSB[6][:, 0:w])),
                   reads=kXN(blk, cc, cw) + ["PS6"], writes=["TH0"])
            sch.op("vector",
                   (lambda b=blk, c=cc, w=cw: nc.vector.tensor_mul(
                       TH1[:, 0:w], TH0[:, 0:w], PSB[5][:, 0:w])),
                   reads=["TH0", "PS5"], writes=["TH1"])
            sch.op("vector",
                   (lambda b=blk, c=cc, w=cw: nc.vector.tensor_scalar(
                       vXN(b, c, w), TH1[:, 0:w],
                       VEC[:, 0 * KB + b:0 * KB + b + 1],
                       VEC[:, 1 * KB + b:1 * KB + b + 1],
                       ALU.mult, ALU.add)),
                   reads=["TH1", "VEC"], writes=kXN(blk, cc, cw))

        def gating(chunks):
            c0 = chunks[0][0]
            cN = chunks[-1][0] + chunks[-1][1]
            glist_all = group_list(c0, cN)
            for gi, gc, gw in glist_all:
                for blk in range(KB):
                    sch.op("tensor",
                           (lambda b=blk, g=gc, w=gw, s=(blk == 0),
                                   e_=(blk == KB - 1):
                            nc.tensor.matmul(PSB[6][0:w, 0:8],
                                             vXN(b, g, w),
                                             GWB[:, b * E:(b + 1) * E],
                                             start=s, stop=e_)),
                           reads=kXN(blk, gc, gw) + ["GWB"], writes=["PS6"],
                           inc=(blk == KB - 1))
                sch.op("vector",
                       (lambda i=gi, w=gw: nc.vector.tensor_add(
                           SCR[0:w, i * 8:(i + 1) * 8], PSB[6][0:w, 0:8],
                           GBB[0:w, 0:8])),
                       reads=["PS6", "GBB"], writes=["SCR"])
            if "sc0" in _TAPS and chunks[0][0] == 0:
                sch.dma("sync", lambda: nc.sync.dma_start(
                    taps["sc0"], SCR[:]), reads=["SCR"])
            for gi, gc, gw in glist_all:
                sch.op("vector",
                       (lambda i=gi, w=gw: nc.vector.reduce_max(
                           M1T[0:w, i:i + 1], SCR[0:w, i * 8:(i + 1) * 8],
                           axis=AXX)),
                       reads=["SCR"], writes=["M1T"])
                sch.op("vector",
                       (lambda i=gi, w=gw: nc.vector.tensor_tensor(
                           EQ1S[0:w, i * 8:(i + 1) * 8],
                           SCR[0:w, i * 8:(i + 1) * 8],
                           M1T[0:w, i:i + 1].broadcast_to([w, 8]),
                           ALU.is_equal)),
                       reads=["SCR", "M1T"], writes=["EQ1S"])
                sch.op("vector",
                       (lambda i=gi, w=gw: nc.vector.scalar_tensor_tensor(
                           MSK8[0:w, 0:8], EQ1S[0:w, i * 8:(i + 1) * 8], -1e30,
                           SCR[0:w, i * 8:(i + 1) * 8], ALU.mult, ALU.add)),
                       reads=["EQ1S", "SCR"], writes=["MSK8"])
                sch.op("vector",
                       (lambda i=gi, w=gw: nc.vector.reduce_max(
                           M2T[0:w, i:i + 1], MSK8[0:w, 0:8], axis=AXX)),
                       reads=["MSK8"], writes=["M2T"])
                sch.op("vector",
                       (lambda i=gi, w=gw: nc.vector.tensor_tensor(
                           EQ2S[0:w, i * 8:(i + 1) * 8], MSK8[0:w, 0:8],
                           M2T[0:w, i:i + 1].broadcast_to([w, 8]),
                           ALU.is_equal)),
                       reads=["MSK8", "M2T"], writes=["EQ2S"])
            sch.op("vector", lambda: nc.vector.tensor_sub(
                G1T[:, 0:9], M1T[:, 0:9], M2T[:, 0:9]),
                reads=["M1T", "M2T"], writes=["G1T"])
            sch.op("scalar", lambda: nc.scalar.activation(
                G1T[:, 0:9], G1T[:, 0:9], AF.Sigmoid),
                reads=["G1T"], writes=["G1T"])
            sch.op("vector", lambda: nc.vector.tensor_scalar(
                G2T[:, 0:9], G1T[:, 0:9], -1.0, 1.0, ALU.mult, ALU.add),
                reads=["G1T"], writes=["G2T"])
            for gi, gc, gw in glist_all:
                sch.op("vector",
                       (lambda i=gi, w=gw: nc.vector.tensor_scalar(
                           TMP8[0:w, 0:8], EQ2S[0:w, i * 8:(i + 1) * 8],
                           G2T[0:w, i:i + 1], None, ALU.mult)),
                       reads=["EQ2S", "G2T"], writes=["TMP8"])
                sch.op("vector",
                       (lambda i=gi, w=gw: nc.vector.scalar_tensor_tensor(
                           SCR[0:w, i * 8:(i + 1) * 8],
                           EQ1S[0:w, i * 8:(i + 1) * 8],
                           G1T[0:w, i:i + 1], TMP8[0:w, 0:8],
                           ALU.mult, ALU.add)),
                       reads=["EQ1S", "G1T", "TMP8"], writes=["SCR"])
                sch.op("vector",
                       (lambda i=gi, w=gw: nc.vector.tensor_copy(
                           CRB[0:w, i * 8:(i + 1) * 8],
                           SCR[0:w, i * 8:(i + 1) * 8])),
                       reads=["SCR"], writes=["CRB"])
            for gi, gc, gw in glist_all:
                sch.op("tensor",
                       (lambda i=gi, w=gw: nc.tensor.transpose(
                           PST[0:8, 0:w], CRB[0:w, i * 8:(i + 1) * 8],
                           IDB[0:w, 0:w])),
                       reads=["CRB", "IDB"], writes=["PS7"])
                sch.op("scalar",
                       (lambda g=gc, w=gw: nc.scalar.activation(
                           COMB05[0:8, g:g + w], PST[0:8, 0:w], AF.Copy,
                           scale=0.5)),
                       reads=["PS7"], writes=["COMB05"])

        def comb_bcast(e, chunks):
            c0 = chunks[0][0]
            cN = chunks[-1][0] + chunks[-1][1]
            groups = group_list(c0, cN)
            for rnd in range(3):
                base = c0 + rnd * 512
                glist = [(gi, gc, gw) for gi, gc, gw in groups
                         if base <= gc < base + 512]
                if not glist:
                    continue
                for j, (gi, gc, gw) in enumerate(glist):
                    sch.op("tensor",
                           (lambda i=gi, g=gc, w=gw, bb=base:
                            nc.tensor.matmul(
                                PSB[5][0:1, g - bb:g - bb + w],
                                CRB[0:w, i * 8 + e:i * 8 + e + 1],
                                IDB[0:w, 0:w], start=True, stop=True)),
                           reads=["CRB", "IDB"], writes=["PS5"],
                           inc=(j == len(glist) - 1))
                lo = base
                hi = min(base + 512, cN)
                sch.op("scalar",
                       (lambda l=lo, h=hi, bb=base: nc.scalar.activation(
                           CROW[0:1, 0:h - l],
                           PSB[5][0:1, 0:h - l],
                           AF.Copy, scale=0.5)),
                       reads=["PS5"], writes=["CROW"])
                sch.op("tensor",
                       (lambda l=lo, h=hi: nc.tensor.matmul(
                           PSB[6][:, 0:h - l], ONE1F[:],
                           CROW[0:1, 0:h - l],
                           start=True, stop=True)),
                       reads=["CROW", "ONE1F"], writes=["PS6"])
                sch.op("scalar",
                       (lambda l=lo, h=hi: nc.scalar.copy(
                           COMBB[:, l:h], PSB[6][:, 0:h - l])),
                       reads=["PS6"], writes=["COMBB"])

        def w2_lhsT(nb, db):
            if nb < 8:
                return B2[:, nb * D + db * 128:nb * D + db * 128 + 128]
            return W2X[:, (nb - 8) * D + db * 128:(nb - 8) * D + db * 128 + 128]

        def kw2(nb, db):
            if nb < 8:
                return kB2c(nb * D + db * 128, nb * D + db * 128 + 128)
            return kW2X((nb - 8) * D + db * 128, (nb - 8) * D + db * 128 + 128)

        def ffn(chunks, mmch, tag):
            layer_norm(chunks, False, apply_plain)
            gating(chunks)
            if tag == "f1" and "xn1" in _TAPS:
                sch.dma("sync", lambda: nc.sync.dma_start(
                    taps["xn1"], XN[:]),
                    reads=[f"XN:{i}" for i in range(KB)])
            if tag == "f1" and "comb" in _TAPS:
                sch.dma("sync", lambda: nc.sync.dma_start(
                    taps["comb"], COMB05[0:8, :]), reads=["COMB05"])
            if tag == "f1" and "sc1" in _TAPS:
                sch.dma("sync", lambda: nc.sync.dma_start(
                    taps["sc1"], SCR[:]), reads=["SCR"])
            for e in range(E):
                sch.dma("gpsimd",
                        (lambda ee=e: nc.gpsimd.dma_start(
                            B2[:, 0:8 * D].rearrange("p (n d) -> p n d", n=8),
                            d_w2T[ee][0:8 * 128, :]
                            .rearrange("(n p) d -> p n d", p=128))),
                        writes=kB2c(0, 8 * D))
                sch.dma("gpsimd",
                        (lambda ee=e: nc.gpsimd.dma_start(
                            W2X[:].rearrange("p (n d) -> p n d", n=3),
                            d_w2T[ee][8 * 128:11 * 128, :]
                            .rearrange("(n p) d -> p n d", p=128))),
                        writes=kW2X(0, 3 * D))
                comb_bcast(e, chunks)
                for nb in range(NB):
                    for kh in range(2):
                        p = (nb * 2 + kh) % 2
                        wg = WG0 if p == 0 else WG1
                        wk = f"WG{p}"
                        sch.dma("gpsimd",
                                (lambda ee=e, n=nb, k=kh, w_=wg:
                                 nc.gpsimd.dma_start(
                                     w_[:].rearrange("p (b n) -> p b n", b=8),
                                     d_w1T[ee, k * 1024:(k + 1) * 1024,
                                           n * 128:(n + 1) * 128]
                                     .rearrange("(b p) n -> p b n", p=128))),
                                writes=[wk])
                    for ci, (cc, cw) in enumerate(mmch):
                        pb = ci % 2
                        for kh in range(2):
                            p = (nb * 2 + kh) % 2
                            wg = WG0 if p == 0 else WG1
                            wk = f"WG{p}"
                            for b8 in range(8):
                                blk = kh * 8 + b8
                                sch.op("tensor",
                                       (lambda b=blk, bb=b8, c=cc, w=cw, pp=pb,
                                               w_=wg, s=(kh == 0 and b8 == 0),
                                               e_=(kh == 1 and b8 == 7):
                                        nc.tensor.matmul(
                                            psum(pp)[:, 0:w],
                                            w_[:, bb * 128:(bb + 1) * 128],
                                            vXN(b, c, w), start=s, stop=e_)),
                                       reads=kXN(blk, cc, cw) + [wk],
                                       writes=[f"PS{pb}"],
                                       inc=(kh == 1 and b8 == 7))
                        th = TH0 if (ci + nb) % 2 == 0 else TH1
                        thk = "TH0" if (ci + nb) % 2 == 0 else "TH1"
                        sch.op("scalar",
                               (lambda ee=e, n=nb, w=cw, pp=pb, t=th:
                                nc.scalar.activation(
                                    t[:, 0:w], psum(pp)[:, 0:w], AF.Silu,
                                    bias=B1R[:, ee * NB + n:ee * NB + n + 1])),
                               reads=[f"PS{pb}", "B1R"], writes=[thk])
                        sch.op("vector",
                               (lambda n=nb, c=cc, w=cw, t=th:
                                nc.vector.tensor_mul(
                                    B1[:, n * M1 + c:n * M1 + c + w],
                                    t[:, 0:w], COMBB[:, c:c + w])),
                               reads=[thk, "COMBB"],
                               writes=kB1c(nb * M1 + cc, nb * M1 + cc + cw))
                b2c_here = (e == 0)
                for ci, (cc, cw) in enumerate(mmch):
                    for db in range(KB):
                        pi = 2 + ((ci * KB + db) % 4)
                        for nb in range(NB):
                            last = (nb == NB - 1) and not b2c_here
                            sch.op("tensor",
                                   (lambda n=nb, d=db, c=cc, w=cw, p=pi,
                                           s=(nb == 0), e_=last:
                                    nc.tensor.matmul(
                                        psum(p)[:, 0:w], w2_lhsT(n, d),
                                        B1[:, n * M1 + c:n * M1 + c + w],
                                        start=s, stop=e_)),
                                   reads=kB1c(nb * M1 + cc, nb * M1 + cc + cw)
                                   + kw2(nb, db),
                                   writes=[f"PS{pi}"], inc=last)
                        if b2c_here:
                            sch.op("tensor",
                                   (lambda d=db, c=cc, w=cw, p=pi:
                                    nc.tensor.matmul(
                                        psum(p)[:, 0:w],
                                        B2B[0:8, d * 128:d * 128 + 128],
                                        COMB05[0:8, c:c + w],
                                        start=False, stop=True)),
                                   reads=["B2B", "COMB05"], writes=[f"PS{pi}"],
                                   inc=True)
                        sch.op("vector",
                               (lambda d=db, c=cc, w=cw, p=pi:
                                nc.vector.tensor_add(
                                    vXF(d, c, w), vXF(d, c, w),
                                    psum(p)[:, 0:w])),
                               reads=[f"PS{pi}"] + kXF(db, cc, cw),
                               writes=kXF(db, cc, cw))

        ffn(CH_FULL, MM_FULL, "f1")

        if "x1" in _TAPS:
            sch.dma("sync", lambda: nc.sync.dma_start(
                taps["x1"].rearrange("(b p) t -> p b t", p=128),
                XF[:].rearrange("p (b t) -> p b t", b=KB)),
                reads=[f"XF:{i}" for i in range(KB)])

        layer_norm(CH_FULL, False, apply_plain_s)
        if "xh" in _TAPS:
            sch.dma("sync", lambda: nc.sync.dma_start(taps["xh"], XN[:]),
                    reads=[f"XN:{i}" for i in range(KB)])
        sch.dma("sync", lambda: nc.sync.dma_start(d_xsp[:, :], XF[:]),
                reads=[f"XF:{i}" for i in range(KB)], writes=["dXSP"])
        layer_norm(CH_FULL, True, apply_internal)
        for blk in range(KB):
            sch.op("vector",
                   (lambda b=blk: nc.vector.tensor_mul(
                       XN[:, b * M1:b * M1 + 1], XN[:, b * M1:b * M1 + 1],
                       PV[:, 0:1])),
                   reads=kXN(blk, 0, 1) + ["PV"], writes=kXN(blk, 0, 1))
        for blk in range(KB):
            for h in range(2):
                sch.op("vector",
                       (lambda b=blk, o=h * 512: nc.vector.tensor_sub(
                           TSQ0[:, 0:512], vXN(b, 1 + o, 512), vXN(b, o, 512))),
                       reads=kXN(blk, h * 512, 513), writes=["TSQ0"])
                sch.op("vector",
                       (lambda b=blk, o=h * 512: nc.vector.scalar_tensor_tensor(
                           B1[:, b * M + o:b * M + o + 512], TSQ0[:, 0:512],
                           VEC[:, 2 * KB + b:2 * KB + b + 1], vXN(b, o, 512),
                           ALU.mult, ALU.add)),
                       reads=["TSQ0", "VEC"] + kXN(blk, h * 512, 512),
                       writes=kB1c(blk * M + h * 512, blk * M + h * 512 + 512))
                sch.op("vector",
                       (lambda b=blk, o=h * 512: nc.vector.scalar_tensor_tensor(
                           B2[:, b * M + o:b * M + o + 512], TSQ0[:, 0:512],
                           VEC[:, 3 * KB + b:3 * KB + b + 1], vXN(b, o, 512),
                           ALU.mult, ALU.add)),
                       reads=["TSQ0", "VEC"] + kXN(blk, h * 512, 512),
                       writes=kB2c(blk * M + h * 512, blk * M + h * 512 + 512))

        def proj_phase(d_w, gslot, out_emit, rhs_B1):
            off = gslot * D
            for db in range(KB):
                sch.dma("gpsimd",
                        (lambda d=db, o=off: nc.gpsimd.dma_start(
                            W2X[:, o:o + D].rearrange("p (b n) -> p b n", b=KB),
                            d_w[:, d * 128:(d + 1) * 128]
                            .rearrange("(b p) n -> p b n", p=128))),
                        writes=kW2X(off, off + D))
                pbase = (gslot * 4) + (db % 2) * 2
                srcb = B1 if rhs_B1 else B2
                ksrc = kB1c if rhs_B1 else kB2c
                for blk in range(KB):
                    for mc in range(2):
                        sch.op("tensor",
                               (lambda b=blk, m=mc, pp=pbase + mc, o=off,
                                       s_=srcb,
                                       s=(blk == 0), e_=(blk == KB - 1):
                                nc.tensor.matmul(
                                    psum(pp)[:, 0:512],
                                    W2X[:, o + b * 128:o + b * 128 + 128],
                                    s_[:, b * M + m * 512:b * M + m * 512 + 512],
                                    start=s, stop=e_)),
                               reads=ksrc(blk * M + mc * 512,
                                          blk * M + mc * 512 + 512)
                               + kW2X(off, off + D),
                               writes=[f"PS{pbase + mc}"],
                               inc=(blk == KB - 1 and mc == 1))
                for mc in range(2):
                    out_emit(db, mc, pbase + mc)

        def k_out(db, mc, p):
            sch.op("scalar",
                   (lambda d=db, m=mc, pp=p: nc.scalar.activation(
                       vEK(d, m * 512, 512), psum(pp)[:, 0:512], AF.Exp)),
                   reads=[f"PS{p}"], writes=kEK(db, mc * 512, 512))

        def v_out(db, mc, p):
            sch.op("vector",
                   (lambda d=db, m=mc, pp=p: nc.vector.tensor_mul(
                       vZ(d, m * 512, 512), psum(pp)[:, 0:512],
                       vEK(d, m * 512, 512))),
                   reads=[f"PS{p}"] + kEK(db, mc * 512, 512),
                   writes=kZ(db, mc * 512, 512))

        if "xn2" in _TAPS:
            sch.dma("sync", lambda: nc.sync.dma_start(taps["xn2"], XN[:]),
                    reads=[f"XN:{i}" for i in range(KB)])
        proj_phase(d_wk, 0, k_out, rhs_B1=True)
        proj_phase(d_wv, 1, v_out, rhs_B1=False)
        if "ekz" in _TAPS:
            sch.dma("sync", lambda: nc.sync.dma_start(
                taps["ekz"], XFbf[:, 0:2 * KB * M]),
                reads=[f"XF:{i}" for i in range(KB)])

        def emit_scans(pass2):
            for db in range(KB):
                if pass2:
                    sch.op("vector",
                           (lambda d=db: nc.vector.tensor_copy(
                               B1[:, d * M1:d * M1 + 1], AIN[:, d:d + 1])),
                           reads=["AIN"], writes=kB1c(db * M1, db * M1 + 1))
                    sch.op("vector",
                           (lambda d=db: nc.vector.tensor_copy(
                               B2[:, d * M1:d * M1 + 1],
                               AIN[:, KB + d:KB + d + 1])),
                           reads=["AIN"], writes=kB2c(db * M1, db * M1 + 1))
                sch.op("vector",
                       (lambda d=db, p2=pass2: nc.vector.tensor_tensor_scan(
                           B1[:, d * M1 + 1:d * M1 + 1 + M],
                           DD[:, d:d + 1].broadcast_to([128, M]),
                           vZ(d, 0, M),
                           (AIN[:, d:d + 1] if p2 else 0.0),
                           ALU.mult, ALU.add)),
                       reads=kZ(db, 0, M) + ["DD"] + (["AIN"] if pass2 else []),
                       writes=kB1c(db * M1 + 1, db * M1 + 1 + M))
                sch.op("vector",
                       (lambda d=db, p2=pass2: nc.vector.tensor_tensor_scan(
                           B2[:, d * M1 + 1:d * M1 + 1 + M],
                           DD[:, d:d + 1].broadcast_to([128, M]),
                           vEK(d, 0, M),
                           (AIN[:, KB + d:KB + d + 1] if p2 else 0.0),
                           ALU.mult, ALU.add)),
                       reads=kEK(db, 0, M) + ["DD"] + (["AIN"] if pass2 else []),
                       writes=kB2c(db * M1 + 1, db * M1 + 1 + M))
                if not pass2:
                    sch.op("vector",
                           (lambda d=db: nc.vector.tensor_copy(
                               AGP[:, d:d + 1],
                               B1[:, d * M1 + M:d * M1 + M + 1])),
                           reads=kB1c(db * M1 + M, db * M1 + M + 1),
                           writes=["AGP"])
                    sch.op("vector",
                           (lambda d=db: nc.vector.tensor_copy(
                               AGP[:, KB + d:KB + d + 1],
                               B2[:, d * M1 + M:d * M1 + M + 1])),
                           reads=kB2c(db * M1 + M, db * M1 + M + 1),
                           writes=["AGP"])

        emit_scans(pass2=False)
        sch.dma("gpsimd", lambda: nc.gpsimd.dma_start(d_agi[:, :], AGP[:]),
                reads=["AGP"], writes=["dAGI"])
        sch.collective(lambda: nc.gpsimd.collective_compute(
            "AllGather", ALU.bypass,
            replica_groups=[list(range(NCORES))],
            ins=[d_agi.ap().opt()],
            outs=[d_ago.ap().opt()]),
            reads=["dAGI"], writes=["dAGO"])
        sch.dma("gpsimd", lambda: nc.gpsimd.dma_start(
            AGALL[:].rearrange("p (r c) -> p r c", r=NCORES),
            d_ago.ap().rearrange("(r p) c -> p r c", p=128)),
            reads=["dAGO"], writes=["AGALL"])
        sch.op("vector", lambda: nc.vector.memset(AIN[:], 0.0), writes=["AIN"])
        for r in range(NCORES):
            sch.op("vector",
                   (lambda rr=r: nc.vector.scalar_tensor_tensor(
                       AIN[:], AGALL[:, rr * 2 * KB:(rr + 1) * 2 * KB],
                       SEL[:, rr:rr + 1], AIN[:], ALU.mult, ALU.add)),
                   reads=["AGALL", "SEL", "AIN"], writes=["AIN"])
        emit_scans(pass2=True)

        for db in range(KB):
            for h in range(2):
                o = h * 512
                sch.op("vector",
                       (lambda d=db, oo=o: nc.vector.tensor_scalar(
                           TH0[:, 0:512], vZ(d, oo, 512), EU[:, d:d + 1],
                           None, ALU.mult)),
                       reads=kZ(db, o, 512) + ["EU"], writes=["TH0"])
                sch.op("vector",
                       (lambda d=db, oo=o: nc.vector.tensor_add(
                           TH0[:, 0:512], TH0[:, 0:512],
                           B1[:, d * M1 + oo:d * M1 + oo + 512])),
                       reads=["TH0"] + kB1c(db * M1 + o, db * M1 + o + 512),
                       writes=["TH0"])
                sch.op("vector",
                       (lambda d=db, oo=o: nc.vector.scalar_tensor_tensor(
                           TH1[:, 0:512], vEK(d, oo, 512), EU[:, d:d + 1],
                           B2[:, d * M1 + oo:d * M1 + oo + 512],
                           ALU.mult, ALU.add)),
                       reads=kEK(db, o, 512) + ["EU"]
                       + kB2c(db * M1 + o, db * M1 + o + 512),
                       writes=["TH1"])
                sch.op("scalar",
                       (lambda: nc.scalar.activation(
                           PSB[6][:, 0:512], TH1[:, 0:512], AF.Ln)),
                       reads=["TH1"], writes=["PS6"])
                sch.op("scalar",
                       (lambda: nc.scalar.activation(
                           TH1[:, 0:512], PSB[6][:, 0:512], AF.Exp,
                           scale=-1.0)),
                       reads=["PS6"], writes=["TH1"])
                sch.op("vector",
                       (lambda d=db, oo=o: nc.vector.tensor_mul(
                           vZ(d, oo, 512), TH0[:, 0:512], TH1[:, 0:512])),
                       reads=["TH0", "TH1"], writes=kZ(db, o, 512))

        if "wkv" in _TAPS:
            sch.dma("sync", lambda: nc.sync.dma_start(
                taps["wkv"], XFbf[:, 0:2 * KB * M]),
                reads=[f"XF:{i}" for i in range(KB)])
        if "scn" in _TAPS:
            sch.dma("sync", lambda: nc.sync.dma_start(
                taps["scn"][:, 0:KB * M1], B1[:]),
                reads=[f"B1:{i}" for i in range(KB)])
            sch.dma("sync", lambda: nc.sync.dma_start(
                taps["scn"][:, KB * M1:2 * KB * M1], B2[:]),
                reads=[f"B2:{i}" for i in range(KB)])
        for blk in range(KB):
            for h in range(2):
                sch.op("vector",
                       (lambda b=blk, o=h * 512: nc.vector.tensor_sub(
                           TSQ0[:, 0:512], vXN(b, 1 + o, 512), vXN(b, o, 512))),
                       reads=kXN(blk, h * 512, 513), writes=["TSQ0"])
                sch.op("vector",
                       (lambda b=blk, o=h * 512: nc.vector.scalar_tensor_tensor(
                           B2[:, b * M + o:b * M + o + 512], TSQ0[:, 0:512],
                           VEC[:, 4 * KB + b:4 * KB + b + 1], vXN(b, o, 512),
                           ALU.mult, ALU.add)),
                       reads=["TSQ0", "VEC"] + kXN(blk, h * 512, 512),
                       writes=kB2c(blk * M + h * 512, blk * M + h * 512 + 512))

        def r_out(db, mc, p):
            th = TH0 if mc == 0 else TH1
            thk = "TH0" if mc == 0 else "TH1"
            sch.op("scalar",
                   (lambda pp=p, t=th: nc.scalar.activation(
                       t[:, 0:512], psum(pp)[:, 0:512], AF.Sigmoid)),
                   reads=[f"PS{p}"], writes=[thk])
            sch.op("vector",
                   (lambda d=db, m=mc, t=th: nc.vector.tensor_mul(
                       B1[:, d * M + m * 512:d * M + m * 512 + 512],
                       t[:, 0:512], vZ(d, m * 512, 512))),
                   reads=[thk] + kZ(db, mc * 512, 512),
                   writes=kB1c(db * M + mc * 512, db * M + mc * 512 + 512))

        proj_phase(d_wr, 0, r_out, rhs_B1=False)

        sch.dma("sync", lambda: nc.sync.dma_start(XF[:], d_xsp[:, :]),
                reads=["dXSP"], writes=[f"XF:{i}" for i in range(KB)])

        def o_out(db, mc, p):
            cc = 1 + mc * 512
            sch.op("vector",
                   (lambda d=db, c=cc, pp=p: nc.vector.tensor_add(
                       vXF(d, c, 512), vXF(d, c, 512), psum(pp)[:, 0:512])),
                   reads=[f"PS{p}"] + kXF(db, cc, 512),
                   writes=kXF(db, cc, 512))

        proj_phase(d_wo, 1, o_out, rhs_B1=True)

        if "x2" in _TAPS:
            sch.dma("sync", lambda: nc.sync.dma_start(
                taps["x2"].rearrange("(b p) t -> p b t", p=128),
                XF[:].rearrange("p (b t) -> p b t", b=KB)),
                reads=[f"XF:{i}" for i in range(KB)])

        ffn(CH_OWN, CH_OWN, "f2")

        sch.dma("sync", lambda: nc.sync.dma_start(
            d_y.rearrange("(b p) t -> p b t", p=128),
            XF[:].rearrange("p (b t) -> p b t", b=KB)[:, :, 1:M1]),
            reads=[f"XF:{i}" for i in range(KB)])

        sch.final_waits("sync")
        sch.final_waits("gpsimd")
        sch.replay(block, esems, qsems, ccsem)

    return nc


def _stage_vec(v):
    return np.ascontiguousarray(v.reshape(KB, 128).T)


def kernel(**inputs):
    x = np.asarray(inputs["L0_attn_x"], np.float32)
    Wk = np.ascontiguousarray(np.asarray(inputs["L0_attn_Wk"], np.float32))
    Wv = np.ascontiguousarray(np.asarray(inputs["L0_attn_Wv"], np.float32))
    Wr = np.ascontiguousarray(np.asarray(inputs["L0_attn_Wr"], np.float32))
    Wo = np.ascontiguousarray(np.asarray(inputs["L0_attn_Wo"], np.float32))
    w1 = np.asarray(inputs["L0_ffn_w1"], np.float32)
    w2 = np.asarray(inputs["L0_ffn_w2"], np.float32)
    gate_w = np.asarray(inputs["L0_ffn_gate_w"], np.float32)
    gate_b = np.asarray(inputs["L0_ffn_gate_b"], np.float32)
    bias = np.asarray(inputs["L0_ffn_bias"], np.float32)
    b1 = np.asarray(inputs["L0_ffn_b1"], np.float32)
    b2 = np.asarray(inputs["L0_ffn_b2"], np.float32)

    w1T = np.ascontiguousarray(w1.transpose(0, 2, 1))
    w2T = np.ascontiguousarray(w2.transpose(0, 2, 1))
    gwT = np.ascontiguousarray(gate_w.T)
    gb2 = np.ascontiguousarray(np.stack([gate_b, bias], axis=1))
    gbr = np.ascontiguousarray(
        np.concatenate([gate_b, bias]).reshape(1, 2 * E))
    b1r = np.ascontiguousarray(
        b1.reshape(E, NB, 128).transpose(2, 0, 1).reshape(128, E * NB))
    vecs = np.concatenate([
        _stage_vec(np.asarray(inputs["L0_attn_ln_scale"], np.float32)),
        _stage_vec(np.asarray(inputs["L0_attn_ln_bias"], np.float32)),
        _stage_vec(np.asarray(inputs["L0_attn_mix_k"], np.float32)),
        _stage_vec(np.asarray(inputs["L0_attn_mix_v"], np.float32)),
        _stage_vec(np.asarray(inputs["L0_attn_mix_r"], np.float32)),
        _stage_vec(np.asarray(inputs["L0_attn_time_first"], np.float32)),
        _stage_vec(np.asarray(inputs["L0_attn_time_decay"], np.float32)),
    ], axis=1)
    idm = np.eye(128, dtype=np.float32)

    xt = np.ascontiguousarray(x[0].T)

    in_maps = []
    for r in range(NCORES):
        lo = r * M
        if r == 0:
            xs = np.concatenate(
                [np.zeros((D, 1), np.float32), xt[:, 0:M]], axis=1)
        else:
            xs = xt[:, lo - 1:lo + M]
        sel = np.zeros((128, NCORES), np.float32)
        if r > 0:
            sel[:, r - 1] = 1.0
        pv = np.full((128, 1), 0.0 if r == 0 else 1.0, np.float32)
        in_maps.append(dict(
            xT=np.ascontiguousarray(xs), w1T=w1T, w2T=w2T,
            Wk=Wk, Wv=Wv, Wr=Wr, Wo=Wo, gwT=gwT, gb2=gb2, gbr=gbr, vecs=vecs,
            b1r=b1r, b2=np.ascontiguousarray(b2), id128=idm, sel=sel, pv=pv,
        ))

    nc = build_program()
    res = run_bass_kernel_spmd(nc, in_maps, core_ids=list(range(NCORES)))

    out = np.empty((1, T, D), np.float32)
    for r in range(NCORES):
        out[0, r * M:(r + 1) * M, :] = res.results[r]["y"].T
    if _TAPS:
        kernel.taps = [{k: res.results[r]["tap_" + k] for k in _TAPS}
                       for r in range(NCORES)]
    return out


# version 04
# speedup vs baseline: 1.1057x; 1.1057x over previous
import sys
sys.path.insert(0, "/opt/trn_rl_repo")

import contextlib
import numpy as np

import concourse.bass as bass
import concourse.mybir as mybir
from concourse.bass_utils import run_bass_kernel_spmd

F32 = mybir.dt.float32
BF16 = mybir.dt.bfloat16
AF = mybir.ActivationFunctionType
ALU = mybir.AluOpType
AXX = mybir.AxisListType.X

NCORES = 8
D = 2048
KB = 16
T = 8192
M = 1024
M1 = 1025
E = 8
HFF = 1408
NB = 11
EPS = 1e-5

CH_FULL = [(0, 342), (342, 342), (684, 341)]
CH_OWN = [(1, 512), (513, 512)]
MM_FULL = [(0, 512), (512, 512), (1024, 1)]
def group_list(c0, cN):
    out = []
    gi = 0
    gc = c0
    while gc < cN:
        out.append((gi, gc, min(128, cN - gc)))
        gc += 128
        gi += 1
    return out

_TAPS = []


class Sched:
    ENGINES = ("tensor", "vector", "scalar", "gpsimd", "sync")

    def __init__(self):
        self.prog = {e: [] for e in self.ENGINES}
        self.ecount = {e: 0 for e in self.ENGINES}
        self.qcount = {e: 0 for e in self.ENGINES}
        self.ccount = 0
        self.bufs = {}
        self.waited = {}
        self.deferred = {e: [] for e in self.ENGINES}

    def _buf(self, k):
        if k not in self.bufs:
            self.bufs[k] = {"w": None, "r": []}
        return self.bufs[k]

    def _need(self, eng, tag):
        kind, src, val = tag
        if kind == "e" and src == eng:
            if eng == "tensor":
                return False
            return True
        return True

    def _collect_waits(self, eng, reads, writes):
        waits = {}
        for k in reads:
            b = self._buf(k)
            if b["w"] is not None and self._need(eng, b["w"]):
                kind, src, val = b["w"]
                waits[(kind, src)] = max(waits.get((kind, src), 0), val)
        for k in writes:
            b = self._buf(k)
            if b["w"] is not None and self._need(eng, b["w"]):
                kind, src, val = b["w"]
                waits[(kind, src)] = max(waits.get((kind, src), 0), val)
            for tag in b["r"]:
                if self._need(eng, tag):
                    kind, src, val = tag
                    waits[(kind, src)] = max(waits.get((kind, src), 0), val)
        out = []
        for key, val in waits.items():
            if val > self.waited.get((eng, key), 0):
                self.waited[(eng, key)] = val
                out.append((key, val))
        return out

    def _register(self, tag, reads, writes):
        for k in writes:
            b = self._buf(k)
            b["w"] = tag
            b["r"] = []
        for k in reads:
            self._buf(k)["r"].append(tag)

    def op(self, eng, fn, reads=(), writes=(), inc=True):
        reads, writes = list(reads), list(writes)
        for key, val in self._collect_waits(eng, reads, writes):
            self.prog[eng].append(("wait", key, val))
        if inc:
            self.ecount[eng] += 1
            tag = ("e", eng, self.ecount[eng])
            self.prog[eng].append(("op", fn, True))
            for r, w in self.deferred[eng]:
                self._register(tag, r, w)
            self.deferred[eng] = []
            self._register(tag, reads, writes)
        else:
            self.prog[eng].append(("op", fn, False))
            self.deferred[eng].append((reads, writes))

    def dma(self, qeng, fn, reads=(), writes=()):
        reads, writes = list(reads), list(writes)
        for key, val in self._collect_waits(qeng, reads, writes):
            self.prog[qeng].append(("wait", key, val))
        self.qcount[qeng] += 1
        val = 16 * self.qcount[qeng]
        tag = ("q", qeng, val)
        self.prog[qeng].append(("dma", fn, val))
        self._register(tag, reads, writes)

    def collective(self, fn, reads=(), writes=()):
        eng = "gpsimd"
        for key, val in self._collect_waits(eng, list(reads), list(writes)):
            self.prog[eng].append(("wait", key, val))
        self.ccount += 1
        tag = ("c", "cc", self.ccount)
        self.prog[eng].append(("cc", fn, self.ccount))
        self._register(tag, list(reads), list(writes))

    def final_waits(self, eng):
        for q in self.ENGINES:
            if self.qcount[q]:
                key, val = ("q", q), 16 * self.qcount[q]
                if self.waited.get((eng, key), 0) < val:
                    self.waited[(eng, key)] = val
                    self.prog[eng].append(("wait", key, val))

    def replay(self, block, esems, qsems, ccsem):
        def run_engine(eng, engobj):
            for entry in self.prog[eng]:
                if entry[0] == "wait":
                    (kind, src), val = entry[1], entry[2]
                    sem = esems[src] if kind == "e" else (
                        qsems[src] if kind == "q" else ccsem)
                    engobj.wait_ge(sem, val)
                elif entry[0] == "op":
                    instr = entry[1]()
                    if entry[2]:
                        instr.then_inc(esems[eng], 1)
                elif entry[0] == "dma":
                    entry[1]().then_inc(qsems[eng], 16)
                else:
                    entry[1]().then_inc(ccsem, 1)

        @block.tensor
        def _(t):
            run_engine("tensor", t)

        @block.vector
        def _(v):
            run_engine("vector", v)

        @block.scalar
        def _(s):
            run_engine("scalar", s)

        @block.gpsimd
        def _(g):
            run_engine("gpsimd", g)

        @block.sync
        def _(s):
            run_engine("sync", s)


def strips(region, b0, b1, strip_bytes):
    lo = b0 // strip_bytes
    hi = (b1 - 1) // strip_bytes
    return [f"{region}:{i}" for i in range(lo, hi + 1)]


def build_program():
    nc = bass.Bass("TRN2", target_bir_lowering=False)

    d_x = nc.dram_tensor("xT", [D, M1], F32, kind="ExternalInput").ap()
    d_w1T = nc.dram_tensor("w1T", [E, D, HFF], F32, kind="ExternalInput").ap()
    d_w2T = nc.dram_tensor("w2T", [E, HFF, D], F32, kind="ExternalInput").ap()
    d_wk = nc.dram_tensor("Wk", [D, D], F32, kind="ExternalInput").ap()
    d_wv = nc.dram_tensor("Wv", [D, D], F32, kind="ExternalInput").ap()
    d_wr = nc.dram_tensor("Wr", [D, D], F32, kind="ExternalInput").ap()
    d_wo = nc.dram_tensor("Wo", [D, D], F32, kind="ExternalInput").ap()
    d_gwT = nc.dram_tensor("gwT", [D, E], F32, kind="ExternalInput").ap()
    d_gb = nc.dram_tensor("gb2", [E, 2], F32, kind="ExternalInput").ap()
    d_gbr = nc.dram_tensor("gbr", [1, 2 * E], F32, kind="ExternalInput").ap()
    d_vecs = nc.dram_tensor("vecs", [128, 7 * KB], F32, kind="ExternalInput").ap()
    d_b1r = nc.dram_tensor("b1r", [128, E * NB], F32, kind="ExternalInput").ap()
    d_b2 = nc.dram_tensor("b2", [E, D], F32, kind="ExternalInput").ap()
    d_id = nc.dram_tensor("id128", [128, 128], F32, kind="ExternalInput").ap()
    d_sel = nc.dram_tensor("sel", [128, NCORES], F32, kind="ExternalInput").ap()
    d_pv = nc.dram_tensor("pv", [128, 1], F32, kind="ExternalInput").ap()

    d_y = nc.dram_tensor("y", [D, M], F32, kind="ExternalOutput").ap()
    taps = {}
    TAP_SHAPES = {"x0": ([D, M1], F32), "x1": ([D, M1], F32),
                  "x2": ([D, M1], F32),
                  "xn1": ([128, KB * M1], BF16), "comb": ([8, M1], BF16),
                  "hs0": ([128, NB * M1], BF16), "sc1": ([128, 9 * 8], F32),
                  "sc0": ([128, 9 * 8], F32),
                  "xn2": ([128, KB * M1], BF16),
                  "xh": ([128, KB * M1], BF16),
                  "ekz": ([128, 2 * KB * M], BF16),
                  "wkv": ([128, 2 * KB * M], BF16),
                  "scn": ([128, 2 * KB * M1], BF16)}
    for tname in _TAPS:
        shp, dt = TAP_SHAPES[tname]
        taps[tname] = nc.dram_tensor("tap_" + tname, shp, dt,
                                     kind="ExternalOutput").ap()

    d_xsp = nc.dram_tensor("xspill", [128, KB * M1], F32)
    d_agi = nc.dram_tensor("agi", [128, 2 * KB], F32)
    d_ago = nc.dram_tensor("ago", [NCORES * 128, 2 * KB], F32)

    with contextlib.ExitStack() as ctx:
        sb = lambda name, shape, dt: ctx.enter_context(nc.sbuf_tensor(name, shape, dt))
        pst_ = lambda name, shape, dt: ctx.enter_context(nc.psum_tensor(name, shape, dt))

        XF = sb("XF", [128, KB * M1], F32)
        XN = sb("XN", [128, KB * M1], BF16)
        B1 = sb("B1", [128, KB * M1], BF16)
        B2 = sb("B2", [128, KB * M1], BF16)
        W2X = sb("W2X", [128, 3 * D], BF16)
        WG0 = sb("WG0", [128, 8 * 128], BF16)
        WG1 = sb("WG1", [128, 8 * 128], BF16)
        SMEAN = sb("SMEAN", [1, 512], F32)
        SRSTD = sb("SRSTD", [1, 512], F32)
        STMP = sb("STMP", [1, 512], F32)
        TH0 = sb("TH0", [128, 512], F32)
        TH1 = sb("TH1", [128, 512], F32)
        TSQ0 = sb("TSQ0", [128, 512], BF16)
        TSQ1 = sb("TSQ1", [128, 512], BF16)
        VEC = sb("VEC", [128, 7 * KB], F32)
        DD = sb("DD", [128, KB], F32)
        EU = sb("EU", [128, KB], F32)
        B1R = sb("B1R", [128, E * NB], F32)
        B2B = sb("B2B", [8, D], BF16)
        GWB = sb("GWB", [128, KB * E], BF16)
        GB8 = sb("GB8", [8, 1], F32)
        GBB2 = sb("GBB2", [128, 16], F32)
        GBB = sb("GBB", [128, 8], F32)
        GB2 = sb("GB2", [8, 2], F32)
        IDB = sb("IDB", [128, 128], BF16)
        ONE1B = sb("ONE1B", [1, 128], BF16)
        ONE1F = sb("ONE1F", [1, 128], F32)
        ONEKB = sb("ONEKB", [128, 1], BF16)
        EPSB = sb("EPSB", [1, 1], F32)
        SEL = sb("SEL", [128, NCORES], F32)
        PV = sb("PV", [128, 1], F32)
        AGP = sb("AGP", [128, 2 * KB], F32)
        AGALL = sb("AGALL", [128, NCORES * 2 * KB], BF16)
        AIN = sb("AIN", [128, 2 * KB], F32)
        IDF = sb("IDF", [128, 16], F32)
        SCR = sb("SCR", [128, 9 * 8], F32)
        CRB = sb("CRB", [128, 9 * 8], BF16)
        EQ1S = sb("EQ1S", [128, 9 * 8], BF16)
        EQ2S = sb("EQ2S", [128, 9 * 8], BF16)
        MSK8 = sb("MSK8", [128, 8], F32)
        TMP8 = sb("TMP8", [128, 8], F32)
        M1T = sb("M1T", [128, 9], F32)
        M2T = sb("M2T", [128, 9], F32)
        G1T = sb("G1T", [128, 9], F32)
        G2T = sb("G2T", [128, 9], F32)
        COMB05 = sb("COMB05", [8, M1], BF16)
        CROW = sb("CROW", [1, 512], F32)
        COMBB = sb("COMBB", [128, M1], BF16)

        PSB = [pst_(f"psb{i}", [128, 512], F32) for i in range(7)]
        PST = pst_("pst", [128, 1024], BF16)
        PS7F = PST[:].bitcast(F32)

        def psum(i):
            return PSB[i][:] if i < 7 else PS7F

        esems = {e: ctx.enter_context(nc.semaphore("se_" + e))
                 for e in Sched.ENGINES}
        qsems = {e: ctx.enter_context(nc.semaphore("sq_" + e))
                 for e in ("gpsimd", "sync", "scalar")}
        ccsem = ctx.enter_context(nc.semaphore("scc"))
        block = ctx.enter_context(nc.Block())

        sch = Sched()

        def kXF(blk, c0=0, w=M1):
            return strips("XF", (blk * M1 + c0) * 4, (blk * M1 + c0 + w) * 4, M1 * 4)

        def kXFbf(bc0, bc1):
            return strips("XF", bc0 * 2, bc1 * 2, M1 * 4)

        def kEK(blk, c0=0, w=M):
            return kXFbf(blk * M + c0, blk * M + c0 + w)

        def kZ(blk, c0=0, w=M):
            return kXFbf(KB * M + blk * M + c0, KB * M + blk * M + c0 + w)

        def kXN(blk, c0=0, w=M1):
            return strips("XN", (blk * M1 + c0) * 2, (blk * M1 + c0 + w) * 2, M1 * 2)

        def kB1c(bc0, bc1):
            return strips("B1", bc0 * 2, bc1 * 2, M1 * 2)

        def kB2c(bc0, bc1):
            return strips("B2", bc0 * 2, bc1 * 2, M1 * 2)

        def kW2X(bc0, bc1):
            return strips("W2X", bc0 * 2, bc1 * 2, 768)

        XFbf = XF[:].bitcast(BF16)

        def vXF(blk, c0, w):
            return XF[:, blk * M1 + c0: blk * M1 + c0 + w]

        def vEK(blk, c0, w):
            return XFbf[:, blk * M + c0: blk * M + c0 + w]

        def vZ(blk, c0, w):
            return XFbf[:, KB * M + blk * M + c0: KB * M + blk * M + c0 + w]

        def vXN(blk, c0, w):
            return XN[:, blk * M1 + c0: blk * M1 + c0 + w]

        sch.dma("sync", lambda: nc.sync.dma_start(
            XF[:].rearrange("p (b t) -> p b t", b=KB),
            d_x.rearrange("(b p) t -> p b t", p=128)),
            writes=[f"XF:{i}" for i in range(KB)])
        sch.dma("sync", lambda: nc.sync.dma_start(VEC[:], d_vecs), writes=["VEC"])
        sch.dma("sync", lambda: nc.sync.dma_start(B1R[:], d_b1r), writes=["B1R"])
        sch.dma("sync", lambda: nc.sync.dma_start(SEL[:], d_sel), writes=["SEL"])
        sch.dma("sync", lambda: nc.sync.dma_start(PV[:], d_pv), writes=["PV"])
        sch.dma("sync", lambda: nc.sync.dma_start(GB2[:], d_gb), writes=["GB2"])
        sch.dma("gpsimd", lambda: nc.gpsimd.dma_start(
            GBB2[:], d_gbr.broadcast_to([128, 2 * E])), writes=["GBB2"])
        sch.dma("gpsimd", lambda: nc.gpsimd.dma_start(B2B[:], d_b2), writes=["B2B"])
        sch.dma("gpsimd", lambda: nc.gpsimd.dma_start(IDB[:], d_id), writes=["IDB"])
        sch.dma("sync", lambda: nc.sync.dma_start(IDF[:], d_id[:, 0:16]), writes=["IDF"])
        sch.dma("gpsimd", lambda: nc.gpsimd.dma_start(
            GWB[:].rearrange("p (b e) -> p b e", b=KB),
            d_gwT.rearrange("(b p) e -> p b e", p=128)), writes=["GWB"])

        if "x0" in _TAPS:
            sch.dma("sync", lambda: nc.sync.dma_start(
                taps["x0"].rearrange("(b p) t -> p b t", p=128),
                XF[:].rearrange("p (b t) -> p b t", b=KB)),
                reads=[f"XF:{i}" for i in range(KB)])
        sch.op("vector", lambda: nc.vector.memset(ONE1B[:], 1.0), writes=["ONE1B"])
        sch.op("vector", lambda: nc.vector.memset(ONE1F[:], 1.0), writes=["ONE1F"])
        sch.op("vector", lambda: nc.vector.memset(ONEKB[:], 1.0), writes=["ONEKB"])
        sch.op("vector", lambda: nc.vector.memset(EPSB[:], EPS), writes=["EPSB"])
        sch.op("vector", lambda: nc.vector.tensor_add(GB8[:], GB2[:, 0:1], GB2[:, 1:2]),
               reads=["GB2"], writes=["GB8"])
        sch.op("vector", lambda: nc.vector.tensor_add(
            GBB[:, 0:8], GBB2[:, 0:8], GBB2[:, 8:16]),
            reads=["GBB2"], writes=["GBB"])
        sch.op("scalar", lambda: nc.scalar.activation(
            EU[:], VEC[:, 5 * KB:6 * KB], AF.Exp), reads=["VEC"], writes=["EU"])
        sch.op("scalar", lambda: nc.scalar.activation(
            DD[:], VEC[:, 6 * KB:7 * KB], AF.Exp), reads=["VEC"], writes=["DD"])
        sch.op("scalar", lambda: nc.scalar.activation(
            DD[:], DD[:], AF.Exp, scale=-1.0), reads=["DD"], writes=["DD"])

        def layer_norm(chunks, src_xn, apply_fn):
            for (cc, cw) in chunks:
                for blk in range(KB):
                    p = blk % 2
                    tile = TSQ0 if p == 0 else TSQ1
                    if not src_xn:
                        sch.op("scalar",
                               (lambda b=blk, t=tile, c=cc, w=cw:
                                nc.scalar.copy(t[:, 0:w], vXF(b, c, w))),
                               reads=kXF(blk, cc, cw), writes=[f"TSQ{p}"])
                    else:
                        sch.op("scalar",
                               (lambda b=blk, t=tile, c=cc, w=cw:
                                nc.scalar.copy(t[:, 0:w], vXN(b, c, w))),
                               reads=kXN(blk, cc, cw), writes=[f"TSQ{p}"])
                    mmk = [f"TSQ{p}"]
                    mm_in = (lambda t=tile, w=cw: t[:, 0:w])
                    tq = TSQ1 if p == 0 else TSQ0
                    tqk = f"TSQ{1 - p}"
                    sch.op("vector",
                           (lambda f=mm_in, t=tq, w=cw:
                            nc.vector.tensor_mul(t[:, 0:w], f(), f())),
                           reads=mmk, writes=[tqk])
                    sch.op("tensor",
                           (lambda f=mm_in, w=cw, s=(blk == 0),
                                   e_=(blk == KB - 1):
                            nc.tensor.matmul(PSB[5][0:1, 0:w], ONEKB[:], f(),
                                             start=s, stop=e_)),
                           reads=mmk + ["ONEKB"], writes=["PS5"],
                           inc=(blk == KB - 1))
                    sch.op("tensor",
                           (lambda t=tq, w=cw, s=(blk == 0), e_=(blk == KB - 1):
                            nc.tensor.matmul(PSB[6][0:1, 0:w], ONEKB[:],
                                             t[:, 0:w], start=s, stop=e_)),
                           reads=[tqk, "ONEKB"], writes=["PS6"],
                           inc=(blk == KB - 1))
                sch.op("scalar",
                       (lambda w=cw: nc.scalar.activation(
                           SMEAN[:, 0:w], PSB[5][0:1, 0:w], AF.Copy,
                           scale=1.0 / D)),
                       reads=["PS5"], writes=["SMEAN"])
                sch.op("scalar",
                       (lambda w=cw: nc.scalar.activation(
                           STMP[:, 0:w], PSB[6][0:1, 0:w], AF.Copy,
                           scale=1.0 / D)),
                       reads=["PS6"], writes=["STMP"])
                sch.op("vector",
                       (lambda w=cw: nc.vector.tensor_mul(
                           SRSTD[:, 0:w], SMEAN[:, 0:w], SMEAN[:, 0:w])),
                       reads=["SMEAN"], writes=["SRSTD"])
                sch.op("vector",
                       (lambda w=cw: nc.vector.tensor_sub(
                           SRSTD[:, 0:w], STMP[:, 0:w], SRSTD[:, 0:w])),
                       reads=["STMP", "SRSTD"], writes=["SRSTD"])
                sch.op("scalar",
                       (lambda w=cw: nc.scalar.activation(
                           SRSTD[:, 0:w], SRSTD[:, 0:w], AF.Ln,
                           bias=EPSB[0:1, 0:1])),
                       reads=["SRSTD", "EPSB"], writes=["SRSTD"])
                sch.op("scalar",
                       (lambda w=cw: nc.scalar.activation(
                           SRSTD[:, 0:w], SRSTD[:, 0:w], AF.Exp, scale=-0.5)),
                       reads=["SRSTD"], writes=["SRSTD"])
                sch.op("tensor",
                       (lambda w=cw: nc.tensor.matmul(
                           PSB[5][:, 0:w], ONE1F[:], SRSTD[:, 0:w],
                           start=True, stop=True)),
                       reads=["SRSTD", "ONE1F"], writes=["PS5"])
                sch.op("tensor",
                       (lambda w=cw: nc.tensor.matmul(
                           PSB[6][:, 0:w], ONE1F[:], SMEAN[:, 0:w],
                           start=True, stop=True)),
                       reads=["SMEAN", "ONE1F"], writes=["PS6"])
                for blk in range(KB):
                    apply_fn(blk, cc, cw)

        def apply_plain(blk, cc, cw):
            sch.op("vector",
                   (lambda b=blk, c=cc, w=cw: nc.vector.tensor_sub(
                       TH0[:, 0:w], vXF(b, c, w), PSB[6][:, 0:w])),
                   reads=kXF(blk, cc, cw) + ["PS6"], writes=["TH0"])
            sch.op("vector",
                   (lambda b=blk, c=cc, w=cw: nc.vector.tensor_mul(
                       vXN(b, c, w), TH0[:, 0:w], PSB[5][:, 0:w])),
                   reads=["TH0", "PS5"], writes=kXN(blk, cc, cw))

        def apply_plain_s(blk, cc, cw):
            sch.op("vector",
                   (lambda b=blk, c=cc, w=cw: nc.vector.tensor_sub(
                       TH0[:, 0:w], vXF(b, c, w), PSB[6][:, 0:w])),
                   reads=kXF(blk, cc, cw) + ["PS6"], writes=["TH0"])
            sch.op("vector",
                   (lambda b=blk, c=cc, w=cw: nc.vector.tensor_mul(
                       TH1[:, 0:w], TH0[:, 0:w], PSB[5][:, 0:w])),
                   reads=["TH0", "PS5"], writes=["TH1"])
            sch.op("vector",
                   (lambda b=blk, c=cc, w=cw: nc.vector.tensor_copy(
                       vXN(b, c, w), TH1[:, 0:w])),
                   reads=["TH1"], writes=kXN(blk, cc, cw))
            sch.op("vector",
                   (lambda b=blk, c=cc, w=cw: nc.vector.tensor_add(
                       vXF(b, c, w), vXF(b, c, w), TH1[:, 0:w])),
                   reads=["TH1"] + kXF(blk, cc, cw), writes=kXF(blk, cc, cw))

        def apply_internal(blk, cc, cw):
            sch.op("vector",
                   (lambda b=blk, c=cc, w=cw: nc.vector.tensor_sub(
                       TH0[:, 0:w], vXN(b, c, w), PSB[6][:, 0:w])),
                   reads=kXN(blk, cc, cw) + ["PS6"], writes=["TH0"])
            sch.op("vector",
                   (lambda b=blk, c=cc, w=cw: nc.vector.tensor_mul(
                       TH1[:, 0:w], TH0[:, 0:w], PSB[5][:, 0:w])),
                   reads=["TH0", "PS5"], writes=["TH1"])
            sch.op("vector",
                   (lambda b=blk, c=cc, w=cw: nc.vector.tensor_scalar(
                       vXN(b, c, w), TH1[:, 0:w],
                       VEC[:, 0 * KB + b:0 * KB + b + 1],
                       VEC[:, 1 * KB + b:1 * KB + b + 1],
                       ALU.mult, ALU.add)),
                   reads=["TH1", "VEC"], writes=kXN(blk, cc, cw))

        def gating(chunks):
            c0 = chunks[0][0]
            cN = chunks[-1][0] + chunks[-1][1]
            glist_all = group_list(c0, cN)
            for gi, gc, gw in glist_all:
                for blk in range(KB):
                    sch.op("tensor",
                           (lambda b=blk, g=gc, w=gw, s=(blk == 0),
                                   e_=(blk == KB - 1):
                            nc.tensor.matmul(PSB[6][0:w, 0:8],
                                             vXN(b, g, w),
                                             GWB[:, b * E:(b + 1) * E],
                                             start=s, stop=e_)),
                           reads=kXN(blk, gc, gw) + ["GWB"], writes=["PS6"],
                           inc=(blk == KB - 1))
                sch.op("vector",
                       (lambda i=gi, w=gw: nc.vector.tensor_add(
                           SCR[0:w, i * 8:(i + 1) * 8], PSB[6][0:w, 0:8],
                           GBB[0:w, 0:8])),
                       reads=["PS6", "GBB"], writes=["SCR"])
            if "sc0" in _TAPS and chunks[0][0] == 0:
                sch.dma("sync", lambda: nc.sync.dma_start(
                    taps["sc0"], SCR[:]), reads=["SCR"])
            for gi, gc, gw in glist_all:
                sch.op("vector",
                       (lambda i=gi, w=gw: nc.vector.reduce_max(
                           M1T[0:w, i:i + 1], SCR[0:w, i * 8:(i + 1) * 8],
                           axis=AXX)),
                       reads=["SCR"], writes=["M1T"])
                sch.op("vector",
                       (lambda i=gi, w=gw: nc.vector.tensor_tensor(
                           EQ1S[0:w, i * 8:(i + 1) * 8],
                           SCR[0:w, i * 8:(i + 1) * 8],
                           M1T[0:w, i:i + 1].broadcast_to([w, 8]),
                           ALU.is_equal)),
                       reads=["SCR", "M1T"], writes=["EQ1S"])
                sch.op("vector",
                       (lambda i=gi, w=gw: nc.vector.scalar_tensor_tensor(
                           MSK8[0:w, 0:8], EQ1S[0:w, i * 8:(i + 1) * 8], -1e30,
                           SCR[0:w, i * 8:(i + 1) * 8], ALU.mult, ALU.add)),
                       reads=["EQ1S", "SCR"], writes=["MSK8"])
                sch.op("vector",
                       (lambda i=gi, w=gw: nc.vector.reduce_max(
                           M2T[0:w, i:i + 1], MSK8[0:w, 0:8], axis=AXX)),
                       reads=["MSK8"], writes=["M2T"])
                sch.op("vector",
                       (lambda i=gi, w=gw: nc.vector.tensor_tensor(
                           EQ2S[0:w, i * 8:(i + 1) * 8], MSK8[0:w, 0:8],
                           M2T[0:w, i:i + 1].broadcast_to([w, 8]),
                           ALU.is_equal)),
                       reads=["MSK8", "M2T"], writes=["EQ2S"])
            sch.op("vector", lambda: nc.vector.tensor_sub(
                G1T[:, 0:9], M1T[:, 0:9], M2T[:, 0:9]),
                reads=["M1T", "M2T"], writes=["G1T"])
            sch.op("scalar", lambda: nc.scalar.activation(
                G1T[:, 0:9], G1T[:, 0:9], AF.Sigmoid),
                reads=["G1T"], writes=["G1T"])
            sch.op("vector", lambda: nc.vector.tensor_scalar(
                G2T[:, 0:9], G1T[:, 0:9], -1.0, 1.0, ALU.mult, ALU.add),
                reads=["G1T"], writes=["G2T"])
            for gi, gc, gw in glist_all:
                sch.op("vector",
                       (lambda i=gi, w=gw: nc.vector.tensor_scalar(
                           TMP8[0:w, 0:8], EQ2S[0:w, i * 8:(i + 1) * 8],
                           G2T[0:w, i:i + 1], None, ALU.mult)),
                       reads=["EQ2S", "G2T"], writes=["TMP8"])
                sch.op("vector",
                       (lambda i=gi, w=gw: nc.vector.scalar_tensor_tensor(
                           SCR[0:w, i * 8:(i + 1) * 8],
                           EQ1S[0:w, i * 8:(i + 1) * 8],
                           G1T[0:w, i:i + 1], TMP8[0:w, 0:8],
                           ALU.mult, ALU.add)),
                       reads=["EQ1S", "G1T", "TMP8"], writes=["SCR"])
                sch.op("vector",
                       (lambda i=gi, w=gw: nc.vector.tensor_copy(
                           CRB[0:w, i * 8:(i + 1) * 8],
                           SCR[0:w, i * 8:(i + 1) * 8])),
                       reads=["SCR"], writes=["CRB"])
            for gi, gc, gw in glist_all:
                sch.op("tensor",
                       (lambda i=gi, w=gw: nc.tensor.transpose(
                           PST[0:8, 0:w], CRB[0:w, i * 8:(i + 1) * 8],
                           IDB[0:w, 0:w])),
                       reads=["CRB", "IDB"], writes=["PS7"])
                sch.op("scalar",
                       (lambda g=gc, w=gw: nc.scalar.activation(
                           COMB05[0:8, g:g + w], PST[0:8, 0:w], AF.Copy,
                           scale=0.5)),
                       reads=["PS7"], writes=["COMB05"])

        def comb_bcast(e, chunks):
            c0 = chunks[0][0]
            cN = chunks[-1][0] + chunks[-1][1]
            groups = group_list(c0, cN)
            for rnd in range(3):
                base = c0 + rnd * 512
                glist = [(gi, gc, gw) for gi, gc, gw in groups
                         if base <= gc < base + 512]
                if not glist:
                    continue
                for j, (gi, gc, gw) in enumerate(glist):
                    sch.op("tensor",
                           (lambda i=gi, g=gc, w=gw, bb=base:
                            nc.tensor.matmul(
                                PSB[5][0:1, g - bb:g - bb + w],
                                CRB[0:w, i * 8 + e:i * 8 + e + 1],
                                IDB[0:w, 0:w], start=True, stop=True)),
                           reads=["CRB", "IDB"], writes=["PS5"],
                           inc=(j == len(glist) - 1))
                lo = base
                hi = min(base + 512, cN)
                sch.op("scalar",
                       (lambda l=lo, h=hi, bb=base: nc.scalar.activation(
                           CROW[0:1, 0:h - l],
                           PSB[5][0:1, 0:h - l],
                           AF.Copy, scale=0.5)),
                       reads=["PS5"], writes=["CROW"])
                sch.op("tensor",
                       (lambda l=lo, h=hi: nc.tensor.matmul(
                           PSB[6][:, 0:h - l], ONE1F[:],
                           CROW[0:1, 0:h - l],
                           start=True, stop=True)),
                       reads=["CROW", "ONE1F"], writes=["PS6"])
                sch.op("scalar",
                       (lambda l=lo, h=hi: nc.scalar.copy(
                           COMBB[:, l:h], PSB[6][:, 0:h - l])),
                       reads=["PS6"], writes=["COMBB"])

        def w2_lhsT(nb, db):
            if nb < 8:
                return B2[:, nb * D + db * 128:nb * D + db * 128 + 128]
            return W2X[:, (nb - 8) * D + db * 128:(nb - 8) * D + db * 128 + 128]

        def kw2(nb, db):
            if nb < 8:
                return kB2c(nb * D + db * 128, nb * D + db * 128 + 128)
            return kW2X((nb - 8) * D + db * 128, (nb - 8) * D + db * 128 + 128)

        def ffn(chunks, mmch, tag):
            layer_norm(chunks, False, apply_plain)
            gating(chunks)
            if tag == "f1" and "xn1" in _TAPS:
                sch.dma("sync", lambda: nc.sync.dma_start(
                    taps["xn1"], XN[:]),
                    reads=[f"XN:{i}" for i in range(KB)])
            if tag == "f1" and "comb" in _TAPS:
                sch.dma("sync", lambda: nc.sync.dma_start(
                    taps["comb"], COMB05[0:8, :]), reads=["COMB05"])
            if tag == "f1" and "sc1" in _TAPS:
                sch.dma("sync", lambda: nc.sync.dma_start(
                    taps["sc1"], SCR[:]), reads=["SCR"])
            for e in range(E):
                sch.dma("gpsimd",
                        (lambda ee=e: nc.gpsimd.dma_start(
                            B2[:, 0:8 * D].rearrange("p (n d) -> p n d", n=8),
                            d_w2T[ee][0:8 * 128, :]
                            .rearrange("(n p) d -> p n d", p=128))),
                        writes=kB2c(0, 8 * D))
                sch.dma("gpsimd",
                        (lambda ee=e: nc.gpsimd.dma_start(
                            W2X[:].rearrange("p (n d) -> p n d", n=3),
                            d_w2T[ee][8 * 128:11 * 128, :]
                            .rearrange("(n p) d -> p n d", p=128))),
                        writes=kW2X(0, 3 * D))
                comb_bcast(e, chunks)
                for nb in range(NB):
                    for kh in range(2):
                        p = (nb * 2 + kh) % 2
                        wg = WG0 if p == 0 else WG1
                        wk = f"WG{p}"
                        sch.dma("gpsimd",
                                (lambda ee=e, n=nb, k=kh, w_=wg:
                                 nc.gpsimd.dma_start(
                                     w_[:].rearrange("p (b n) -> p b n", b=8),
                                     d_w1T[ee, k * 1024:(k + 1) * 1024,
                                           n * 128:(n + 1) * 128]
                                     .rearrange("(b p) n -> p b n", p=128))),
                                writes=[wk])
                    for ci, (cc, cw) in enumerate(mmch):
                        pb = ci % 2
                        for kh in range(2):
                            p = (nb * 2 + kh) % 2
                            wg = WG0 if p == 0 else WG1
                            wk = f"WG{p}"
                            for b8 in range(8):
                                blk = kh * 8 + b8
                                sch.op("tensor",
                                       (lambda b=blk, bb=b8, c=cc, w=cw, pp=pb,
                                               w_=wg, s=(kh == 0 and b8 == 0),
                                               e_=(kh == 1 and b8 == 7):
                                        nc.tensor.matmul(
                                            psum(pp)[:, 0:w],
                                            w_[:, bb * 128:(bb + 1) * 128],
                                            vXN(b, c, w), start=s, stop=e_)),
                                       reads=kXN(blk, cc, cw) + [wk],
                                       writes=[f"PS{pb}"],
                                       inc=(kh == 1 and b8 == 7))
                        th = TH0 if (ci + nb) % 2 == 0 else TH1
                        thk = "TH0" if (ci + nb) % 2 == 0 else "TH1"
                        sch.op("scalar",
                               (lambda ee=e, n=nb, w=cw, pp=pb, t=th:
                                nc.scalar.activation(
                                    t[:, 0:w], psum(pp)[:, 0:w], AF.Silu,
                                    bias=B1R[:, ee * NB + n:ee * NB + n + 1])),
                               reads=[f"PS{pb}", "B1R"], writes=[thk])
                        sch.op("vector",
                               (lambda n=nb, c=cc, w=cw, t=th:
                                nc.vector.tensor_mul(
                                    B1[:, n * M1 + c:n * M1 + c + w],
                                    t[:, 0:w], COMBB[:, c:c + w])),
                               reads=[thk, "COMBB"],
                               writes=kB1c(nb * M1 + cc, nb * M1 + cc + cw))
                b2c_here = (e == 0)
                for ci, (cc, cw) in enumerate(mmch):
                    for db in range(KB):
                        pi = 2 + ((ci * KB + db) % 4)
                        for nb in range(NB):
                            last = (nb == NB - 1) and not b2c_here
                            sch.op("tensor",
                                   (lambda n=nb, d=db, c=cc, w=cw, p=pi,
                                           s=(nb == 0), e_=last:
                                    nc.tensor.matmul(
                                        psum(p)[:, 0:w], w2_lhsT(n, d),
                                        B1[:, n * M1 + c:n * M1 + c + w],
                                        start=s, stop=e_)),
                                   reads=kB1c(nb * M1 + cc, nb * M1 + cc + cw)
                                   + kw2(nb, db),
                                   writes=[f"PS{pi}"], inc=last)
                        if b2c_here:
                            sch.op("tensor",
                                   (lambda d=db, c=cc, w=cw, p=pi:
                                    nc.tensor.matmul(
                                        psum(p)[:, 0:w],
                                        B2B[0:8, d * 128:d * 128 + 128],
                                        COMB05[0:8, c:c + w],
                                        start=False, stop=True)),
                                   reads=["B2B", "COMB05"], writes=[f"PS{pi}"],
                                   inc=True)
                        sch.op("vector",
                               (lambda d=db, c=cc, w=cw, p=pi:
                                nc.vector.tensor_add(
                                    vXF(d, c, w), vXF(d, c, w),
                                    psum(p)[:, 0:w])),
                               reads=[f"PS{pi}"] + kXF(db, cc, cw),
                               writes=kXF(db, cc, cw))

        ffn(CH_FULL, MM_FULL, "f1")

        if "x1" in _TAPS:
            sch.dma("sync", lambda: nc.sync.dma_start(
                taps["x1"].rearrange("(b p) t -> p b t", p=128),
                XF[:].rearrange("p (b t) -> p b t", b=KB)),
                reads=[f"XF:{i}" for i in range(KB)])

        layer_norm(CH_FULL, False, apply_plain_s)
        if "xh" in _TAPS:
            sch.dma("sync", lambda: nc.sync.dma_start(taps["xh"], XN[:]),
                    reads=[f"XN:{i}" for i in range(KB)])
        sch.dma("sync", lambda: nc.sync.dma_start(d_xsp[:, :], XF[:]),
                reads=[f"XF:{i}" for i in range(KB)], writes=["dXSP"])
        layer_norm(CH_FULL, True, apply_internal)
        for blk in range(KB):
            sch.op("vector",
                   (lambda b=blk: nc.vector.tensor_mul(
                       XN[:, b * M1:b * M1 + 1], XN[:, b * M1:b * M1 + 1],
                       PV[:, 0:1])),
                   reads=kXN(blk, 0, 1) + ["PV"], writes=kXN(blk, 0, 1))
        for blk in range(KB):
            for h in range(2):
                sch.op("vector",
                       (lambda b=blk, o=h * 512: nc.vector.tensor_sub(
                           TSQ0[:, 0:512], vXN(b, 1 + o, 512), vXN(b, o, 512))),
                       reads=kXN(blk, h * 512, 513), writes=["TSQ0"])
                sch.op("vector",
                       (lambda b=blk, o=h * 512: nc.vector.scalar_tensor_tensor(
                           B1[:, b * M + o:b * M + o + 512], TSQ0[:, 0:512],
                           VEC[:, 2 * KB + b:2 * KB + b + 1], vXN(b, o, 512),
                           ALU.mult, ALU.add)),
                       reads=["TSQ0", "VEC"] + kXN(blk, h * 512, 512),
                       writes=kB1c(blk * M + h * 512, blk * M + h * 512 + 512))
                sch.op("vector",
                       (lambda b=blk, o=h * 512: nc.vector.scalar_tensor_tensor(
                           B2[:, b * M + o:b * M + o + 512], TSQ0[:, 0:512],
                           VEC[:, 3 * KB + b:3 * KB + b + 1], vXN(b, o, 512),
                           ALU.mult, ALU.add)),
                       reads=["TSQ0", "VEC"] + kXN(blk, h * 512, 512),
                       writes=kB2c(blk * M + h * 512, blk * M + h * 512 + 512))

        def proj_phase(d_w, gslot, out_emit, rhs_B1):
            off = gslot * D
            for db in range(KB):
                sch.dma("gpsimd",
                        (lambda d=db, o=off: nc.gpsimd.dma_start(
                            W2X[:, o:o + D].rearrange("p (b n) -> p b n", b=KB),
                            d_w[:, d * 128:(d + 1) * 128]
                            .rearrange("(b p) n -> p b n", p=128))),
                        writes=kW2X(off, off + D))
                for mc in range(2):
                    p = (gslot * 4) + (db % 2) * 2 + mc
                    srcb = B1 if rhs_B1 else B2
                    ksrc = kB1c if rhs_B1 else kB2c
                    for blk in range(KB):
                        sch.op("tensor",
                               (lambda b=blk, m=mc, pp=p, o=off, s_=srcb,
                                       s=(blk == 0), e_=(blk == KB - 1):
                                nc.tensor.matmul(
                                    psum(pp)[:, 0:512],
                                    W2X[:, o + b * 128:o + b * 128 + 128],
                                    s_[:, b * M + m * 512:b * M + m * 512 + 512],
                                    start=s, stop=e_)),
                               reads=ksrc(blk * M + mc * 512,
                                          blk * M + mc * 512 + 512)
                               + kW2X(off, off + D),
                               writes=[f"PS{p}"], inc=(blk == KB - 1))
                    out_emit(db, mc, p)

        def k_out(db, mc, p):
            sch.op("scalar",
                   (lambda d=db, m=mc, pp=p: nc.scalar.activation(
                       vEK(d, m * 512, 512), psum(pp)[:, 0:512], AF.Exp)),
                   reads=[f"PS{p}"], writes=kEK(db, mc * 512, 512))

        def v_out(db, mc, p):
            sch.op("vector",
                   (lambda d=db, m=mc, pp=p: nc.vector.tensor_mul(
                       vZ(d, m * 512, 512), psum(pp)[:, 0:512],
                       vEK(d, m * 512, 512))),
                   reads=[f"PS{p}"] + kEK(db, mc * 512, 512),
                   writes=kZ(db, mc * 512, 512))

        if "xn2" in _TAPS:
            sch.dma("sync", lambda: nc.sync.dma_start(taps["xn2"], XN[:]),
                    reads=[f"XN:{i}" for i in range(KB)])
        proj_phase(d_wk, 0, k_out, rhs_B1=True)
        proj_phase(d_wv, 1, v_out, rhs_B1=False)
        if "ekz" in _TAPS:
            sch.dma("sync", lambda: nc.sync.dma_start(
                taps["ekz"], XFbf[:, 0:2 * KB * M]),
                reads=[f"XF:{i}" for i in range(KB)])

        def emit_scans(pass2):
            for db in range(KB):
                if pass2:
                    sch.op("vector",
                           (lambda d=db: nc.vector.tensor_copy(
                               B1[:, d * M1:d * M1 + 1], AIN[:, d:d + 1])),
                           reads=["AIN"], writes=kB1c(db * M1, db * M1 + 1))
                    sch.op("vector",
                           (lambda d=db: nc.vector.tensor_copy(
                               B2[:, d * M1:d * M1 + 1],
                               AIN[:, KB + d:KB + d + 1])),
                           reads=["AIN"], writes=kB2c(db * M1, db * M1 + 1))
                sch.op("vector",
                       (lambda d=db, p2=pass2: nc.vector.tensor_tensor_scan(
                           B1[:, d * M1 + 1:d * M1 + 1 + M],
                           DD[:, d:d + 1].broadcast_to([128, M]),
                           vZ(d, 0, M),
                           (AIN[:, d:d + 1] if p2 else 0.0),
                           ALU.mult, ALU.add)),
                       reads=kZ(db, 0, M) + ["DD"] + (["AIN"] if pass2 else []),
                       writes=kB1c(db * M1 + 1, db * M1 + 1 + M))
                sch.op("vector",
                       (lambda d=db, p2=pass2: nc.vector.tensor_tensor_scan(
                           B2[:, d * M1 + 1:d * M1 + 1 + M],
                           DD[:, d:d + 1].broadcast_to([128, M]),
                           vEK(d, 0, M),
                           (AIN[:, KB + d:KB + d + 1] if p2 else 0.0),
                           ALU.mult, ALU.add)),
                       reads=kEK(db, 0, M) + ["DD"] + (["AIN"] if pass2 else []),
                       writes=kB2c(db * M1 + 1, db * M1 + 1 + M))
                if not pass2:
                    sch.op("vector",
                           (lambda d=db: nc.vector.tensor_copy(
                               AGP[:, d:d + 1],
                               B1[:, d * M1 + M:d * M1 + M + 1])),
                           reads=kB1c(db * M1 + M, db * M1 + M + 1),
                           writes=["AGP"])
                    sch.op("vector",
                           (lambda d=db: nc.vector.tensor_copy(
                               AGP[:, KB + d:KB + d + 1],
                               B2[:, d * M1 + M:d * M1 + M + 1])),
                           reads=kB2c(db * M1 + M, db * M1 + M + 1),
                           writes=["AGP"])

        emit_scans(pass2=False)
        sch.dma("gpsimd", lambda: nc.gpsimd.dma_start(d_agi[:, :], AGP[:]),
                reads=["AGP"], writes=["dAGI"])
        sch.collective(lambda: nc.gpsimd.collective_compute(
            "AllGather", ALU.bypass,
            replica_groups=[list(range(NCORES))],
            ins=[d_agi.ap().opt()],
            outs=[d_ago.ap().opt()]),
            reads=["dAGI"], writes=["dAGO"])
        sch.dma("gpsimd", lambda: nc.gpsimd.dma_start(
            AGALL[:].rearrange("p (r c) -> p r c", r=NCORES),
            d_ago.ap().rearrange("(r p) c -> p r c", p=128)),
            reads=["dAGO"], writes=["AGALL"])
        sch.op("vector", lambda: nc.vector.memset(AIN[:], 0.0), writes=["AIN"])
        for r in range(NCORES):
            sch.op("vector",
                   (lambda rr=r: nc.vector.scalar_tensor_tensor(
                       AIN[:], AGALL[:, rr * 2 * KB:(rr + 1) * 2 * KB],
                       SEL[:, rr:rr + 1], AIN[:], ALU.mult, ALU.add)),
                   reads=["AGALL", "SEL", "AIN"], writes=["AIN"])
        emit_scans(pass2=True)

        for db in range(KB):
            for h in range(2):
                o = h * 512
                sch.op("vector",
                       (lambda d=db, oo=o: nc.vector.tensor_scalar(
                           TH0[:, 0:512], vZ(d, oo, 512), EU[:, d:d + 1],
                           None, ALU.mult)),
                       reads=kZ(db, o, 512) + ["EU"], writes=["TH0"])
                sch.op("vector",
                       (lambda d=db, oo=o: nc.vector.tensor_add(
                           TH0[:, 0:512], TH0[:, 0:512],
                           B1[:, d * M1 + oo:d * M1 + oo + 512])),
                       reads=["TH0"] + kB1c(db * M1 + o, db * M1 + o + 512),
                       writes=["TH0"])
                sch.op("vector",
                       (lambda d=db, oo=o: nc.vector.scalar_tensor_tensor(
                           TH1[:, 0:512], vEK(d, oo, 512), EU[:, d:d + 1],
                           B2[:, d * M1 + oo:d * M1 + oo + 512],
                           ALU.mult, ALU.add)),
                       reads=kEK(db, o, 512) + ["EU"]
                       + kB2c(db * M1 + o, db * M1 + o + 512),
                       writes=["TH1"])
                sch.op("scalar",
                       (lambda: nc.scalar.activation(
                           PSB[6][:, 0:512], TH1[:, 0:512], AF.Ln)),
                       reads=["TH1"], writes=["PS6"])
                sch.op("scalar",
                       (lambda: nc.scalar.activation(
                           TH1[:, 0:512], PSB[6][:, 0:512], AF.Exp,
                           scale=-1.0)),
                       reads=["PS6"], writes=["TH1"])
                sch.op("vector",
                       (lambda d=db, oo=o: nc.vector.tensor_mul(
                           vZ(d, oo, 512), TH0[:, 0:512], TH1[:, 0:512])),
                       reads=["TH0", "TH1"], writes=kZ(db, o, 512))

        if "wkv" in _TAPS:
            sch.dma("sync", lambda: nc.sync.dma_start(
                taps["wkv"], XFbf[:, 0:2 * KB * M]),
                reads=[f"XF:{i}" for i in range(KB)])
        if "scn" in _TAPS:
            sch.dma("sync", lambda: nc.sync.dma_start(
                taps["scn"][:, 0:KB * M1], B1[:]),
                reads=[f"B1:{i}" for i in range(KB)])
            sch.dma("sync", lambda: nc.sync.dma_start(
                taps["scn"][:, KB * M1:2 * KB * M1], B2[:]),
                reads=[f"B2:{i}" for i in range(KB)])
        for blk in range(KB):
            for h in range(2):
                sch.op("vector",
                       (lambda b=blk, o=h * 512: nc.vector.tensor_sub(
                           TSQ0[:, 0:512], vXN(b, 1 + o, 512), vXN(b, o, 512))),
                       reads=kXN(blk, h * 512, 513), writes=["TSQ0"])
                sch.op("vector",
                       (lambda b=blk, o=h * 512: nc.vector.scalar_tensor_tensor(
                           B2[:, b * M + o:b * M + o + 512], TSQ0[:, 0:512],
                           VEC[:, 4 * KB + b:4 * KB + b + 1], vXN(b, o, 512),
                           ALU.mult, ALU.add)),
                       reads=["TSQ0", "VEC"] + kXN(blk, h * 512, 512),
                       writes=kB2c(blk * M + h * 512, blk * M + h * 512 + 512))

        def r_out(db, mc, p):
            th = TH0 if mc == 0 else TH1
            thk = "TH0" if mc == 0 else "TH1"
            sch.op("scalar",
                   (lambda pp=p, t=th: nc.scalar.activation(
                       t[:, 0:512], psum(pp)[:, 0:512], AF.Sigmoid)),
                   reads=[f"PS{p}"], writes=[thk])
            sch.op("vector",
                   (lambda d=db, m=mc, t=th: nc.vector.tensor_mul(
                       B1[:, d * M + m * 512:d * M + m * 512 + 512],
                       t[:, 0:512], vZ(d, m * 512, 512))),
                   reads=[thk] + kZ(db, mc * 512, 512),
                   writes=kB1c(db * M + mc * 512, db * M + mc * 512 + 512))

        proj_phase(d_wr, 0, r_out, rhs_B1=False)

        sch.dma("sync", lambda: nc.sync.dma_start(XF[:], d_xsp[:, :]),
                reads=["dXSP"], writes=[f"XF:{i}" for i in range(KB)])

        def o_out(db, mc, p):
            cc = 1 + mc * 512
            sch.op("vector",
                   (lambda d=db, c=cc, pp=p: nc.vector.tensor_add(
                       vXF(d, c, 512), vXF(d, c, 512), psum(pp)[:, 0:512])),
                   reads=[f"PS{p}"] + kXF(db, cc, 512),
                   writes=kXF(db, cc, 512))

        proj_phase(d_wo, 1, o_out, rhs_B1=True)

        if "x2" in _TAPS:
            sch.dma("sync", lambda: nc.sync.dma_start(
                taps["x2"].rearrange("(b p) t -> p b t", p=128),
                XF[:].rearrange("p (b t) -> p b t", b=KB)),
                reads=[f"XF:{i}" for i in range(KB)])

        ffn(CH_OWN, CH_OWN, "f2")

        sch.dma("sync", lambda: nc.sync.dma_start(
            d_y.rearrange("(b p) t -> p b t", p=128),
            XF[:].rearrange("p (b t) -> p b t", b=KB)[:, :, 1:M1]),
            reads=[f"XF:{i}" for i in range(KB)])

        sch.final_waits("sync")
        sch.final_waits("gpsimd")
        sch.replay(block, esems, qsems, ccsem)

    return nc


def _stage_vec(v):
    return np.ascontiguousarray(v.reshape(KB, 128).T)


def kernel(**inputs):
    x = np.asarray(inputs["L0_attn_x"], np.float32)
    Wk = np.ascontiguousarray(np.asarray(inputs["L0_attn_Wk"], np.float32))
    Wv = np.ascontiguousarray(np.asarray(inputs["L0_attn_Wv"], np.float32))
    Wr = np.ascontiguousarray(np.asarray(inputs["L0_attn_Wr"], np.float32))
    Wo = np.ascontiguousarray(np.asarray(inputs["L0_attn_Wo"], np.float32))
    w1 = np.asarray(inputs["L0_ffn_w1"], np.float32)
    w2 = np.asarray(inputs["L0_ffn_w2"], np.float32)
    gate_w = np.asarray(inputs["L0_ffn_gate_w"], np.float32)
    gate_b = np.asarray(inputs["L0_ffn_gate_b"], np.float32)
    bias = np.asarray(inputs["L0_ffn_bias"], np.float32)
    b1 = np.asarray(inputs["L0_ffn_b1"], np.float32)
    b2 = np.asarray(inputs["L0_ffn_b2"], np.float32)

    w1T = np.ascontiguousarray(w1.transpose(0, 2, 1))
    w2T = np.ascontiguousarray(w2.transpose(0, 2, 1))
    gwT = np.ascontiguousarray(gate_w.T)
    gb2 = np.ascontiguousarray(np.stack([gate_b, bias], axis=1))
    gbr = np.ascontiguousarray(
        np.concatenate([gate_b, bias]).reshape(1, 2 * E))
    b1r = np.ascontiguousarray(
        b1.reshape(E, NB, 128).transpose(2, 0, 1).reshape(128, E * NB))
    vecs = np.concatenate([
        _stage_vec(np.asarray(inputs["L0_attn_ln_scale"], np.float32)),
        _stage_vec(np.asarray(inputs["L0_attn_ln_bias"], np.float32)),
        _stage_vec(np.asarray(inputs["L0_attn_mix_k"], np.float32)),
        _stage_vec(np.asarray(inputs["L0_attn_mix_v"], np.float32)),
        _stage_vec(np.asarray(inputs["L0_attn_mix_r"], np.float32)),
        _stage_vec(np.asarray(inputs["L0_attn_time_first"], np.float32)),
        _stage_vec(np.asarray(inputs["L0_attn_time_decay"], np.float32)),
    ], axis=1)
    idm = np.eye(128, dtype=np.float32)

    xt = np.ascontiguousarray(x[0].T)

    in_maps = []
    for r in range(NCORES):
        lo = r * M
        if r == 0:
            xs = np.concatenate(
                [np.zeros((D, 1), np.float32), xt[:, 0:M]], axis=1)
        else:
            xs = xt[:, lo - 1:lo + M]
        sel = np.zeros((128, NCORES), np.float32)
        if r > 0:
            sel[:, r - 1] = 1.0
        pv = np.full((128, 1), 0.0 if r == 0 else 1.0, np.float32)
        in_maps.append(dict(
            xT=np.ascontiguousarray(xs), w1T=w1T, w2T=w2T,
            Wk=Wk, Wv=Wv, Wr=Wr, Wo=Wo, gwT=gwT, gb2=gb2, gbr=gbr, vecs=vecs,
            b1r=b1r, b2=np.ascontiguousarray(b2), id128=idm, sel=sel, pv=pv,
        ))

    nc = build_program()
    res = run_bass_kernel_spmd(nc, in_maps, core_ids=list(range(NCORES)))

    out = np.empty((1, T, D), np.float32)
    for r in range(NCORES):
        out[0, r * M:(r + 1) * M, :] = res.results[r]["y"].T
    if _TAPS:
        kernel.taps = [{k: res.results[r]["tap_" + k] for k in _TAPS}
                       for r in range(NCORES)]
    return out


# version 05
# speedup vs baseline: 1.1361x; 1.0275x over previous
import sys
sys.path.insert(0, "/opt/trn_rl_repo")

import contextlib
import numpy as np

import concourse.bass as bass
import concourse.mybir as mybir
from concourse.bass_utils import run_bass_kernel_spmd

F32 = mybir.dt.float32
BF16 = mybir.dt.bfloat16
AF = mybir.ActivationFunctionType
ALU = mybir.AluOpType
AXX = mybir.AxisListType.X

NCORES = 8
D = 2048
KB = 16
T = 8192
M = 1024
M1 = 1025
E = 8
HFF = 1408
NB = 11
EPS = 1e-5

CH_FULL = [(0, 342), (342, 342), (684, 341)]
CH_OWN = [(1, 512), (513, 512)]
MM_FULL = [(0, 512), (512, 512), (1024, 1)]
def group_list(c0, cN):
    out = []
    gi = 0
    gc = c0
    while gc < cN:
        out.append((gi, gc, min(128, cN - gc)))
        gc += 128
        gi += 1
    return out

_TAPS = []


class Sched:
    ENGINES = ("tensor", "vector", "scalar", "gpsimd", "sync")

    def __init__(self):
        self.prog = {e: [] for e in self.ENGINES}
        self.ecount = {e: 0 for e in self.ENGINES}
        self.qcount = {e: 0 for e in self.ENGINES}
        self.ccount = 0
        self.bufs = {}
        self.waited = {}
        self.deferred = {e: [] for e in self.ENGINES}

    def _buf(self, k):
        if k not in self.bufs:
            self.bufs[k] = {"w": None, "r": []}
        return self.bufs[k]

    def _need(self, eng, tag):
        kind, src, val = tag
        if kind == "e" and src == eng:
            if eng == "tensor":
                return False
            return True
        return True

    def _collect_waits(self, eng, reads, writes):
        waits = {}
        for k in reads:
            b = self._buf(k)
            if b["w"] is not None and self._need(eng, b["w"]):
                kind, src, val = b["w"]
                waits[(kind, src)] = max(waits.get((kind, src), 0), val)
        for k in writes:
            b = self._buf(k)
            if b["w"] is not None and self._need(eng, b["w"]):
                kind, src, val = b["w"]
                waits[(kind, src)] = max(waits.get((kind, src), 0), val)
            for tag in b["r"]:
                if self._need(eng, tag):
                    kind, src, val = tag
                    waits[(kind, src)] = max(waits.get((kind, src), 0), val)
        out = []
        for key, val in waits.items():
            if val > self.waited.get((eng, key), 0):
                self.waited[(eng, key)] = val
                out.append((key, val))
        return out

    def _register(self, tag, reads, writes):
        for k in writes:
            b = self._buf(k)
            b["w"] = tag
            b["r"] = []
        for k in reads:
            self._buf(k)["r"].append(tag)

    def op(self, eng, fn, reads=(), writes=(), inc=True):
        reads, writes = list(reads), list(writes)
        for key, val in self._collect_waits(eng, reads, writes):
            self.prog[eng].append(("wait", key, val))
        if inc:
            self.ecount[eng] += 1
            tag = ("e", eng, self.ecount[eng])
            self.prog[eng].append(("op", fn, True))
            for r, w in self.deferred[eng]:
                self._register(tag, r, w)
            self.deferred[eng] = []
            self._register(tag, reads, writes)
        else:
            self.prog[eng].append(("op", fn, False))
            self.deferred[eng].append((reads, writes))

    def dma(self, qeng, fn, reads=(), writes=()):
        reads, writes = list(reads), list(writes)
        for key, val in self._collect_waits(qeng, reads, writes):
            self.prog[qeng].append(("wait", key, val))
        self.qcount[qeng] += 1
        val = 16 * self.qcount[qeng]
        tag = ("q", qeng, val)
        self.prog[qeng].append(("dma", fn, val))
        self._register(tag, reads, writes)

    def collective(self, fn, reads=(), writes=()):
        eng = "gpsimd"
        for key, val in self._collect_waits(eng, list(reads), list(writes)):
            self.prog[eng].append(("wait", key, val))
        self.ccount += 1
        tag = ("c", "cc", self.ccount)
        self.prog[eng].append(("cc", fn, self.ccount))
        self._register(tag, list(reads), list(writes))

    def final_waits(self, eng):
        for q in self.ENGINES:
            if self.qcount[q]:
                key, val = ("q", q), 16 * self.qcount[q]
                if self.waited.get((eng, key), 0) < val:
                    self.waited[(eng, key)] = val
                    self.prog[eng].append(("wait", key, val))

    def replay(self, block, esems, qsems, ccsem):
        def run_engine(eng, engobj):
            for entry in self.prog[eng]:
                if entry[0] == "wait":
                    (kind, src), val = entry[1], entry[2]
                    sem = esems[src] if kind == "e" else (
                        qsems[src] if kind == "q" else ccsem)
                    engobj.wait_ge(sem, val)
                elif entry[0] == "op":
                    instr = entry[1]()
                    if entry[2]:
                        instr.then_inc(esems[eng], 1)
                elif entry[0] == "dma":
                    entry[1]().then_inc(qsems[eng], 16)
                else:
                    entry[1]().then_inc(ccsem, 1)

        @block.tensor
        def _(t):
            run_engine("tensor", t)

        @block.vector
        def _(v):
            run_engine("vector", v)

        @block.scalar
        def _(s):
            run_engine("scalar", s)

        @block.gpsimd
        def _(g):
            run_engine("gpsimd", g)

        @block.sync
        def _(s):
            run_engine("sync", s)


def strips(region, b0, b1, strip_bytes):
    lo = b0 // strip_bytes
    hi = (b1 - 1) // strip_bytes
    return [f"{region}:{i}" for i in range(lo, hi + 1)]


def build_program():
    nc = bass.Bass("TRN2", target_bir_lowering=False)

    d_x = nc.dram_tensor("xT", [D, M1], F32, kind="ExternalInput").ap()
    d_w1T = nc.dram_tensor("w1T", [E, D, HFF], F32, kind="ExternalInput").ap()
    d_w2T = nc.dram_tensor("w2T", [E, HFF, D], F32, kind="ExternalInput").ap()
    d_wk = nc.dram_tensor("Wk", [D, D], F32, kind="ExternalInput").ap()
    d_wv = nc.dram_tensor("Wv", [D, D], F32, kind="ExternalInput").ap()
    d_wr = nc.dram_tensor("Wr", [D, D], F32, kind="ExternalInput").ap()
    d_wo = nc.dram_tensor("Wo", [D, D], F32, kind="ExternalInput").ap()
    d_gwT = nc.dram_tensor("gwT", [D, E], F32, kind="ExternalInput").ap()
    d_gb = nc.dram_tensor("gb2", [E, 2], F32, kind="ExternalInput").ap()
    d_gbr = nc.dram_tensor("gbr", [1, 2 * E], F32, kind="ExternalInput").ap()
    d_vecs = nc.dram_tensor("vecs", [128, 7 * KB], F32, kind="ExternalInput").ap()
    d_b1r = nc.dram_tensor("b1r", [128, E * NB], F32, kind="ExternalInput").ap()
    d_b2 = nc.dram_tensor("b2", [E, D], F32, kind="ExternalInput").ap()
    d_id = nc.dram_tensor("id128", [128, 128], F32, kind="ExternalInput").ap()
    d_sel = nc.dram_tensor("sel", [128, NCORES], F32, kind="ExternalInput").ap()
    d_pv = nc.dram_tensor("pv", [128, 1], F32, kind="ExternalInput").ap()

    d_y = nc.dram_tensor("y", [D, M], F32, kind="ExternalOutput").ap()
    taps = {}
    TAP_SHAPES = {"x0": ([D, M1], F32), "x1": ([D, M1], F32),
                  "x2": ([D, M1], F32),
                  "xn1": ([128, KB * M1], BF16), "comb": ([8, M1], BF16),
                  "hs0": ([128, NB * M1], BF16), "sc1": ([128, 9 * 8], F32),
                  "sc0": ([128, 9 * 8], F32),
                  "xn2": ([128, KB * M1], BF16),
                  "xh": ([128, KB * M1], BF16),
                  "ekz": ([128, 2 * KB * M], BF16),
                  "wkv": ([128, 2 * KB * M], BF16),
                  "scn": ([128, 2 * KB * M1], BF16)}
    for tname in _TAPS:
        shp, dt = TAP_SHAPES[tname]
        taps[tname] = nc.dram_tensor("tap_" + tname, shp, dt,
                                     kind="ExternalOutput").ap()

    d_xsp = nc.dram_tensor("xspill", [128, KB * M1], F32)
    d_agi = nc.dram_tensor("agi", [128, 2 * KB], F32)
    d_ago = nc.dram_tensor("ago", [NCORES * 128, 2 * KB], F32)

    with contextlib.ExitStack() as ctx:
        sb = lambda name, shape, dt: ctx.enter_context(nc.sbuf_tensor(name, shape, dt))
        pst_ = lambda name, shape, dt: ctx.enter_context(nc.psum_tensor(name, shape, dt))

        XF = sb("XF", [128, KB * M1], F32)
        XN = sb("XN", [128, KB * M1], BF16)
        B1 = sb("B1", [128, KB * M1], BF16)
        B2 = sb("B2", [128, KB * M1], BF16)
        W2X = sb("W2X", [128, 3 * D], BF16)
        WG0 = sb("WG0", [128, 8 * 128], BF16)
        WG1 = sb("WG1", [128, 8 * 128], BF16)
        SMEAN = sb("SMEAN", [1, 512], F32)
        SRSTD = sb("SRSTD", [1, 512], F32)
        STMP = sb("STMP", [1, 512], F32)
        TH0 = sb("TH0", [128, 512], F32)
        TH1 = sb("TH1", [128, 512], F32)
        TSQ0 = sb("TSQ0", [128, 512], BF16)
        TSQ1 = sb("TSQ1", [128, 512], BF16)
        VEC = sb("VEC", [128, 7 * KB], F32)
        DD = sb("DD", [128, KB], F32)
        EU = sb("EU", [128, KB], F32)
        B1R = sb("B1R", [128, E * NB], F32)
        B2B = sb("B2B", [8, D], BF16)
        GWB = sb("GWB", [128, KB * E], BF16)
        GB8 = sb("GB8", [8, 1], F32)
        GBB2 = sb("GBB2", [128, 16], F32)
        GBB = sb("GBB", [128, 8], F32)
        GB2 = sb("GB2", [8, 2], F32)
        IDB = sb("IDB", [128, 128], BF16)
        ONE1B = sb("ONE1B", [1, 128], BF16)
        ONE1F = sb("ONE1F", [1, 128], F32)
        ONEKB = sb("ONEKB", [128, 1], BF16)
        EPSB = sb("EPSB", [1, 1], F32)
        SEL = sb("SEL", [128, NCORES], F32)
        PV = sb("PV", [128, 1], F32)
        AGP = sb("AGP", [128, 2 * KB], F32)
        AGALL = sb("AGALL", [128, NCORES * 2 * KB], BF16)
        AIN = sb("AIN", [128, 2 * KB], F32)
        IDF = sb("IDF", [128, 16], F32)
        SCR = sb("SCR", [128, 9 * 8], F32)
        CRB = sb("CRB", [128, 9 * 8], BF16)
        EQ1S = sb("EQ1S", [128, 9 * 8], BF16)
        EQ2S = sb("EQ2S", [128, 9 * 8], BF16)
        MSK8 = sb("MSK8", [128, 8], F32)
        TMP8 = sb("TMP8", [128, 8], F32)
        M1T = sb("M1T", [128, 9], F32)
        M2T = sb("M2T", [128, 9], F32)
        G1T = sb("G1T", [128, 9], F32)
        G2T = sb("G2T", [128, 9], F32)
        COMB05 = sb("COMB05", [8, M1], BF16)
        CROW = sb("CROW", [1, 512], F32)
        COMBB = sb("COMBB", [128, M1], BF16)

        PSB = [pst_(f"psb{i}", [128, 512], F32) for i in range(7)]
        PST = pst_("pst", [128, 1024], BF16)
        PS7F = PST[:].bitcast(F32)

        def psum(i):
            return PSB[i][:] if i < 7 else PS7F

        esems = {e: ctx.enter_context(nc.semaphore("se_" + e))
                 for e in Sched.ENGINES}
        qsems = {e: ctx.enter_context(nc.semaphore("sq_" + e))
                 for e in ("gpsimd", "sync", "scalar")}
        ccsem = ctx.enter_context(nc.semaphore("scc"))
        block = ctx.enter_context(nc.Block())

        sch = Sched()

        def kXF(blk, c0=0, w=M1):
            return strips("XF", (blk * M1 + c0) * 4, (blk * M1 + c0 + w) * 4, M1 * 4)

        def kXFbf(bc0, bc1):
            return strips("XF", bc0 * 2, bc1 * 2, M1 * 4)

        def kEK(blk, c0=0, w=M):
            return kXFbf(blk * M + c0, blk * M + c0 + w)

        def kZ(blk, c0=0, w=M):
            return kXFbf(KB * M + blk * M + c0, KB * M + blk * M + c0 + w)

        def kXN(blk, c0=0, w=M1):
            return strips("XN", (blk * M1 + c0) * 2, (blk * M1 + c0 + w) * 2, M1 * 2)

        def kB1c(bc0, bc1):
            return strips("B1", bc0 * 2, bc1 * 2, M1 * 2)

        def kB2c(bc0, bc1):
            return strips("B2", bc0 * 2, bc1 * 2, M1 * 2)

        def kW2X(bc0, bc1):
            return strips("W2X", bc0 * 2, bc1 * 2, 768)

        XFbf = XF[:].bitcast(BF16)

        def vXF(blk, c0, w):
            return XF[:, blk * M1 + c0: blk * M1 + c0 + w]

        def vEK(blk, c0, w):
            return XFbf[:, blk * M + c0: blk * M + c0 + w]

        def vZ(blk, c0, w):
            return XFbf[:, KB * M + blk * M + c0: KB * M + blk * M + c0 + w]

        def vXN(blk, c0, w):
            return XN[:, blk * M1 + c0: blk * M1 + c0 + w]

        sch.dma("sync", lambda: nc.sync.dma_start(
            XF[:].rearrange("p (b t) -> p b t", b=KB),
            d_x.rearrange("(b p) t -> p b t", p=128)),
            writes=[f"XF:{i}" for i in range(KB)])
        sch.dma("sync", lambda: nc.sync.dma_start(VEC[:], d_vecs), writes=["VEC"])
        sch.dma("sync", lambda: nc.sync.dma_start(B1R[:], d_b1r), writes=["B1R"])
        sch.dma("sync", lambda: nc.sync.dma_start(SEL[:], d_sel), writes=["SEL"])
        sch.dma("sync", lambda: nc.sync.dma_start(PV[:], d_pv), writes=["PV"])
        sch.dma("sync", lambda: nc.sync.dma_start(GB2[:], d_gb), writes=["GB2"])
        sch.dma("gpsimd", lambda: nc.gpsimd.dma_start(
            GBB2[:], d_gbr.broadcast_to([128, 2 * E])), writes=["GBB2"])
        sch.dma("gpsimd", lambda: nc.gpsimd.dma_start(B2B[:], d_b2), writes=["B2B"])
        sch.dma("gpsimd", lambda: nc.gpsimd.dma_start(IDB[:], d_id), writes=["IDB"])
        sch.dma("sync", lambda: nc.sync.dma_start(IDF[:], d_id[:, 0:16]), writes=["IDF"])
        sch.dma("gpsimd", lambda: nc.gpsimd.dma_start(
            GWB[:].rearrange("p (b e) -> p b e", b=KB),
            d_gwT.rearrange("(b p) e -> p b e", p=128)), writes=["GWB"])

        if "x0" in _TAPS:
            sch.dma("sync", lambda: nc.sync.dma_start(
                taps["x0"].rearrange("(b p) t -> p b t", p=128),
                XF[:].rearrange("p (b t) -> p b t", b=KB)),
                reads=[f"XF:{i}" for i in range(KB)])
        sch.op("vector", lambda: nc.vector.memset(ONE1B[:], 1.0), writes=["ONE1B"])
        sch.op("vector", lambda: nc.vector.memset(ONE1F[:], 1.0), writes=["ONE1F"])
        sch.op("vector", lambda: nc.vector.memset(ONEKB[:], 1.0), writes=["ONEKB"])
        sch.op("vector", lambda: nc.vector.memset(EPSB[:], EPS), writes=["EPSB"])
        sch.op("vector", lambda: nc.vector.tensor_add(GB8[:], GB2[:, 0:1], GB2[:, 1:2]),
               reads=["GB2"], writes=["GB8"])
        sch.op("vector", lambda: nc.vector.tensor_add(
            GBB[:, 0:8], GBB2[:, 0:8], GBB2[:, 8:16]),
            reads=["GBB2"], writes=["GBB"])
        sch.op("scalar", lambda: nc.scalar.activation(
            EU[:], VEC[:, 5 * KB:6 * KB], AF.Exp), reads=["VEC"], writes=["EU"])
        sch.op("scalar", lambda: nc.scalar.activation(
            DD[:], VEC[:, 6 * KB:7 * KB], AF.Exp), reads=["VEC"], writes=["DD"])
        sch.op("scalar", lambda: nc.scalar.activation(
            DD[:], DD[:], AF.Exp, scale=-1.0), reads=["DD"], writes=["DD"])

        def layer_norm(chunks, src_xn, apply_fn):
            for (cc, cw) in chunks:
                for blk in range(KB):
                    p = blk % 2
                    tile = TSQ0 if p == 0 else TSQ1
                    if not src_xn:
                        sch.op("scalar",
                               (lambda b=blk, t=tile, c=cc, w=cw:
                                nc.scalar.copy(t[:, 0:w], vXF(b, c, w))),
                               reads=kXF(blk, cc, cw), writes=[f"TSQ{p}"])
                    else:
                        sch.op("scalar",
                               (lambda b=blk, t=tile, c=cc, w=cw:
                                nc.scalar.copy(t[:, 0:w], vXN(b, c, w))),
                               reads=kXN(blk, cc, cw), writes=[f"TSQ{p}"])
                    mmk = [f"TSQ{p}"]
                    mm_in = (lambda t=tile, w=cw: t[:, 0:w])
                    tq = TSQ1 if p == 0 else TSQ0
                    tqk = f"TSQ{1 - p}"
                    sch.op("vector",
                           (lambda f=mm_in, t=tq, w=cw:
                            nc.vector.tensor_mul(t[:, 0:w], f(), f())),
                           reads=mmk, writes=[tqk])
                    sch.op("tensor",
                           (lambda f=mm_in, w=cw, s=(blk == 0),
                                   e_=(blk == KB - 1):
                            nc.tensor.matmul(PSB[5][0:1, 0:w], ONEKB[:], f(),
                                             start=s, stop=e_)),
                           reads=mmk + ["ONEKB"], writes=["PS5"],
                           inc=(blk == KB - 1))
                    sch.op("tensor",
                           (lambda t=tq, w=cw, s=(blk == 0), e_=(blk == KB - 1):
                            nc.tensor.matmul(PSB[6][0:1, 0:w], ONEKB[:],
                                             t[:, 0:w], start=s, stop=e_)),
                           reads=[tqk, "ONEKB"], writes=["PS6"],
                           inc=(blk == KB - 1))
                sch.op("scalar",
                       (lambda w=cw: nc.scalar.activation(
                           SMEAN[:, 0:w], PSB[5][0:1, 0:w], AF.Copy,
                           scale=1.0 / D)),
                       reads=["PS5"], writes=["SMEAN"])
                sch.op("scalar",
                       (lambda w=cw: nc.scalar.activation(
                           STMP[:, 0:w], PSB[6][0:1, 0:w], AF.Copy,
                           scale=1.0 / D)),
                       reads=["PS6"], writes=["STMP"])
                sch.op("vector",
                       (lambda w=cw: nc.vector.tensor_mul(
                           SRSTD[:, 0:w], SMEAN[:, 0:w], SMEAN[:, 0:w])),
                       reads=["SMEAN"], writes=["SRSTD"])
                sch.op("vector",
                       (lambda w=cw: nc.vector.tensor_sub(
                           SRSTD[:, 0:w], STMP[:, 0:w], SRSTD[:, 0:w])),
                       reads=["STMP", "SRSTD"], writes=["SRSTD"])
                sch.op("scalar",
                       (lambda w=cw: nc.scalar.activation(
                           SRSTD[:, 0:w], SRSTD[:, 0:w], AF.Ln,
                           bias=EPSB[0:1, 0:1])),
                       reads=["SRSTD", "EPSB"], writes=["SRSTD"])
                sch.op("scalar",
                       (lambda w=cw: nc.scalar.activation(
                           SRSTD[:, 0:w], SRSTD[:, 0:w], AF.Exp, scale=-0.5)),
                       reads=["SRSTD"], writes=["SRSTD"])
                sch.op("tensor",
                       (lambda w=cw: nc.tensor.matmul(
                           PSB[5][:, 0:w], ONE1F[:], SRSTD[:, 0:w],
                           start=True, stop=True)),
                       reads=["SRSTD", "ONE1F"], writes=["PS5"])
                sch.op("tensor",
                       (lambda w=cw: nc.tensor.matmul(
                           PSB[6][:, 0:w], ONE1F[:], SMEAN[:, 0:w],
                           start=True, stop=True)),
                       reads=["SMEAN", "ONE1F"], writes=["PS6"])
                for blk in range(KB):
                    apply_fn(blk, cc, cw)

        def apply_plain(blk, cc, cw):
            sch.op("vector",
                   (lambda b=blk, c=cc, w=cw: nc.vector.tensor_sub(
                       TH0[:, 0:w], vXF(b, c, w), PSB[6][:, 0:w])),
                   reads=kXF(blk, cc, cw) + ["PS6"], writes=["TH0"])
            sch.op("vector",
                   (lambda b=blk, c=cc, w=cw: nc.vector.tensor_mul(
                       vXN(b, c, w), TH0[:, 0:w], PSB[5][:, 0:w])),
                   reads=["TH0", "PS5"], writes=kXN(blk, cc, cw))

        def apply_plain_s(blk, cc, cw):
            sch.op("vector",
                   (lambda b=blk, c=cc, w=cw: nc.vector.tensor_sub(
                       TH0[:, 0:w], vXF(b, c, w), PSB[6][:, 0:w])),
                   reads=kXF(blk, cc, cw) + ["PS6"], writes=["TH0"])
            sch.op("vector",
                   (lambda b=blk, c=cc, w=cw: nc.vector.tensor_mul(
                       TH1[:, 0:w], TH0[:, 0:w], PSB[5][:, 0:w])),
                   reads=["TH0", "PS5"], writes=["TH1"])
            sch.op("vector",
                   (lambda b=blk, c=cc, w=cw: nc.vector.tensor_copy(
                       vXN(b, c, w), TH1[:, 0:w])),
                   reads=["TH1"], writes=kXN(blk, cc, cw))
            sch.op("vector",
                   (lambda b=blk, c=cc, w=cw: nc.vector.tensor_add(
                       vXF(b, c, w), vXF(b, c, w), TH1[:, 0:w])),
                   reads=["TH1"] + kXF(blk, cc, cw), writes=kXF(blk, cc, cw))

        def apply_internal(blk, cc, cw):
            sch.op("vector",
                   (lambda b=blk, c=cc, w=cw: nc.vector.tensor_sub(
                       TH0[:, 0:w], vXN(b, c, w), PSB[6][:, 0:w])),
                   reads=kXN(blk, cc, cw) + ["PS6"], writes=["TH0"])
            sch.op("vector",
                   (lambda b=blk, c=cc, w=cw: nc.vector.tensor_mul(
                       TH1[:, 0:w], TH0[:, 0:w], PSB[5][:, 0:w])),
                   reads=["TH0", "PS5"], writes=["TH1"])
            sch.op("vector",
                   (lambda b=blk, c=cc, w=cw: nc.vector.tensor_scalar(
                       vXN(b, c, w), TH1[:, 0:w],
                       VEC[:, 0 * KB + b:0 * KB + b + 1],
                       VEC[:, 1 * KB + b:1 * KB + b + 1],
                       ALU.mult, ALU.add)),
                   reads=["TH1", "VEC"], writes=kXN(blk, cc, cw))

        def gating(chunks):
            c0 = chunks[0][0]
            cN = chunks[-1][0] + chunks[-1][1]
            glist_all = group_list(c0, cN)
            for gi, gc, gw in glist_all:
                for blk in range(KB):
                    sch.op("tensor",
                           (lambda b=blk, g=gc, w=gw, s=(blk == 0),
                                   e_=(blk == KB - 1):
                            nc.tensor.matmul(PSB[6][0:w, 0:8],
                                             vXN(b, g, w),
                                             GWB[:, b * E:(b + 1) * E],
                                             start=s, stop=e_)),
                           reads=kXN(blk, gc, gw) + ["GWB"], writes=["PS6"],
                           inc=(blk == KB - 1))
                sch.op("vector",
                       (lambda i=gi, w=gw: nc.vector.tensor_add(
                           SCR[0:w, i * 8:(i + 1) * 8], PSB[6][0:w, 0:8],
                           GBB[0:w, 0:8])),
                       reads=["PS6", "GBB"], writes=["SCR"])
            if "sc0" in _TAPS and chunks[0][0] == 0:
                sch.dma("sync", lambda: nc.sync.dma_start(
                    taps["sc0"], SCR[:]), reads=["SCR"])
            for gi, gc, gw in glist_all:
                sch.op("vector",
                       (lambda i=gi, w=gw: nc.vector.reduce_max(
                           M1T[0:w, i:i + 1], SCR[0:w, i * 8:(i + 1) * 8],
                           axis=AXX)),
                       reads=["SCR"], writes=["M1T"])
                sch.op("vector",
                       (lambda i=gi, w=gw: nc.vector.tensor_tensor(
                           EQ1S[0:w, i * 8:(i + 1) * 8],
                           SCR[0:w, i * 8:(i + 1) * 8],
                           M1T[0:w, i:i + 1].broadcast_to([w, 8]),
                           ALU.is_equal)),
                       reads=["SCR", "M1T"], writes=["EQ1S"])
                sch.op("vector",
                       (lambda i=gi, w=gw: nc.vector.scalar_tensor_tensor(
                           MSK8[0:w, 0:8], EQ1S[0:w, i * 8:(i + 1) * 8], -1e30,
                           SCR[0:w, i * 8:(i + 1) * 8], ALU.mult, ALU.add)),
                       reads=["EQ1S", "SCR"], writes=["MSK8"])
                sch.op("vector",
                       (lambda i=gi, w=gw: nc.vector.reduce_max(
                           M2T[0:w, i:i + 1], MSK8[0:w, 0:8], axis=AXX)),
                       reads=["MSK8"], writes=["M2T"])
                sch.op("vector",
                       (lambda i=gi, w=gw: nc.vector.tensor_tensor(
                           EQ2S[0:w, i * 8:(i + 1) * 8], MSK8[0:w, 0:8],
                           M2T[0:w, i:i + 1].broadcast_to([w, 8]),
                           ALU.is_equal)),
                       reads=["MSK8", "M2T"], writes=["EQ2S"])
            sch.op("vector", lambda: nc.vector.tensor_sub(
                G1T[:, 0:9], M1T[:, 0:9], M2T[:, 0:9]),
                reads=["M1T", "M2T"], writes=["G1T"])
            sch.op("scalar", lambda: nc.scalar.activation(
                G1T[:, 0:9], G1T[:, 0:9], AF.Sigmoid),
                reads=["G1T"], writes=["G1T"])
            sch.op("vector", lambda: nc.vector.tensor_scalar(
                G2T[:, 0:9], G1T[:, 0:9], -1.0, 1.0, ALU.mult, ALU.add),
                reads=["G1T"], writes=["G2T"])
            for gi, gc, gw in glist_all:
                sch.op("vector",
                       (lambda i=gi, w=gw: nc.vector.tensor_scalar(
                           TMP8[0:w, 0:8], EQ2S[0:w, i * 8:(i + 1) * 8],
                           G2T[0:w, i:i + 1], None, ALU.mult)),
                       reads=["EQ2S", "G2T"], writes=["TMP8"])
                sch.op("vector",
                       (lambda i=gi, w=gw: nc.vector.scalar_tensor_tensor(
                           SCR[0:w, i * 8:(i + 1) * 8],
                           EQ1S[0:w, i * 8:(i + 1) * 8],
                           G1T[0:w, i:i + 1], TMP8[0:w, 0:8],
                           ALU.mult, ALU.add)),
                       reads=["EQ1S", "G1T", "TMP8"], writes=["SCR"])
                sch.op("vector",
                       (lambda i=gi, w=gw: nc.vector.tensor_copy(
                           CRB[0:w, i * 8:(i + 1) * 8],
                           SCR[0:w, i * 8:(i + 1) * 8])),
                       reads=["SCR"], writes=["CRB"])
            for gi, gc, gw in glist_all:
                sch.op("tensor",
                       (lambda i=gi, w=gw: nc.tensor.transpose(
                           PST[0:8, 0:w], CRB[0:w, i * 8:(i + 1) * 8],
                           IDB[0:w, 0:w])),
                       reads=["CRB", "IDB"], writes=["PS7"])
                sch.op("scalar",
                       (lambda g=gc, w=gw: nc.scalar.activation(
                           COMB05[0:8, g:g + w], PST[0:8, 0:w], AF.Copy,
                           scale=0.5)),
                       reads=["PS7"], writes=["COMB05"])

        def comb_bcast(e, chunks):
            c0 = chunks[0][0]
            cN = chunks[-1][0] + chunks[-1][1]
            groups = group_list(c0, cN)
            for rnd in range(3):
                base = c0 + rnd * 512
                glist = [(gi, gc, gw) for gi, gc, gw in groups
                         if base <= gc < base + 512]
                if not glist:
                    continue
                for j, (gi, gc, gw) in enumerate(glist):
                    sch.op("tensor",
                           (lambda i=gi, g=gc, w=gw, bb=base:
                            nc.tensor.matmul(
                                PSB[5][0:1, g - bb:g - bb + w],
                                CRB[0:w, i * 8 + e:i * 8 + e + 1],
                                IDB[0:w, 0:w], start=True, stop=True)),
                           reads=["CRB", "IDB"], writes=["PS5"],
                           inc=(j == len(glist) - 1))
                lo = base
                hi = min(base + 512, cN)
                sch.op("scalar",
                       (lambda l=lo, h=hi, bb=base: nc.scalar.activation(
                           CROW[0:1, 0:h - l],
                           PSB[5][0:1, 0:h - l],
                           AF.Copy, scale=0.5)),
                       reads=["PS5"], writes=["CROW"])
                sch.op("tensor",
                       (lambda l=lo, h=hi: nc.tensor.matmul(
                           PSB[6][:, 0:h - l], ONE1F[:],
                           CROW[0:1, 0:h - l],
                           start=True, stop=True)),
                       reads=["CROW", "ONE1F"], writes=["PS6"])
                sch.op("scalar",
                       (lambda l=lo, h=hi: nc.scalar.copy(
                           COMBB[:, l:h], PSB[6][:, 0:h - l])),
                       reads=["PS6"], writes=["COMBB"])

        def w2_lhsT(nb, db):
            if nb < 8:
                return B2[:, nb * D + db * 128:nb * D + db * 128 + 128]
            return W2X[:, (nb - 8) * D + db * 128:(nb - 8) * D + db * 128 + 128]

        def kw2(nb, db):
            if nb < 8:
                return kB2c(nb * D + db * 128, nb * D + db * 128 + 128)
            return kW2X((nb - 8) * D + db * 128, (nb - 8) * D + db * 128 + 128)

        def ffn(chunks, mmch, tag):
            layer_norm(chunks, False, apply_plain)
            gating(chunks)
            if tag == "f1" and "xn1" in _TAPS:
                sch.dma("sync", lambda: nc.sync.dma_start(
                    taps["xn1"], XN[:]),
                    reads=[f"XN:{i}" for i in range(KB)])
            if tag == "f1" and "comb" in _TAPS:
                sch.dma("sync", lambda: nc.sync.dma_start(
                    taps["comb"], COMB05[0:8, :]), reads=["COMB05"])
            if tag == "f1" and "sc1" in _TAPS:
                sch.dma("sync", lambda: nc.sync.dma_start(
                    taps["sc1"], SCR[:]), reads=["SCR"])
            for e in range(E):
                sch.dma("gpsimd",
                        (lambda ee=e: nc.gpsimd.dma_start(
                            B2[:, 0:8 * D].rearrange("p (n d) -> p n d", n=8),
                            d_w2T[ee][0:8 * 128, :]
                            .rearrange("(n p) d -> p n d", p=128))),
                        writes=kB2c(0, 8 * D))
                sch.dma("gpsimd",
                        (lambda ee=e: nc.gpsimd.dma_start(
                            W2X[:].rearrange("p (n d) -> p n d", n=3),
                            d_w2T[ee][8 * 128:11 * 128, :]
                            .rearrange("(n p) d -> p n d", p=128))),
                        writes=kW2X(0, 3 * D))
                comb_bcast(e, chunks)
                for nb in range(NB):
                    for kh in range(2):
                        p = (nb * 2 + kh) % 2
                        wg = WG0 if p == 0 else WG1
                        wk = f"WG{p}"
                        sch.dma("gpsimd",
                                (lambda ee=e, n=nb, k=kh, w_=wg:
                                 nc.gpsimd.dma_start(
                                     w_[:].rearrange("p (b n) -> p b n", b=8),
                                     d_w1T[ee, k * 1024:(k + 1) * 1024,
                                           n * 128:(n + 1) * 128]
                                     .rearrange("(b p) n -> p b n", p=128))),
                                writes=[wk])
                    for ci, (cc, cw) in enumerate(mmch):
                        pb = ci % 2
                        for kh in range(2):
                            p = (nb * 2 + kh) % 2
                            wg = WG0 if p == 0 else WG1
                            wk = f"WG{p}"
                            for b8 in range(8):
                                blk = kh * 8 + b8
                                sch.op("tensor",
                                       (lambda b=blk, bb=b8, c=cc, w=cw, pp=pb,
                                               w_=wg, s=(kh == 0 and b8 == 0),
                                               e_=(kh == 1 and b8 == 7):
                                        nc.tensor.matmul(
                                            psum(pp)[:, 0:w],
                                            w_[:, bb * 128:(bb + 1) * 128],
                                            vXN(b, c, w), start=s, stop=e_)),
                                       reads=kXN(blk, cc, cw) + [wk],
                                       writes=[f"PS{pb}"],
                                       inc=(kh == 1 and b8 == 7))
                        th = TH0 if (ci + nb) % 2 == 0 else TH1
                        thk = "TH0" if (ci + nb) % 2 == 0 else "TH1"
                        sch.op("scalar",
                               (lambda ee=e, n=nb, w=cw, pp=pb, t=th:
                                nc.scalar.activation(
                                    t[:, 0:w], psum(pp)[:, 0:w], AF.Silu,
                                    bias=B1R[:, ee * NB + n:ee * NB + n + 1])),
                               reads=[f"PS{pb}", "B1R"], writes=[thk])
                        sch.op("vector",
                               (lambda n=nb, c=cc, w=cw, t=th:
                                nc.vector.tensor_mul(
                                    B1[:, n * M1 + c:n * M1 + c + w],
                                    t[:, 0:w], COMBB[:, c:c + w])),
                               reads=[thk, "COMBB"],
                               writes=kB1c(nb * M1 + cc, nb * M1 + cc + cw))
                b2c_here = (e == 0)
                for ci, (cc, cw) in enumerate(mmch):
                    for db in range(KB):
                        pi = 2 + ((ci * KB + db) % 4)
                        for nb in range(NB):
                            last = (nb == NB - 1) and not b2c_here
                            sch.op("tensor",
                                   (lambda n=nb, d=db, c=cc, w=cw, p=pi,
                                           s=(nb == 0), e_=last:
                                    nc.tensor.matmul(
                                        psum(p)[:, 0:w], w2_lhsT(n, d),
                                        B1[:, n * M1 + c:n * M1 + c + w],
                                        start=s, stop=e_)),
                                   reads=kB1c(nb * M1 + cc, nb * M1 + cc + cw)
                                   + kw2(nb, db),
                                   writes=[f"PS{pi}"], inc=last)
                        if b2c_here:
                            sch.op("tensor",
                                   (lambda d=db, c=cc, w=cw, p=pi:
                                    nc.tensor.matmul(
                                        psum(p)[:, 0:w],
                                        B2B[0:8, d * 128:d * 128 + 128],
                                        COMB05[0:8, c:c + w],
                                        start=False, stop=True)),
                                   reads=["B2B", "COMB05"], writes=[f"PS{pi}"],
                                   inc=True)
                        sch.op("vector",
                               (lambda d=db, c=cc, w=cw, p=pi:
                                nc.vector.tensor_add(
                                    vXF(d, c, w), vXF(d, c, w),
                                    psum(p)[:, 0:w])),
                               reads=[f"PS{pi}"] + kXF(db, cc, cw),
                               writes=kXF(db, cc, cw))

        ffn(CH_FULL, CH_FULL, "f1")

        if "x1" in _TAPS:
            sch.dma("sync", lambda: nc.sync.dma_start(
                taps["x1"].rearrange("(b p) t -> p b t", p=128),
                XF[:].rearrange("p (b t) -> p b t", b=KB)),
                reads=[f"XF:{i}" for i in range(KB)])

        layer_norm(CH_FULL, False, apply_plain_s)
        if "xh" in _TAPS:
            sch.dma("sync", lambda: nc.sync.dma_start(taps["xh"], XN[:]),
                    reads=[f"XN:{i}" for i in range(KB)])
        sch.dma("sync", lambda: nc.sync.dma_start(d_xsp[:, :], XF[:]),
                reads=[f"XF:{i}" for i in range(KB)], writes=["dXSP"])
        layer_norm(CH_FULL, True, apply_internal)
        for blk in range(KB):
            sch.op("vector",
                   (lambda b=blk: nc.vector.tensor_mul(
                       XN[:, b * M1:b * M1 + 1], XN[:, b * M1:b * M1 + 1],
                       PV[:, 0:1])),
                   reads=kXN(blk, 0, 1) + ["PV"], writes=kXN(blk, 0, 1))
        for blk in range(KB):
            for h in range(2):
                sch.op("vector",
                       (lambda b=blk, o=h * 512: nc.vector.tensor_sub(
                           TSQ0[:, 0:512], vXN(b, 1 + o, 512), vXN(b, o, 512))),
                       reads=kXN(blk, h * 512, 513), writes=["TSQ0"])
                sch.op("vector",
                       (lambda b=blk, o=h * 512: nc.vector.scalar_tensor_tensor(
                           B1[:, b * M + o:b * M + o + 512], TSQ0[:, 0:512],
                           VEC[:, 2 * KB + b:2 * KB + b + 1], vXN(b, o, 512),
                           ALU.mult, ALU.add)),
                       reads=["TSQ0", "VEC"] + kXN(blk, h * 512, 512),
                       writes=kB1c(blk * M + h * 512, blk * M + h * 512 + 512))
                sch.op("vector",
                       (lambda b=blk, o=h * 512: nc.vector.scalar_tensor_tensor(
                           B2[:, b * M + o:b * M + o + 512], TSQ0[:, 0:512],
                           VEC[:, 3 * KB + b:3 * KB + b + 1], vXN(b, o, 512),
                           ALU.mult, ALU.add)),
                       reads=["TSQ0", "VEC"] + kXN(blk, h * 512, 512),
                       writes=kB2c(blk * M + h * 512, blk * M + h * 512 + 512))

        def proj_phase(d_w, gslot, out_emit, rhs_B1):
            off = gslot * D
            for db in range(KB):
                sch.dma("gpsimd",
                        (lambda d=db, o=off: nc.gpsimd.dma_start(
                            W2X[:, o:o + D].rearrange("p (b n) -> p b n", b=KB),
                            d_w[:, d * 128:(d + 1) * 128]
                            .rearrange("(b p) n -> p b n", p=128))),
                        writes=kW2X(off, off + D))
                for mc in range(2):
                    p = (gslot * 4) + (db % 2) * 2 + mc
                    srcb = B1 if rhs_B1 else B2
                    ksrc = kB1c if rhs_B1 else kB2c
                    for blk in range(KB):
                        sch.op("tensor",
                               (lambda b=blk, m=mc, pp=p, o=off, s_=srcb,
                                       s=(blk == 0), e_=(blk == KB - 1):
                                nc.tensor.matmul(
                                    psum(pp)[:, 0:512],
                                    W2X[:, o + b * 128:o + b * 128 + 128],
                                    s_[:, b * M + m * 512:b * M + m * 512 + 512],
                                    start=s, stop=e_)),
                               reads=ksrc(blk * M + mc * 512,
                                          blk * M + mc * 512 + 512)
                               + kW2X(off, off + D),
                               writes=[f"PS{p}"], inc=(blk == KB - 1))
                    out_emit(db, mc, p)

        def k_out(db, mc, p):
            sch.op("scalar",
                   (lambda d=db, m=mc, pp=p: nc.scalar.activation(
                       vEK(d, m * 512, 512), psum(pp)[:, 0:512], AF.Exp)),
                   reads=[f"PS{p}"], writes=kEK(db, mc * 512, 512))

        def v_out(db, mc, p):
            sch.op("vector",
                   (lambda d=db, m=mc, pp=p: nc.vector.tensor_mul(
                       vZ(d, m * 512, 512), psum(pp)[:, 0:512],
                       vEK(d, m * 512, 512))),
                   reads=[f"PS{p}"] + kEK(db, mc * 512, 512),
                   writes=kZ(db, mc * 512, 512))

        if "xn2" in _TAPS:
            sch.dma("sync", lambda: nc.sync.dma_start(taps["xn2"], XN[:]),
                    reads=[f"XN:{i}" for i in range(KB)])
        proj_phase(d_wk, 0, k_out, rhs_B1=True)
        for blk in range(KB):
            for h in range(2):
                sch.op("vector",
                       (lambda b=blk, o=h * 512: nc.vector.tensor_sub(
                           TSQ0[:, 0:512], vXN(b, 1 + o, 512), vXN(b, o, 512))),
                       reads=kXN(blk, h * 512, 513), writes=["TSQ0"])
                sch.op("vector",
                       (lambda b=blk, o=h * 512: nc.vector.scalar_tensor_tensor(
                           B1[:, b * M + o:b * M + o + 512], TSQ0[:, 0:512],
                           VEC[:, 4 * KB + b:4 * KB + b + 1], vXN(b, o, 512),
                           ALU.mult, ALU.add)),
                       reads=["TSQ0", "VEC"] + kXN(blk, h * 512, 512),
                       writes=kB1c(blk * M + h * 512, blk * M + h * 512 + 512))
        proj_phase(d_wv, 1, v_out, rhs_B1=False)

        def scr_a(par):
            return B2[:, par * 2050:par * 2050 + 1025]

        def scr_b(par):
            return B2[:, par * 2050 + 1025:par * 2050 + 2050]

        def kscr(par):
            return kB2c(par * 2050, par * 2050 + 2050)

        for db in range(KB):
            par = db % 2
            sch.op("vector",
                   (lambda d=db, p=par: nc.vector.tensor_tensor_scan(
                       scr_a(p)[:, 0:M],
                       DD[:, d:d + 1].broadcast_to([128, M]),
                       vZ(d, 0, M), 0.0, ALU.mult, ALU.add)),
                   reads=kZ(db, 0, M) + ["DD"], writes=kscr(par)[:1])
            sch.op("vector",
                   (lambda d=db, p=par: nc.vector.tensor_tensor_scan(
                       scr_b(p)[:, 0:M],
                       DD[:, d:d + 1].broadcast_to([128, M]),
                       vEK(d, 0, M), 0.0, ALU.mult, ALU.add)),
                   reads=kEK(db, 0, M) + ["DD"], writes=kscr(par)[1:])
            sch.op("vector",
                   (lambda d=db, p=par: nc.vector.tensor_copy(
                       AGP[:, d:d + 1], scr_a(p)[:, M - 1:M])),
                   reads=kscr(par)[:1], writes=["AGP"])
            sch.op("vector",
                   (lambda d=db, p=par: nc.vector.tensor_copy(
                       AGP[:, KB + d:KB + d + 1], scr_b(p)[:, M - 1:M])),
                   reads=kscr(par)[1:], writes=["AGP"])
        sch.dma("gpsimd", lambda: nc.gpsimd.dma_start(d_agi[:, :], AGP[:]),
                reads=["AGP"], writes=["dAGI"])
        sch.collective(lambda: nc.gpsimd.collective_compute(
            "AllGather", ALU.bypass,
            replica_groups=[list(range(NCORES))],
            ins=[d_agi.ap().opt()],
            outs=[d_ago.ap().opt()]),
            reads=["dAGI"], writes=["dAGO"])
        sch.dma("gpsimd", lambda: nc.gpsimd.dma_start(
            AGALL[:].rearrange("p (r c) -> p r c", r=NCORES),
            d_ago.ap().rearrange("(r p) c -> p r c", p=128)),
            reads=["dAGO"], writes=["AGALL"])
        sch.op("vector", lambda: nc.vector.memset(AIN[:], 0.0), writes=["AIN"])
        for r in range(NCORES):
            sch.op("vector",
                   (lambda rr=r: nc.vector.scalar_tensor_tensor(
                       AIN[:], AGALL[:, rr * 2 * KB:(rr + 1) * 2 * KB],
                       SEL[:, rr:rr + 1], AIN[:], ALU.mult, ALU.add)),
                   reads=["AGALL", "SEL", "AIN"], writes=["AIN"])

        def r_out(db, mc, p):
            sch.op("scalar",
                   (lambda d=db, m=mc, pp=p: nc.scalar.activation(
                       XN[:, d * M1 + m * 512:d * M1 + m * 512 + 512],
                       psum(pp)[:, 0:512], AF.Sigmoid)),
                   reads=[f"PS{p}"],
                   writes=kXN(db, mc * 512, 512))

        proj_phase(d_wr, 0, r_out, rhs_B1=True)

        for db in range(KB):
            par = db % 2
            sch.op("vector",
                   (lambda d=db, p=par: nc.vector.tensor_copy(
                       scr_a(p)[:, 0:1], AIN[:, d:d + 1])),
                   reads=["AIN"], writes=kscr(par)[:1])
            sch.op("vector",
                   (lambda d=db, p=par: nc.vector.tensor_copy(
                       scr_b(p)[:, 0:1], AIN[:, KB + d:KB + d + 1])),
                   reads=["AIN"], writes=kscr(par)[1:])
            sch.op("vector",
                   (lambda d=db, p=par: nc.vector.tensor_tensor_scan(
                       scr_a(p)[:, 1:1 + M],
                       DD[:, d:d + 1].broadcast_to([128, M]),
                       vZ(d, 0, M), AIN[:, d:d + 1], ALU.mult, ALU.add)),
                   reads=kZ(db, 0, M) + ["DD", "AIN"], writes=kscr(par)[:1])
            sch.op("vector",
                   (lambda d=db, p=par: nc.vector.tensor_tensor_scan(
                       scr_b(p)[:, 1:1 + M],
                       DD[:, d:d + 1].broadcast_to([128, M]),
                       vEK(d, 0, M), AIN[:, KB + d:KB + d + 1],
                       ALU.mult, ALU.add)),
                   reads=kEK(db, 0, M) + ["DD", "AIN"], writes=kscr(par)[1:])
            for h in range(2):
                o = h * 512
                sch.op("vector",
                       (lambda d=db, oo=o, p=par: nc.vector.tensor_scalar(
                           TH0[:, 0:512], vZ(d, oo, 512), EU[:, d:d + 1],
                           None, ALU.mult)),
                       reads=kZ(db, o, 512) + ["EU"], writes=["TH0"])
                sch.op("vector",
                       (lambda d=db, oo=o, p=par: nc.vector.tensor_add(
                           TH0[:, 0:512], TH0[:, 0:512],
                           scr_a(p)[:, oo:oo + 512])),
                       reads=["TH0"] + kscr(par)[:1], writes=["TH0"])
                sch.op("vector",
                       (lambda d=db, oo=o, p=par: nc.vector.scalar_tensor_tensor(
                           TH1[:, 0:512], vEK(d, oo, 512), EU[:, d:d + 1],
                           scr_b(p)[:, oo:oo + 512], ALU.mult, ALU.add)),
                       reads=kEK(db, o, 512) + ["EU"] + kscr(par)[1:],
                       writes=["TH1"])
                sch.op("scalar",
                       (lambda: nc.scalar.activation(
                           PSB[0][:, 0:512], TH1[:, 0:512], AF.Ln)),
                       reads=["TH1"], writes=["PS0"])
                sch.op("scalar",
                       (lambda: nc.scalar.activation(
                           TH1[:, 0:512], PSB[0][:, 0:512], AF.Exp,
                           scale=-1.0)),
                       reads=["PS0"], writes=["TH1"])
                sch.op("vector",
                       (lambda d=db, oo=o: nc.vector.tensor_mul(
                           vZ(d, oo, 512), TH0[:, 0:512], TH1[:, 0:512])),
                       reads=["TH0", "TH1"], writes=kZ(db, o, 512))
                sch.op("vector",
                       (lambda d=db, oo=o: nc.vector.tensor_mul(
                           B1[:, d * M + oo:d * M + oo + 512],
                           XN[:, d * M1 + oo:d * M1 + oo + 512],
                           vZ(d, oo, 512))),
                       reads=kXN(db, o, 512) + kZ(db, o, 512),
                       writes=kB1c(db * M + o, db * M + o + 512))

        sch.dma("sync", lambda: nc.sync.dma_start(XF[:], d_xsp[:, :]),
                reads=["dXSP"], writes=[f"XF:{i}" for i in range(KB)])

        def o_out(db, mc, p):
            cc = 1 + mc * 512
            sch.op("vector",
                   (lambda d=db, c=cc, pp=p: nc.vector.tensor_add(
                       vXF(d, c, 512), vXF(d, c, 512), psum(pp)[:, 0:512])),
                   reads=[f"PS{p}"] + kXF(db, cc, 512),
                   writes=kXF(db, cc, 512))

        proj_phase(d_wo, 1, o_out, rhs_B1=True)

        if "x2" in _TAPS:
            sch.dma("sync", lambda: nc.sync.dma_start(
                taps["x2"].rearrange("(b p) t -> p b t", p=128),
                XF[:].rearrange("p (b t) -> p b t", b=KB)),
                reads=[f"XF:{i}" for i in range(KB)])

        ffn(CH_OWN, CH_OWN, "f2")

        sch.dma("sync", lambda: nc.sync.dma_start(
            d_y.rearrange("(b p) t -> p b t", p=128),
            XF[:].rearrange("p (b t) -> p b t", b=KB)[:, :, 1:M1]),
            reads=[f"XF:{i}" for i in range(KB)])

        sch.final_waits("sync")
        sch.final_waits("gpsimd")
        sch.replay(block, esems, qsems, ccsem)

    return nc


def _stage_vec(v):
    return np.ascontiguousarray(v.reshape(KB, 128).T)


def kernel(**inputs):
    x = np.asarray(inputs["L0_attn_x"], np.float32)
    Wk = np.ascontiguousarray(np.asarray(inputs["L0_attn_Wk"], np.float32))
    Wv = np.ascontiguousarray(np.asarray(inputs["L0_attn_Wv"], np.float32))
    Wr = np.ascontiguousarray(np.asarray(inputs["L0_attn_Wr"], np.float32))
    Wo = np.ascontiguousarray(np.asarray(inputs["L0_attn_Wo"], np.float32))
    w1 = np.asarray(inputs["L0_ffn_w1"], np.float32)
    w2 = np.asarray(inputs["L0_ffn_w2"], np.float32)
    gate_w = np.asarray(inputs["L0_ffn_gate_w"], np.float32)
    gate_b = np.asarray(inputs["L0_ffn_gate_b"], np.float32)
    bias = np.asarray(inputs["L0_ffn_bias"], np.float32)
    b1 = np.asarray(inputs["L0_ffn_b1"], np.float32)
    b2 = np.asarray(inputs["L0_ffn_b2"], np.float32)

    w1T = np.ascontiguousarray(w1.transpose(0, 2, 1))
    w2T = np.ascontiguousarray(w2.transpose(0, 2, 1))
    gwT = np.ascontiguousarray(gate_w.T)
    gb2 = np.ascontiguousarray(np.stack([gate_b, bias], axis=1))
    gbr = np.ascontiguousarray(
        np.concatenate([gate_b, bias]).reshape(1, 2 * E))
    b1r = np.ascontiguousarray(
        b1.reshape(E, NB, 128).transpose(2, 0, 1).reshape(128, E * NB))
    vecs = np.concatenate([
        _stage_vec(np.asarray(inputs["L0_attn_ln_scale"], np.float32)),
        _stage_vec(np.asarray(inputs["L0_attn_ln_bias"], np.float32)),
        _stage_vec(np.asarray(inputs["L0_attn_mix_k"], np.float32)),
        _stage_vec(np.asarray(inputs["L0_attn_mix_v"], np.float32)),
        _stage_vec(np.asarray(inputs["L0_attn_mix_r"], np.float32)),
        _stage_vec(np.asarray(inputs["L0_attn_time_first"], np.float32)),
        _stage_vec(np.asarray(inputs["L0_attn_time_decay"], np.float32)),
    ], axis=1)
    idm = np.eye(128, dtype=np.float32)

    xt = np.ascontiguousarray(x[0].T)

    in_maps = []
    for r in range(NCORES):
        lo = r * M
        if r == 0:
            xs = np.concatenate(
                [np.zeros((D, 1), np.float32), xt[:, 0:M]], axis=1)
        else:
            xs = xt[:, lo - 1:lo + M]
        sel = np.zeros((128, NCORES), np.float32)
        if r > 0:
            sel[:, r - 1] = 1.0
        pv = np.full((128, 1), 0.0 if r == 0 else 1.0, np.float32)
        in_maps.append(dict(
            xT=np.ascontiguousarray(xs), w1T=w1T, w2T=w2T,
            Wk=Wk, Wv=Wv, Wr=Wr, Wo=Wo, gwT=gwT, gb2=gb2, gbr=gbr, vecs=vecs,
            b1r=b1r, b2=np.ascontiguousarray(b2), id128=idm, sel=sel, pv=pv,
        ))

    nc = build_program()
    res = run_bass_kernel_spmd(nc, in_maps, core_ids=list(range(NCORES)))

    out = np.empty((1, T, D), np.float32)
    for r in range(NCORES):
        out[0, r * M:(r + 1) * M, :] = res.results[r]["y"].T
    if _TAPS:
        kernel.taps = [{k: res.results[r]["tap_" + k] for k in _TAPS}
                       for r in range(NCORES)]
    return out


# version 14 (final)
# speedup vs baseline: 1.3202x; 1.1621x over previous
import sys
sys.path.insert(0, "/opt/trn_rl_repo")

import contextlib
import numpy as np

import concourse.bass as bass
import concourse.mybir as mybir
from concourse.bass_utils import run_bass_kernel_spmd

F32 = mybir.dt.float32
BF16 = mybir.dt.bfloat16
AF = mybir.ActivationFunctionType
ALU = mybir.AluOpType
AXX = mybir.AxisListType.X

NCORES = 8
D = 2048
KB = 16
T = 8192
M = 1024
M1 = 1025
E = 8
HFF = 1408
NB = 11
EPS = 1e-5

CH_FULL = [(0, 342), (342, 342), (684, 341)]
CH_OWN = [(1, 512), (513, 512)]
MM_FULL = [(0, 512), (512, 512), (1024, 1)]
def group_list(c0, cN):
    out = []
    gi = 0
    gc = c0
    while gc < cN:
        out.append((gi, gc, min(128, cN - gc)))
        gc += 128
        gi += 1
    return out

_TAPS = []


class Sched:
    ENGINES = ("tensor", "vector", "scalar", "gpsimd", "sync")

    def __init__(self):
        self.prog = {e: [] for e in self.ENGINES}
        self.ecount = {e: 0 for e in self.ENGINES}
        self.qcount = {e: 0 for e in self.ENGINES}
        self.ccount = 0
        self.bufs = {}
        self.waited = {}
        self.deferred = {e: [] for e in self.ENGINES}

    def _buf(self, k):
        if k not in self.bufs:
            self.bufs[k] = {"w": None, "r": []}
        return self.bufs[k]

    def _need(self, eng, tag):
        kind, src, val = tag
        if kind == "e" and src == eng:
            if eng == "tensor":
                return False
            return True
        return True

    def _collect_waits(self, eng, reads, writes):
        waits = {}
        for k in reads:
            b = self._buf(k)
            if b["w"] is not None and self._need(eng, b["w"]):
                kind, src, val = b["w"]
                waits[(kind, src)] = max(waits.get((kind, src), 0), val)
        for k in writes:
            b = self._buf(k)
            if b["w"] is not None and self._need(eng, b["w"]):
                kind, src, val = b["w"]
                waits[(kind, src)] = max(waits.get((kind, src), 0), val)
            for tag in b["r"]:
                if self._need(eng, tag):
                    kind, src, val = tag
                    waits[(kind, src)] = max(waits.get((kind, src), 0), val)
        out = []
        for key, val in waits.items():
            if val > self.waited.get((eng, key), 0):
                self.waited[(eng, key)] = val
                out.append((key, val))
        return out

    def _register(self, tag, reads, writes):
        for k in writes:
            b = self._buf(k)
            b["w"] = tag
            b["r"] = []
        for k in reads:
            self._buf(k)["r"].append(tag)

    def op(self, eng, fn, reads=(), writes=(), inc=True):
        reads, writes = list(reads), list(writes)
        for key, val in self._collect_waits(eng, reads, writes):
            self.prog[eng].append(("wait", key, val))
        if inc:
            self.ecount[eng] += 1
            tag = ("e", eng, self.ecount[eng])
            self.prog[eng].append(("op", fn, True, reads, writes))
            for r, w in self.deferred[eng]:
                self._register(tag, r, w)
            self.deferred[eng] = []
            self._register(tag, reads, writes)
        else:
            self.prog[eng].append(("op", fn, False, reads, writes))
            self.deferred[eng].append((reads, writes))

    def dma(self, qeng, fn, reads=(), writes=()):
        reads, writes = list(reads), list(writes)
        for key, val in self._collect_waits(qeng, reads, writes):
            self.prog[qeng].append(("wait", key, val))
        self.qcount[qeng] += 1
        val = 16 * self.qcount[qeng]
        tag = ("q", qeng, val)
        self.prog[qeng].append(("dma", fn, val, reads, writes))
        self._register(tag, reads, writes)

    def collective(self, fn, reads=(), writes=()):
        eng = "gpsimd"
        for key, val in self._collect_waits(eng, list(reads), list(writes)):
            self.prog[eng].append(("wait", key, val))
        self.ccount += 1
        tag = ("c", "cc", self.ccount)
        self.prog[eng].append(("cc", fn, self.ccount, list(reads), list(writes)))
        self._register(tag, list(reads), list(writes))

    def barrier(self):
        tags = [("e", e, self.ecount[e]) for e in self.ENGINES
                if self.ecount[e] > 0]
        tags += [("q", q, 16 * self.qcount[q]) for q in self.ENGINES
                 if self.qcount[q] > 0]
        if self.ccount:
            tags.append(("c", "cc", self.ccount))
        for eng in self.ENGINES:
            for kind, src_, val in tags:
                if kind == "e" and src_ == eng:
                    continue
                key = (kind, src_)
                if val > self.waited.get((eng, key), 0):
                    self.waited[(eng, key)] = val
                    self.prog[eng].append(("wait", key, val))

    def final_waits(self, eng):
        for q in self.ENGINES:
            if self.qcount[q]:
                key, val = ("q", q), 16 * self.qcount[q]
                if self.waited.get((eng, key), 0) < val:
                    self.waited[(eng, key)] = val
                    self.prog[eng].append(("wait", key, val))

    STRIPW = {"XF": 1025 * 4, "XN": 1025 * 2, "B1": 1025 * 2, "B2": 1025 * 2,
              "W2X": 768}
    PSMAP = {"psb0": "PS0", "psb1": "PS1", "psb2": "PS2", "psb3": "PS3",
             "psb4": "PS4", "psb5": "PS5", "psb6": "PS6", "pst": "PS7"}
    DRMAP = {"xspill": "dXSP", "agi": "dAGI", "ago": "dAGO"}
    IGNORE = {"xT", "w1T", "w2T", "Wk", "Wv", "Wr", "Wo", "gwT", "gb2", "gbr",
              "vecs", "b1r", "b2", "id128", "sel", "pv", "y"}

    def _keys_of(self, pap):
        if not hasattr(pap, "memref"):
            return []
        name = pap.memref
        if name in self.IGNORE or name.startswith("const-") or                 name.startswith("tap_"):
            return []
        import concourse.mybir as _mb
        sz = _mb.dt.size(pap.dtype)
        dims = list(pap.ap)
        span = 0
        for st, ct in dims[1:]:
            span += abs(int(st)) * (int(ct) - 1)
        b0 = int(pap.offset) * sz
        b1 = (int(pap.offset) + span + 1) * sz
        if name in self.PSMAP:
            return [self.PSMAP[name]]
        if name in self.DRMAP:
            return [self.DRMAP[name]]
        if name in self.STRIPW:
            w = self.STRIPW[name]
            return [f"{name}:{i}" for i in range(b0 // w, (b1 - 1) // w + 1)]
        return [name]

    def replay(self, block, esems, qsems, ccsem, verify=False):
        report = []

        def check(eng, idx, instr, reads, writes):
            i = instr.ins
            decl = set()
            for k in list(reads) + list(writes):
                decl.add(k)
            declw = set(writes)
            try:
                real_in = []
                for a in i.ins:
                    real_in += self._keys_of(a)
                real_out = []
                for a in i.outs:
                    real_out += self._keys_of(a)
            except Exception as ex:
                report.append((eng, idx, "EXC", str(ex)))
                return
            miss_r = [k for k in real_in if k not in decl]
            miss_w = [k for k in real_out if k not in declw]
            if miss_r or miss_w:
                report.append((eng, idx, i.opcode,
                               "miss_r=" + ",".join(sorted(set(miss_r))) +
                               " miss_w=" + ",".join(sorted(set(miss_w))) +
                               " decl_r=" + ",".join(sorted(set(reads))) +
                               " decl_w=" + ",".join(sorted(set(writes)))))

        def run_engine(eng, engobj):
            for idx, entry in enumerate(self.prog[eng]):
                if entry[0] == "wait":
                    (kind, src), val = entry[1], entry[2]
                    sem = esems[src] if kind == "e" else (
                        qsems[src] if kind == "q" else ccsem)
                    engobj.wait_ge(sem, val)
                elif entry[0] == "op":
                    instr = entry[1]()
                    if verify:
                        check(eng, idx, instr, entry[3], entry[4])
                    if entry[2]:
                        instr.then_inc(esems[eng], 1)
                elif entry[0] == "dma":
                    instr = entry[1]()
                    if verify:
                        check(eng, idx, instr, entry[3], entry[4])
                    instr.then_inc(qsems[eng], 16)
                else:
                    entry[1]().then_inc(ccsem, 1)
        self._verify_report = report

        @block.tensor
        def _(t):
            run_engine("tensor", t)

        @block.vector
        def _(v):
            run_engine("vector", v)

        @block.scalar
        def _(s):
            run_engine("scalar", s)

        @block.gpsimd
        def _(g):
            run_engine("gpsimd", g)

        @block.sync
        def _(s):
            run_engine("sync", s)


def strips(region, b0, b1, strip_bytes):
    lo = b0 // strip_bytes
    hi = (b1 - 1) // strip_bytes
    return [f"{region}:{i}" for i in range(lo, hi + 1)]


def build_program(verify=False):
    nc = bass.Bass("TRN2", target_bir_lowering=False)

    d_x = nc.dram_tensor("xT", [D, M1], F32, kind="ExternalInput").ap()
    d_w1T = nc.dram_tensor("w1T", [E, D, HFF], F32, kind="ExternalInput").ap()
    d_w2T = nc.dram_tensor("w2T", [E, HFF, D], F32, kind="ExternalInput").ap()
    d_wk = nc.dram_tensor("Wk", [D, D], F32, kind="ExternalInput").ap()
    d_wv = nc.dram_tensor("Wv", [D, D], F32, kind="ExternalInput").ap()
    d_wr = nc.dram_tensor("Wr", [D, D], F32, kind="ExternalInput").ap()
    d_wo = nc.dram_tensor("Wo", [D, D], F32, kind="ExternalInput").ap()
    d_gwT = nc.dram_tensor("gwT", [D, E], F32, kind="ExternalInput").ap()
    d_gb = nc.dram_tensor("gb2", [E, 2], F32, kind="ExternalInput").ap()
    d_gbr = nc.dram_tensor("gbr", [1, 2 * E], F32, kind="ExternalInput").ap()
    d_vecs = nc.dram_tensor("vecs", [128, 7 * KB], F32, kind="ExternalInput").ap()
    d_b1r = nc.dram_tensor("b1r", [128, E * NB], F32, kind="ExternalInput").ap()
    d_b2 = nc.dram_tensor("b2", [E, D], F32, kind="ExternalInput").ap()
    d_id = nc.dram_tensor("id128", [128, 128], F32, kind="ExternalInput").ap()
    d_sel = nc.dram_tensor("sel", [128, NCORES], F32, kind="ExternalInput").ap()
    d_pv = nc.dram_tensor("pv", [128, 1], F32, kind="ExternalInput").ap()

    d_y = nc.dram_tensor("y", [D, M], F32, kind="ExternalOutput").ap()
    taps = {}
    TAP_SHAPES = {"x0": ([D, M1], F32), "x1": ([D, M1], F32),
                  "x2": ([D, M1], F32),
                  "xn1": ([128, KB * M1], BF16), "comb": ([8, M1], BF16),
                  "hs0": ([128, NB * M1], BF16), "sc1": ([128, 9 * 8], F32),
                  "sc0": ([128, 9 * 8], F32),
                  "xn2": ([128, KB * M1], BF16),
                  "xh": ([128, KB * M1], BF16),
                  "ekz": ([128, 2 * KB * M], BF16),
                  "wkv": ([128, 2 * KB * M], BF16),
                  "scn": ([128, 2 * KB * M1], BF16)}
    for tname in _TAPS:
        shp, dt = TAP_SHAPES[tname]
        taps[tname] = nc.dram_tensor("tap_" + tname, shp, dt,
                                     kind="ExternalOutput").ap()

    d_xsp = nc.dram_tensor("xspill", [128, KB * M1], F32)
    d_agi = nc.dram_tensor("agi", [128, 2 * KB], F32)
    d_ago = nc.dram_tensor("ago", [NCORES * 128, 2 * KB], F32)

    with contextlib.ExitStack() as ctx:
        sb = lambda name, shape, dt: ctx.enter_context(nc.sbuf_tensor(name, shape, dt))
        pst_ = lambda name, shape, dt: ctx.enter_context(nc.psum_tensor(name, shape, dt))

        XF = sb("XF", [128, KB * M1], F32)
        XN = sb("XN", [128, KB * M1], BF16)
        B1 = sb("B1", [128, KB * M1], BF16)
        B2 = sb("B2", [128, KB * M1], BF16)
        W2X = sb("W2X", [128, 3 * D], BF16)
        WGS = [sb(f"WG{i}", [128, 8 * 128], BF16) for i in range(4)]
        SMEAN = sb("SMEAN", [1, 512], F32)
        SRSTD = sb("SRSTD", [1, 512], F32)
        STMP = sb("STMP", [1, 512], F32)
        TH0 = sb("TH0", [128, 512], F32)
        TH1 = sb("TH1", [128, 512], F32)
        TSQ0 = sb("TSQ0", [128, 512], BF16)
        TSQ1 = sb("TSQ1", [128, 512], BF16)
        VEC = sb("VEC", [128, 7 * KB], F32)
        DD = sb("DD", [128, KB], F32)
        EU = sb("EU", [128, KB], F32)
        B1R = sb("B1R", [128, E * NB], F32)
        B2B = sb("B2B", [8, D], BF16)
        GWB = sb("GWB", [128, KB * E], BF16)
        GB8 = sb("GB8", [8, 1], F32)
        GBB2 = sb("GBB2", [128, 16], F32)
        GBB = sb("GBB", [128, 8], F32)
        GB2 = sb("GB2", [8, 2], F32)
        IDB = sb("IDB", [128, 128], BF16)
        ONE1B = sb("ONE1B", [1, 128], BF16)
        ONE1F = sb("ONE1F", [1, 128], F32)
        ONEKB = sb("ONEKB", [128, 1], BF16)
        EPSB = sb("EPSB", [1, 1], F32)
        SEL = sb("SEL", [128, NCORES], F32)
        PV = sb("PV", [128, 1], F32)
        AGP = sb("AGP", [128, 2 * KB], F32)
        AGALL = sb("AGALL", [128, NCORES * 2 * KB], BF16)
        AIN = sb("AIN", [128, 2 * KB], F32)
        IDF = sb("IDF", [128, 16], F32)
        SCR = sb("SCR", [128, 9 * 8], F32)
        CRB = sb("CRB", [128, 9 * 8], BF16)
        EQ1S = sb("EQ1S", [128, 9 * 8], BF16)
        EQ2S = sb("EQ2S", [128, 9 * 8], BF16)
        MSK8 = sb("MSK8", [128, 8], F32)
        TMP8 = sb("TMP8", [128, 8], F32)
        M1T = sb("M1T", [128, 9], F32)
        M2T = sb("M2T", [128, 9], F32)
        G1T = sb("G1T", [128, 9], F32)
        G2T = sb("G2T", [128, 9], F32)
        COMB05 = sb("COMB05", [8, M1], BF16)
        CROW = sb("CROW", [1, 512], F32)
        COMBB = sb("COMBB", [128, M1], BF16)

        PSB = [pst_(f"psb{i}", [128, 512], F32) for i in range(7)]
        PST = pst_("pst", [128, 1024], BF16)
        PS7F = PST[:].bitcast(F32)

        def psum(i):
            return PSB[i][:] if i < 7 else PS7F

        esems = {e: ctx.enter_context(nc.semaphore("se_" + e))
                 for e in Sched.ENGINES}
        qsems = {e: ctx.enter_context(nc.semaphore("sq_" + e))
                 for e in ("gpsimd", "sync", "scalar")}
        ccsem = ctx.enter_context(nc.semaphore("scc"))
        block = ctx.enter_context(nc.Block())

        sch = Sched()

        def restage_b1():
            for blk in range(KB):
                for h in range(2):
                    sch.op("scalar",
                           (lambda b=blk, o=h * 512: nc.scalar.copy(
                               B1[:, b * M + o:b * M + o + 512],
                               B1[:, b * M + o:b * M + o + 512])),
                           reads=kB1c(blk * M + h * 512, blk * M + h * 512 + 512),
                           writes=kB1c(blk * M + h * 512, blk * M + h * 512 + 512))

        def flush_b1():
            for blk in range(KB):
                sch.op("vector",
                       (lambda b=blk: nc.vector.tensor_copy(
                           B1[:, b * M1 + M1 - 1:b * M1 + M1],
                           B1[:, b * M1 + M1 - 1:b * M1 + M1])),
                       reads=kB1c(blk * M1 + M1 - 1, blk * M1 + M1),
                       writes=kB1c(blk * M1 + M1 - 1, blk * M1 + M1))

        def kXF(blk, c0=0, w=M1):
            return strips("XF", (blk * M1 + c0) * 4, (blk * M1 + c0 + w) * 4, M1 * 4)

        def kXFbf(bc0, bc1):
            return strips("XF", bc0 * 2, bc1 * 2, M1 * 4)

        def kEK(blk, c0=0, w=M):
            return kXFbf(blk * M + c0, blk * M + c0 + w)

        def kZ(blk, c0=0, w=M):
            return kXFbf(KB * M + blk * M + c0, KB * M + blk * M + c0 + w)

        def kXN(blk, c0=0, w=M1):
            return strips("XN", (blk * M1 + c0) * 2, (blk * M1 + c0 + w) * 2, M1 * 2)

        def kB1c(bc0, bc1):
            return strips("B1", bc0 * 2, bc1 * 2, M1 * 2)

        def kB2c(bc0, bc1):
            return strips("B2", bc0 * 2, bc1 * 2, M1 * 2)

        def kW2X(bc0, bc1):
            return strips("W2X", bc0 * 2, bc1 * 2, 768)

        XFbf = XF[:].bitcast(BF16)

        def vXF(blk, c0, w):
            return XF[:, blk * M1 + c0: blk * M1 + c0 + w]

        def vEK(blk, c0, w):
            return XFbf[:, blk * M + c0: blk * M + c0 + w]

        def vZ(blk, c0, w):
            return XFbf[:, KB * M + blk * M + c0: KB * M + blk * M + c0 + w]

        def vXN(blk, c0, w):
            return XN[:, blk * M1 + c0: blk * M1 + c0 + w]

        sch.dma("sync", lambda: nc.sync.dma_start(
            XF[:].rearrange("p (b t) -> p b t", b=KB),
            d_x.rearrange("(b p) t -> p b t", p=128)),
            writes=[f"XF:{i}" for i in range(KB)])
        sch.dma("sync", lambda: nc.sync.dma_start(VEC[:], d_vecs), writes=["VEC"])
        sch.dma("sync", lambda: nc.sync.dma_start(B1R[:], d_b1r), writes=["B1R"])
        sch.dma("sync", lambda: nc.sync.dma_start(SEL[:], d_sel), writes=["SEL"])
        sch.dma("sync", lambda: nc.sync.dma_start(PV[:], d_pv), writes=["PV"])
        sch.dma("sync", lambda: nc.sync.dma_start(GB2[:], d_gb), writes=["GB2"])
        sch.dma("gpsimd", lambda: nc.gpsimd.dma_start(
            GBB2[:], d_gbr.broadcast_to([128, 2 * E])), writes=["GBB2"])
        sch.dma("gpsimd", lambda: nc.gpsimd.dma_start(B2B[:], d_b2), writes=["B2B"])
        sch.dma("gpsimd", lambda: nc.gpsimd.dma_start(IDB[:], d_id), writes=["IDB"])
        sch.dma("sync", lambda: nc.sync.dma_start(IDF[:], d_id[:, 0:16]), writes=["IDF"])
        sch.dma("gpsimd", lambda: nc.gpsimd.dma_start(
            GWB[:].rearrange("p (b e) -> p b e", b=KB),
            d_gwT.rearrange("(b p) e -> p b e", p=128)), writes=["GWB"])

        if "x0" in _TAPS:
            sch.dma("sync", lambda: nc.sync.dma_start(
                taps["x0"].rearrange("(b p) t -> p b t", p=128),
                XF[:].rearrange("p (b t) -> p b t", b=KB)),
                reads=[f"XF:{i}" for i in range(KB)])
        sch.op("vector", lambda: nc.vector.memset(ONE1B[:], 1.0), writes=["ONE1B"])
        sch.op("vector", lambda: nc.vector.memset(ONE1F[:], 1.0), writes=["ONE1F"])
        sch.op("vector", lambda: nc.vector.memset(ONEKB[:], 1.0), writes=["ONEKB"])
        sch.op("vector", lambda: nc.vector.memset(EPSB[:], EPS), writes=["EPSB"])
        sch.op("vector", lambda: nc.vector.tensor_add(GB8[:], GB2[:, 0:1], GB2[:, 1:2]),
               reads=["GB2"], writes=["GB8"])
        sch.op("vector", lambda: nc.vector.tensor_add(
            GBB[:, 0:8], GBB2[:, 0:8], GBB2[:, 8:16]),
            reads=["GBB2"], writes=["GBB"])
        sch.op("scalar", lambda: nc.scalar.activation(
            EU[:], VEC[:, 5 * KB:6 * KB], AF.Exp), reads=["VEC"], writes=["EU"])
        sch.op("scalar", lambda: nc.scalar.activation(
            DD[:], VEC[:, 6 * KB:7 * KB], AF.Exp), reads=["VEC"], writes=["DD"])
        sch.op("scalar", lambda: nc.scalar.activation(
            DD[:], DD[:], AF.Exp, scale=-1.0), reads=["DD"], writes=["DD"])

        def layer_norm(chunks, src_xn, apply_fn):
            for (cc, cw) in chunks:
                for blk in range(KB):
                    p = blk % 2
                    tile = TSQ0 if p == 0 else TSQ1
                    if not src_xn:
                        sch.op("scalar",
                               (lambda b=blk, t=tile, c=cc, w=cw:
                                nc.scalar.copy(t[:, 0:w], vXF(b, c, w))),
                               reads=kXF(blk, cc, cw), writes=[f"TSQ{p}"])
                    else:
                        sch.op("scalar",
                               (lambda b=blk, t=tile, c=cc, w=cw:
                                nc.scalar.copy(t[:, 0:w], vXN(b, c, w))),
                               reads=kXN(blk, cc, cw), writes=[f"TSQ{p}"])
                    mmk = [f"TSQ{p}"]
                    mm_in = (lambda t=tile, w=cw: t[:, 0:w])
                    tq = TSQ1 if p == 0 else TSQ0
                    tqk = f"TSQ{1 - p}"
                    sch.op("vector",
                           (lambda f=mm_in, t=tq, w=cw:
                            nc.vector.tensor_mul(t[:, 0:w], f(), f())),
                           reads=mmk, writes=[tqk])
                    sch.op("tensor",
                           (lambda f=mm_in, w=cw, s=(blk == 0),
                                   e_=(blk == KB - 1):
                            nc.tensor.matmul(PSB[5][0:1, 0:w], ONEKB[:], f(),
                                             start=s, stop=e_)),
                           reads=mmk + ["ONEKB"], writes=["PS5"],
                           inc=(blk == KB - 1))
                    sch.op("tensor",
                           (lambda t=tq, w=cw, s=(blk == 0), e_=(blk == KB - 1):
                            nc.tensor.matmul(PSB[6][0:1, 0:w], ONEKB[:],
                                             t[:, 0:w], start=s, stop=e_)),
                           reads=[tqk, "ONEKB"], writes=["PS6"],
                           inc=(blk == KB - 1))
                sch.op("scalar",
                       (lambda w=cw: nc.scalar.activation(
                           SMEAN[:, 0:w], PSB[5][0:1, 0:w], AF.Copy,
                           scale=1.0 / D)),
                       reads=["PS5"], writes=["SMEAN"])
                sch.op("scalar",
                       (lambda w=cw: nc.scalar.activation(
                           STMP[:, 0:w], PSB[6][0:1, 0:w], AF.Copy,
                           scale=1.0 / D)),
                       reads=["PS6"], writes=["STMP"])
                sch.op("vector",
                       (lambda w=cw: nc.vector.tensor_mul(
                           SRSTD[:, 0:w], SMEAN[:, 0:w], SMEAN[:, 0:w])),
                       reads=["SMEAN"], writes=["SRSTD"])
                sch.op("vector",
                       (lambda w=cw: nc.vector.tensor_sub(
                           SRSTD[:, 0:w], STMP[:, 0:w], SRSTD[:, 0:w])),
                       reads=["STMP", "SRSTD"], writes=["SRSTD"])
                sch.op("scalar",
                       (lambda w=cw: nc.scalar.activation(
                           SRSTD[:, 0:w], SRSTD[:, 0:w], AF.Ln,
                           bias=EPSB[0:1, 0:1])),
                       reads=["SRSTD", "EPSB"], writes=["SRSTD"])
                sch.op("scalar",
                       (lambda w=cw: nc.scalar.activation(
                           SRSTD[:, 0:w], SRSTD[:, 0:w], AF.Exp, scale=-0.5)),
                       reads=["SRSTD"], writes=["SRSTD"])
                sch.op("tensor",
                       (lambda w=cw: nc.tensor.matmul(
                           PSB[5][:, 0:w], ONE1F[:], SRSTD[:, 0:w],
                           start=True, stop=True)),
                       reads=["SRSTD", "ONE1F"], writes=["PS5"])
                sch.op("tensor",
                       (lambda w=cw: nc.tensor.matmul(
                           PSB[6][:, 0:w], ONE1F[:], SMEAN[:, 0:w],
                           start=True, stop=True)),
                       reads=["SMEAN", "ONE1F"], writes=["PS6"])
                for blk in range(KB):
                    apply_fn(blk, cc, cw)

        def apply_plain(blk, cc, cw):
            sch.op("vector",
                   (lambda b=blk, c=cc, w=cw: nc.vector.tensor_sub(
                       TH0[:, 0:w], vXF(b, c, w), PSB[6][:, 0:w])),
                   reads=kXF(blk, cc, cw) + ["PS6"], writes=["TH0"])
            sch.op("vector",
                   (lambda b=blk, c=cc, w=cw: nc.vector.tensor_mul(
                       vXN(b, c, w), TH0[:, 0:w], PSB[5][:, 0:w])),
                   reads=["TH0", "PS5"], writes=kXN(blk, cc, cw))

        def apply_plain_s(blk, cc, cw):
            sch.op("vector",
                   (lambda b=blk, c=cc, w=cw: nc.vector.tensor_sub(
                       TH0[:, 0:w], vXF(b, c, w), PSB[6][:, 0:w])),
                   reads=kXF(blk, cc, cw) + ["PS6"], writes=["TH0"])
            sch.op("vector",
                   (lambda b=blk, c=cc, w=cw: nc.vector.tensor_mul(
                       TH1[:, 0:w], TH0[:, 0:w], PSB[5][:, 0:w])),
                   reads=["TH0", "PS5"], writes=["TH1"])
            sch.op("vector",
                   (lambda b=blk, c=cc, w=cw: nc.vector.tensor_copy(
                       vXN(b, c, w), TH1[:, 0:w])),
                   reads=["TH1"], writes=kXN(blk, cc, cw))
            sch.op("vector",
                   (lambda b=blk, c=cc, w=cw: nc.vector.tensor_add(
                       vXF(b, c, w), vXF(b, c, w), TH1[:, 0:w])),
                   reads=["TH1"] + kXF(blk, cc, cw), writes=kXF(blk, cc, cw))

        def apply_internal(blk, cc, cw):
            sch.op("vector",
                   (lambda b=blk, c=cc, w=cw: nc.vector.tensor_sub(
                       TH0[:, 0:w], vXN(b, c, w), PSB[6][:, 0:w])),
                   reads=kXN(blk, cc, cw) + ["PS6"], writes=["TH0"])
            sch.op("vector",
                   (lambda b=blk, c=cc, w=cw: nc.vector.tensor_mul(
                       TH1[:, 0:w], TH0[:, 0:w], PSB[5][:, 0:w])),
                   reads=["TH0", "PS5"], writes=["TH1"])
            sch.op("vector",
                   (lambda b=blk, c=cc, w=cw: nc.vector.tensor_scalar(
                       vXN(b, c, w), TH1[:, 0:w],
                       VEC[:, 0 * KB + b:0 * KB + b + 1],
                       VEC[:, 1 * KB + b:1 * KB + b + 1],
                       ALU.mult, ALU.add)),
                   reads=["TH1", "VEC"], writes=kXN(blk, cc, cw))

        def gating(chunks):
            c0 = chunks[0][0]
            cN = chunks[-1][0] + chunks[-1][1]
            glist_all = group_list(c0, cN)
            for gi, gc, gw in glist_all:
                for blk in range(KB):
                    sch.op("tensor",
                           (lambda b=blk, g=gc, w=gw, s=(blk == 0),
                                   e_=(blk == KB - 1):
                            nc.tensor.matmul(PSB[6][0:w, 0:8],
                                             vXN(b, g, w),
                                             GWB[:, b * E:(b + 1) * E],
                                             start=s, stop=e_)),
                           reads=kXN(blk, gc, gw) + ["GWB"], writes=["PS6"],
                           inc=(blk == KB - 1))
                sch.op("vector",
                       (lambda i=gi, w=gw: nc.vector.tensor_add(
                           SCR[0:w, i * 8:(i + 1) * 8], PSB[6][0:w, 0:8],
                           GBB[0:w, 0:8])),
                       reads=["PS6", "GBB"], writes=["SCR"])
            if "sc0" in _TAPS and chunks[0][0] == 0:
                sch.dma("sync", lambda: nc.sync.dma_start(
                    taps["sc0"], SCR[:]), reads=["SCR"])
            for gi, gc, gw in glist_all:
                sch.op("vector",
                       (lambda i=gi, w=gw: nc.vector.reduce_max(
                           M1T[0:w, i:i + 1], SCR[0:w, i * 8:(i + 1) * 8],
                           axis=AXX)),
                       reads=["SCR"], writes=["M1T"])
                sch.op("vector",
                       (lambda i=gi, w=gw: nc.vector.tensor_tensor(
                           EQ1S[0:w, i * 8:(i + 1) * 8],
                           SCR[0:w, i * 8:(i + 1) * 8],
                           M1T[0:w, i:i + 1].broadcast_to([w, 8]),
                           ALU.is_equal)),
                       reads=["SCR", "M1T"], writes=["EQ1S"])
                sch.op("vector",
                       (lambda i=gi, w=gw: nc.vector.scalar_tensor_tensor(
                           MSK8[0:w, 0:8], EQ1S[0:w, i * 8:(i + 1) * 8], -1e30,
                           SCR[0:w, i * 8:(i + 1) * 8], ALU.mult, ALU.add)),
                       reads=["EQ1S", "SCR"], writes=["MSK8"])
                sch.op("vector",
                       (lambda i=gi, w=gw: nc.vector.reduce_max(
                           M2T[0:w, i:i + 1], MSK8[0:w, 0:8], axis=AXX)),
                       reads=["MSK8"], writes=["M2T"])
                sch.op("vector",
                       (lambda i=gi, w=gw: nc.vector.tensor_tensor(
                           EQ2S[0:w, i * 8:(i + 1) * 8], MSK8[0:w, 0:8],
                           M2T[0:w, i:i + 1].broadcast_to([w, 8]),
                           ALU.is_equal)),
                       reads=["MSK8", "M2T"], writes=["EQ2S"])
            sch.op("vector", lambda: nc.vector.tensor_sub(
                G1T[:, 0:9], M1T[:, 0:9], M2T[:, 0:9]),
                reads=["M1T", "M2T"], writes=["G1T"])
            sch.op("scalar", lambda: nc.scalar.activation(
                G1T[:, 0:9], G1T[:, 0:9], AF.Sigmoid),
                reads=["G1T"], writes=["G1T"])
            sch.op("vector", lambda: nc.vector.tensor_scalar(
                G2T[:, 0:9], G1T[:, 0:9], -1.0, 1.0, ALU.mult, ALU.add),
                reads=["G1T"], writes=["G2T"])
            for gi, gc, gw in glist_all:
                sch.op("vector",
                       (lambda i=gi, w=gw: nc.vector.tensor_scalar(
                           TMP8[0:w, 0:8], EQ2S[0:w, i * 8:(i + 1) * 8],
                           G2T[0:w, i:i + 1], None, ALU.mult)),
                       reads=["EQ2S", "G2T"], writes=["TMP8"])
                sch.op("vector",
                       (lambda i=gi, w=gw: nc.vector.scalar_tensor_tensor(
                           SCR[0:w, i * 8:(i + 1) * 8],
                           EQ1S[0:w, i * 8:(i + 1) * 8],
                           G1T[0:w, i:i + 1], TMP8[0:w, 0:8],
                           ALU.mult, ALU.add)),
                       reads=["EQ1S", "G1T", "TMP8"], writes=["SCR"])
                sch.op("vector",
                       (lambda i=gi, w=gw: nc.vector.tensor_copy(
                           CRB[0:w, i * 8:(i + 1) * 8],
                           SCR[0:w, i * 8:(i + 1) * 8])),
                       reads=["SCR"], writes=["CRB"])
            for gi, gc, gw in glist_all:
                sch.op("tensor",
                       (lambda i=gi, w=gw: nc.tensor.transpose(
                           PST[0:8, 0:w], CRB[0:w, i * 8:(i + 1) * 8],
                           IDB[0:w, 0:w])),
                       reads=["CRB", "IDB"], writes=["PS7"])
                sch.op("scalar",
                       (lambda g=gc, w=gw: nc.scalar.activation(
                           COMB05[0:8, g:g + w], PST[0:8, 0:w], AF.Copy,
                           scale=0.5)),
                       reads=["PS7"], writes=["COMB05"])

        def comb_bcast(e, chunks):
            c0 = chunks[0][0]
            cN = chunks[-1][0] + chunks[-1][1]
            groups = group_list(c0, cN)
            for rnd in range(3):
                base = c0 + rnd * 512
                glist = [(gi, gc, gw) for gi, gc, gw in groups
                         if base <= gc < base + 512]
                if not glist:
                    continue
                for j, (gi, gc, gw) in enumerate(glist):
                    sch.op("tensor",
                           (lambda i=gi, g=gc, w=gw, bb=base:
                            nc.tensor.matmul(
                                PSB[5][0:1, g - bb:g - bb + w],
                                CRB[0:w, i * 8 + e:i * 8 + e + 1],
                                IDB[0:w, 0:w], start=True, stop=True)),
                           reads=["CRB", "IDB"], writes=["PS5"],
                           inc=(j == len(glist) - 1))
                lo = base
                hi = min(base + 512, cN)
                sch.op("scalar",
                       (lambda l=lo, h=hi, bb=base: nc.scalar.activation(
                           CROW[0:1, 0:h - l],
                           PSB[5][0:1, 0:h - l],
                           AF.Copy, scale=0.5)),
                       reads=["PS5"], writes=["CROW"])
                sch.op("tensor",
                       (lambda l=lo, h=hi: nc.tensor.matmul(
                           PSB[6][:, 0:h - l], ONE1F[:],
                           CROW[0:1, 0:h - l],
                           start=True, stop=True)),
                       reads=["CROW", "ONE1F"], writes=["PS6"])
                sch.op("scalar",
                       (lambda l=lo, h=hi: nc.scalar.copy(
                           COMBB[:, l:h], PSB[6][:, 0:h - l])),
                       reads=["PS6"], writes=["COMBB"])

        def w2_lhsT(nb, db):
            if nb < 8:
                return B2[:, nb * D + db * 128:nb * D + db * 128 + 128]
            return W2X[:, (nb - 8) * D + db * 128:(nb - 8) * D + db * 128 + 128]

        def kw2(nb, db):
            if nb < 8:
                return kB2c(nb * D + db * 128, nb * D + db * 128 + 128)
            return kW2X((nb - 8) * D + db * 128, (nb - 8) * D + db * 128 + 128)

        def ffn(chunks, mmch, tag):
            layer_norm(chunks, False, apply_plain)
            gating(chunks)
            if tag == "f1" and "xn1" in _TAPS:
                sch.dma("sync", lambda: nc.sync.dma_start(
                    taps["xn1"], XN[:]),
                    reads=[f"XN:{i}" for i in range(KB)])
            if tag == "f1" and "comb" in _TAPS:
                sch.dma("sync", lambda: nc.sync.dma_start(
                    taps["comb"], COMB05[0:8, :]), reads=["COMB05"])
            if tag == "f1" and "sc1" in _TAPS:
                sch.dma("sync", lambda: nc.sync.dma_start(
                    taps["sc1"], SCR[:]), reads=["SCR"])
            def emit_w2_load(e):
                for j in range(4):
                    sch.dma("gpsimd",
                            (lambda ee=e, jj=j: nc.gpsimd.dma_start(
                                B2[:, jj * 2 * D:(jj + 1) * 2 * D]
                                .rearrange("p (n d) -> p n d", n=2),
                                d_w2T[ee][jj * 256:(jj + 1) * 256, :]
                                .rearrange("(n p) d -> p n d", p=128))),
                            writes=kB2c(j * 2 * D, (j + 1) * 2 * D))
                sch.dma("gpsimd",
                        (lambda ee=e: nc.gpsimd.dma_start(
                            W2X[:].rearrange("p (n d) -> p n d", n=3),
                            d_w2T[ee][8 * 128:11 * 128, :]
                            .rearrange("(n p) d -> p n d", p=128))),
                        writes=kW2X(0, 3 * D))

            for e in range(E):
                comb_bcast(e, chunks)
                def emit_gran(nb):
                    for kh in range(2):
                        p = (nb * 2 + kh) % 4
                        sch.dma("gpsimd",
                                (lambda ee=e, n=nb, k=kh, w_=WGS[p]:
                                 nc.gpsimd.dma_start(
                                     w_[:].rearrange("p (b n) -> p b n", b=8),
                                     d_w1T[ee, k * 1024:(k + 1) * 1024,
                                           n * 128:(n + 1) * 128]
                                     .rearrange("(b p) n -> p b n", p=128))),
                                writes=[f"WG{p}"])

                emit_gran(0)
                emit_gran(1)
                emit_w2_load(e)
                for nb in range(NB):
                    for ci, (cc, cw) in enumerate(mmch):
                        pb = ci % 2
                        for kh in range(2):
                            p = (nb * 2 + kh) % 4
                            wg = WGS[p]
                            wk = f"WG{p}"
                            for b8 in range(8):
                                blk = kh * 8 + b8
                                sch.op("tensor",
                                       (lambda b=blk, bb=b8, c=cc, w=cw, pp=pb,
                                               w_=wg, s=(kh == 0 and b8 == 0),
                                               e_=(kh == 1 and b8 == 7):
                                        nc.tensor.matmul(
                                            psum(pp)[:, 0:w],
                                            w_[:, bb * 128:(bb + 1) * 128],
                                            vXN(b, c, w), start=s, stop=e_)),
                                       reads=kXN(blk, cc, cw) + [wk],
                                       writes=[f"PS{pb}"],
                                       inc=(kh == 1 and b8 == 7))
                        th = TH0 if (ci + nb) % 2 == 0 else TH1
                        thk = "TH0" if (ci + nb) % 2 == 0 else "TH1"
                        sch.op("scalar",
                               (lambda ee=e, n=nb, w=cw, pp=pb, t=th:
                                nc.scalar.activation(
                                    t[:, 0:w], psum(pp)[:, 0:w], AF.Silu,
                                    bias=B1R[:, ee * NB + n:ee * NB + n + 1])),
                               reads=[f"PS{pb}", "B1R"], writes=[thk])
                        sch.op("vector",
                               (lambda n=nb, c=cc, w=cw, t=th:
                                nc.vector.tensor_mul(
                                    B1[:, n * M1 + c:n * M1 + c + w],
                                    t[:, 0:w], COMBB[:, c:c + w])),
                               reads=[thk, "COMBB"],
                               writes=kB1c(nb * M1 + cc, nb * M1 + cc + cw))
                    if nb + 2 < NB:
                        emit_gran(nb + 2)
                b2c_here = (e == 0)
                for ci, (cc, cw) in enumerate(mmch):
                    for db in range(KB):
                        pi = 2 + ((ci * KB + db) % 4)
                        for nb in range(NB):
                            last = (nb == NB - 1) and not b2c_here
                            sch.op("tensor",
                                   (lambda n=nb, d=db, c=cc, w=cw, p=pi,
                                           s=(nb == 0), e_=last:
                                    nc.tensor.matmul(
                                        psum(p)[:, 0:w], w2_lhsT(n, d),
                                        B1[:, n * M1 + c:n * M1 + c + w],
                                        start=s, stop=e_)),
                                   reads=kB1c(nb * M1 + cc, nb * M1 + cc + cw)
                                   + kw2(nb, db),
                                   writes=[f"PS{pi}"], inc=last)
                        if b2c_here:
                            sch.op("tensor",
                                   (lambda d=db, c=cc, w=cw, p=pi:
                                    nc.tensor.matmul(
                                        psum(p)[:, 0:w],
                                        B2B[0:8, d * 128:d * 128 + 128],
                                        COMB05[0:8, c:c + w],
                                        start=False, stop=True)),
                                   reads=["B2B", "COMB05"], writes=[f"PS{pi}"],
                                   inc=True)
                        sch.op("vector",
                               (lambda d=db, c=cc, w=cw, p=pi:
                                nc.vector.tensor_add(
                                    vXF(d, c, w), vXF(d, c, w),
                                    psum(p)[:, 0:w])),
                               reads=[f"PS{pi}"] + kXF(db, cc, cw),
                               writes=kXF(db, cc, cw))

        ffn(CH_FULL, CH_FULL, "f1")

        if "x1" in _TAPS:
            sch.dma("sync", lambda: nc.sync.dma_start(
                taps["x1"].rearrange("(b p) t -> p b t", p=128),
                XF[:].rearrange("p (b t) -> p b t", b=KB)),
                reads=[f"XF:{i}" for i in range(KB)])

        layer_norm(CH_FULL, False, apply_plain_s)
        if "xh" in _TAPS:
            sch.dma("sync", lambda: nc.sync.dma_start(taps["xh"], XN[:]),
                    reads=[f"XN:{i}" for i in range(KB)])
        sch.dma("sync", lambda: nc.sync.dma_start(d_xsp[:, :], XF[:]),
                reads=[f"XF:{i}" for i in range(KB)], writes=["dXSP"])
        layer_norm(CH_FULL, True, apply_internal)
        for blk in range(KB):
            sch.op("vector",
                   (lambda b=blk: nc.vector.tensor_mul(
                       XN[:, b * M1:b * M1 + 1], XN[:, b * M1:b * M1 + 1],
                       PV[:, 0:1])),
                   reads=kXN(blk, 0, 1) + ["PV"], writes=kXN(blk, 0, 1))
        for blk in range(KB):
            for h in range(2):
                sch.op("vector",
                       (lambda b=blk, o=h * 512: nc.vector.tensor_sub(
                           TSQ0[:, 0:512], vXN(b, 1 + o, 512), vXN(b, o, 512))),
                       reads=kXN(blk, h * 512, 513), writes=["TSQ0"])
                sch.op("vector",
                       (lambda b=blk, o=h * 512: nc.vector.scalar_tensor_tensor(
                           B1[:, b * M + o:b * M + o + 512], TSQ0[:, 0:512],
                           VEC[:, 2 * KB + b:2 * KB + b + 1], vXN(b, o, 512),
                           ALU.mult, ALU.add)),
                       reads=["TSQ0", "VEC"] + kXN(blk, h * 512, 512),
                       writes=kB1c(blk * M + h * 512, blk * M + h * 512 + 512))
                sch.op("vector",
                       (lambda b=blk, o=h * 512: nc.vector.scalar_tensor_tensor(
                           B2[:, b * M + o:b * M + o + 512], TSQ0[:, 0:512],
                           VEC[:, 3 * KB + b:3 * KB + b + 1], vXN(b, o, 512),
                           ALU.mult, ALU.add)),
                       reads=["TSQ0", "VEC"] + kXN(blk, h * 512, 512),
                       writes=kB2c(blk * M + h * 512, blk * M + h * 512 + 512))

        def proj_phase(d_w, gslot, out_emit, rhs_B1):
            off = gslot * D
            for db in range(KB):
                sch.dma("gpsimd",
                        (lambda d=db, o=off: nc.gpsimd.dma_start(
                            W2X[:, o:o + D].rearrange("p (b n) -> p b n", b=KB),
                            d_w[:, d * 128:(d + 1) * 128]
                            .rearrange("(b p) n -> p b n", p=128))),
                        writes=kW2X(off, off + D))
                for mc in range(2):
                    p = (gslot * 4) + (db % 2) * 2 + mc
                    srcb = B1 if rhs_B1 else B2
                    ksrc = kB1c if rhs_B1 else kB2c
                    for blk in range(KB):
                        sch.op("tensor",
                               (lambda b=blk, m=mc, pp=p, o=off, s_=srcb,
                                       s=(blk == 0), e_=(blk == KB - 1):
                                nc.tensor.matmul(
                                    psum(pp)[:, 0:512],
                                    W2X[:, o + b * 128:o + b * 128 + 128],
                                    s_[:, b * M + m * 512:b * M + m * 512 + 512],
                                    start=s, stop=e_)),
                               reads=ksrc(blk * M + mc * 512,
                                          blk * M + mc * 512 + 512)
                               + kW2X(off, off + D),
                               writes=[f"PS{p}"], inc=(blk == KB - 1))
                    out_emit(db, mc, p)

        def k_out(db, mc, p):
            sch.op("scalar",
                   (lambda d=db, m=mc, pp=p: nc.scalar.activation(
                       vEK(d, m * 512, 512), psum(pp)[:, 0:512], AF.Exp)),
                   reads=[f"PS{p}"], writes=kEK(db, mc * 512, 512))

        def v_out(db, mc, p):
            sch.op("vector",
                   (lambda d=db, m=mc, pp=p: nc.vector.tensor_mul(
                       vZ(d, m * 512, 512), psum(pp)[:, 0:512],
                       vEK(d, m * 512, 512))),
                   reads=[f"PS{p}"] + kEK(db, mc * 512, 512),
                   writes=kZ(db, mc * 512, 512))

        if "xn2" in _TAPS:
            sch.dma("sync", lambda: nc.sync.dma_start(taps["xn2"], XN[:]),
                    reads=[f"XN:{i}" for i in range(KB)])
        proj_phase(d_wk, 0, k_out, rhs_B1=True)
        for blk in range(KB):
            for h in range(2):
                sch.op("vector",
                       (lambda b=blk, o=h * 512: nc.vector.tensor_sub(
                           TSQ0[:, 0:512], vXN(b, 1 + o, 512), vXN(b, o, 512))),
                       reads=kXN(blk, h * 512, 513), writes=["TSQ0"])
                sch.op("vector",
                       (lambda b=blk, o=h * 512: nc.vector.scalar_tensor_tensor(
                           B1[:, b * M + o:b * M + o + 512], TSQ0[:, 0:512],
                           VEC[:, 4 * KB + b:4 * KB + b + 1], vXN(b, o, 512),
                           ALU.mult, ALU.add)),
                       reads=["TSQ0", "VEC"] + kXN(blk, h * 512, 512),
                       writes=kB1c(blk * M + h * 512, blk * M + h * 512 + 512))
        flush_b1()
        proj_phase(d_wv, 1, v_out, rhs_B1=False)


        def scr_a(par):
            return B2[:, par * 2050:par * 2050 + 1025]

        def scr_b(par):
            return B2[:, par * 2050 + 1025:par * 2050 + 2050]

        def kscr(par):
            return kB2c(par * 2050, par * 2050 + 2050)

        for db in range(KB):
            par = db % 2
            sch.op("vector",
                   (lambda d=db, p=par: nc.vector.tensor_tensor_scan(
                       scr_a(p)[:, 0:M],
                       DD[:, d:d + 1].broadcast_to([128, M]),
                       vZ(d, 0, M), 0.0, ALU.mult, ALU.add)),
                   reads=kZ(db, 0, M) + ["DD"], writes=kscr(par)[:1])
            sch.op("vector",
                   (lambda d=db, p=par: nc.vector.tensor_tensor_scan(
                       scr_b(p)[:, 0:M],
                       DD[:, d:d + 1].broadcast_to([128, M]),
                       vEK(d, 0, M), 0.0, ALU.mult, ALU.add)),
                   reads=kEK(db, 0, M) + ["DD"], writes=kscr(par)[1:])
            sch.op("vector",
                   (lambda d=db, p=par: nc.vector.tensor_copy(
                       AGP[:, d:d + 1], scr_a(p)[:, M - 1:M])),
                   reads=kscr(par)[:1], writes=["AGP"])
            sch.op("vector",
                   (lambda d=db, p=par: nc.vector.tensor_copy(
                       AGP[:, KB + d:KB + d + 1], scr_b(p)[:, M - 1:M])),
                   reads=kscr(par)[1:], writes=["AGP"])
        sch.dma("gpsimd", lambda: nc.gpsimd.dma_start(d_agi[:, :], AGP[:]),
                reads=["AGP"], writes=["dAGI"])
        sch.collective(lambda: nc.gpsimd.collective_compute(
            "AllGather", ALU.bypass,
            replica_groups=[list(range(NCORES))],
            ins=[d_agi.ap().opt()],
            outs=[d_ago.ap().opt()]),
            reads=["dAGI"], writes=["dAGO"])
        sch.dma("gpsimd", lambda: nc.gpsimd.dma_start(
            AGALL[:].rearrange("p (r c) -> p r c", r=NCORES),
            d_ago.ap().rearrange("(r p) c -> p r c", p=128)),
            reads=["dAGO"], writes=["AGALL"])
        sch.op("vector", lambda: nc.vector.memset(AIN[:], 0.0), writes=["AIN"])
        for r in range(NCORES):
            sch.op("vector",
                   (lambda rr=r: nc.vector.scalar_tensor_tensor(
                       AIN[:], AGALL[:, rr * 2 * KB:(rr + 1) * 2 * KB],
                       SEL[:, rr:rr + 1], AIN[:], ALU.mult, ALU.add)),
                   reads=["AGALL", "SEL", "AIN"], writes=["AIN"])


        def r_out(db, mc, p):
            sch.op("scalar",
                   (lambda d=db, m=mc, pp=p: nc.scalar.activation(
                       XN[:, d * M1 + m * 512:d * M1 + m * 512 + 512],
                       psum(pp)[:, 0:512], AF.Sigmoid)),
                   reads=[f"PS{p}"],
                   writes=kXN(db, mc * 512, 512))

        restage_b1()
        proj_phase(d_wr, 0, r_out, rhs_B1=True)

        for db in range(KB):
            par = db % 2
            sch.op("vector",
                   (lambda d=db, p=par: nc.vector.tensor_copy(
                       scr_a(p)[:, 0:1], AIN[:, d:d + 1])),
                   reads=["AIN"], writes=kscr(par)[:1])
            sch.op("vector",
                   (lambda d=db, p=par: nc.vector.tensor_copy(
                       scr_b(p)[:, 0:1], AIN[:, KB + d:KB + d + 1])),
                   reads=["AIN"], writes=kscr(par)[1:])
            sch.op("vector",
                   (lambda d=db, p=par: nc.vector.tensor_tensor_scan(
                       scr_a(p)[:, 1:1 + M],
                       DD[:, d:d + 1].broadcast_to([128, M]),
                       vZ(d, 0, M), AIN[:, d:d + 1], ALU.mult, ALU.add)),
                   reads=kZ(db, 0, M) + ["DD", "AIN"], writes=kscr(par)[:1])
            sch.op("vector",
                   (lambda d=db, p=par: nc.vector.tensor_tensor_scan(
                       scr_b(p)[:, 1:1 + M],
                       DD[:, d:d + 1].broadcast_to([128, M]),
                       vEK(d, 0, M), AIN[:, KB + d:KB + d + 1],
                       ALU.mult, ALU.add)),
                   reads=kEK(db, 0, M) + ["DD", "AIN"], writes=kscr(par)[1:])
            for h in range(2):
                o = h * 512
                sch.op("vector",
                       (lambda d=db, oo=o, p=par: nc.vector.scalar_tensor_tensor(
                           TH0[:, 0:512], vZ(d, oo, 512), EU[:, d:d + 1],
                           scr_a(p)[:, oo:oo + 512], ALU.mult, ALU.add)),
                       reads=kZ(db, o, 512) + ["EU"] + kscr(par)[:1],
                       writes=["TH0"])
                sch.op("vector",
                       (lambda d=db, oo=o, p=par: nc.vector.scalar_tensor_tensor(
                           TH1[:, 0:512], vEK(d, oo, 512), EU[:, d:d + 1],
                           scr_b(p)[:, oo:oo + 512], ALU.mult, ALU.add)),
                       reads=kEK(db, o, 512) + ["EU"] + kscr(par)[1:],
                       writes=["TH1"])
                sch.op("scalar",
                       (lambda: nc.scalar.activation(
                           PSB[0][:, 0:512], TH1[:, 0:512], AF.Ln)),
                       reads=["TH1"], writes=["PS0"])
                sch.op("scalar",
                       (lambda: nc.scalar.activation(
                           TH1[:, 0:512], PSB[0][:, 0:512], AF.Exp,
                           scale=-1.0)),
                       reads=["PS0"], writes=["TH1"])
                sch.op("vector",
                       (lambda d=db, oo=o: nc.vector.tensor_mul(
                           vZ(d, oo, 512), TH0[:, 0:512], TH1[:, 0:512])),
                       reads=["TH0", "TH1"], writes=kZ(db, o, 512))
                sch.op("vector",
                       (lambda d=db, oo=o: nc.vector.tensor_mul(
                           B1[:, d * M + oo:d * M + oo + 512],
                           XN[:, d * M1 + oo:d * M1 + oo + 512],
                           vZ(d, oo, 512))),
                       reads=kXN(db, o, 512) + kZ(db, o, 512),
                       writes=kB1c(db * M + o, db * M + o + 512))

        sch.dma("sync", lambda: nc.sync.dma_start(XF[:], d_xsp[:, :]),
                reads=["dXSP"], writes=[f"XF:{i}" for i in range(KB)])

        restage_b1()

        def o_out(db, mc, p):
            cc = 1 + mc * 512
            sch.op("vector",
                   (lambda d=db, c=cc, pp=p: nc.vector.tensor_add(
                       vXF(d, c, 512), vXF(d, c, 512), psum(pp)[:, 0:512])),
                   reads=[f"PS{p}"] + kXF(db, cc, 512),
                   writes=kXF(db, cc, 512))

        proj_phase(d_wo, 1, o_out, rhs_B1=True)

        if "x2" in _TAPS:
            sch.dma("sync", lambda: nc.sync.dma_start(
                taps["x2"].rearrange("(b p) t -> p b t", p=128),
                XF[:].rearrange("p (b t) -> p b t", b=KB)),
                reads=[f"XF:{i}" for i in range(KB)])

        ffn(CH_OWN, CH_OWN, "f2")

        sch.dma("sync", lambda: nc.sync.dma_start(
            d_y.rearrange("(b p) t -> p b t", p=128),
            XF[:].rearrange("p (b t) -> p b t", b=KB)[:, :, 1:M1]),
            reads=[f"XF:{i}" for i in range(KB)])

        sch.final_waits("sync")
        sch.final_waits("gpsimd")
        sch.replay(block, esems, qsems, ccsem, verify=verify)
        build_program.last_sched = sch

    return nc


def _stage_vec(v):
    return np.ascontiguousarray(v.reshape(KB, 128).T)


def kernel(**inputs):
    x = np.asarray(inputs["L0_attn_x"], np.float32)
    Wk = np.ascontiguousarray(np.asarray(inputs["L0_attn_Wk"], np.float32))
    Wv = np.ascontiguousarray(np.asarray(inputs["L0_attn_Wv"], np.float32))
    Wr = np.ascontiguousarray(np.asarray(inputs["L0_attn_Wr"], np.float32))
    Wo = np.ascontiguousarray(np.asarray(inputs["L0_attn_Wo"], np.float32))
    w1 = np.asarray(inputs["L0_ffn_w1"], np.float32)
    w2 = np.asarray(inputs["L0_ffn_w2"], np.float32)
    gate_w = np.asarray(inputs["L0_ffn_gate_w"], np.float32)
    gate_b = np.asarray(inputs["L0_ffn_gate_b"], np.float32)
    bias = np.asarray(inputs["L0_ffn_bias"], np.float32)
    b1 = np.asarray(inputs["L0_ffn_b1"], np.float32)
    b2 = np.asarray(inputs["L0_ffn_b2"], np.float32)

    w1T = np.ascontiguousarray(w1.transpose(0, 2, 1))
    w2T = np.ascontiguousarray(w2.transpose(0, 2, 1))
    gwT = np.ascontiguousarray(gate_w.T)
    gb2 = np.ascontiguousarray(np.stack([gate_b, bias], axis=1))
    gbr = np.ascontiguousarray(
        np.concatenate([gate_b, bias]).reshape(1, 2 * E))
    b1r = np.ascontiguousarray(
        b1.reshape(E, NB, 128).transpose(2, 0, 1).reshape(128, E * NB))
    vecs = np.concatenate([
        _stage_vec(np.asarray(inputs["L0_attn_ln_scale"], np.float32)),
        _stage_vec(np.asarray(inputs["L0_attn_ln_bias"], np.float32)),
        _stage_vec(np.asarray(inputs["L0_attn_mix_k"], np.float32)),
        _stage_vec(np.asarray(inputs["L0_attn_mix_v"], np.float32)),
        _stage_vec(np.asarray(inputs["L0_attn_mix_r"], np.float32)),
        _stage_vec(np.asarray(inputs["L0_attn_time_first"], np.float32)),
        _stage_vec(np.asarray(inputs["L0_attn_time_decay"], np.float32)),
    ], axis=1)
    idm = np.eye(128, dtype=np.float32)

    xt = np.ascontiguousarray(x[0].T)

    in_maps = []
    for r in range(NCORES):
        lo = r * M
        if r == 0:
            xs = np.concatenate(
                [np.zeros((D, 1), np.float32), xt[:, 0:M]], axis=1)
        else:
            xs = xt[:, lo - 1:lo + M]
        sel = np.zeros((128, NCORES), np.float32)
        if r > 0:
            sel[:, r - 1] = 1.0
        pv = np.full((128, 1), 0.0 if r == 0 else 1.0, np.float32)
        in_maps.append(dict(
            xT=np.ascontiguousarray(xs), w1T=w1T, w2T=w2T,
            Wk=Wk, Wv=Wv, Wr=Wr, Wo=Wo, gwT=gwT, gb2=gb2, gbr=gbr, vecs=vecs,
            b1r=b1r, b2=np.ascontiguousarray(b2), id128=idm, sel=sel, pv=pv,
        ))

    nc = build_program()
    res = run_bass_kernel_spmd(nc, in_maps, core_ids=list(range(NCORES)))

    out = np.empty((1, T, D), np.float32)
    for r in range(NCORES):
        out[0, r * M:(r + 1) * M, :] = res.results[r]["y"].T
    if _TAPS:
        kernel.taps = [{k: res.results[r]["tap_" + k] for k in _TAPS}
                       for r in range(NCORES)]
    return out


# version 15
# speedup vs baseline: 1.3240x; 1.0029x over previous
import sys
sys.path.insert(0, "/opt/trn_rl_repo")

import contextlib
import numpy as np

import concourse.bass as bass
import concourse.mybir as mybir
from concourse.bass_utils import run_bass_kernel_spmd

F32 = mybir.dt.float32
BF16 = mybir.dt.bfloat16
AF = mybir.ActivationFunctionType
ALU = mybir.AluOpType
AXX = mybir.AxisListType.X

NCORES = 8
D = 2048
KB = 16
T = 8192
M = 1024
M1 = 1025
E = 8
HFF = 1408
NB = 11
EPS = 1e-5

CH_FULL = [(0, 342), (342, 342), (684, 341)]
CH_OWN = [(1, 512), (513, 512)]
MM_FULL = [(0, 512), (512, 512), (1024, 1)]
def group_list(c0, cN):
    out = []
    gi = 0
    gc = c0
    while gc < cN:
        out.append((gi, gc, min(128, cN - gc)))
        gc += 128
        gi += 1
    return out

_TAPS = []


class Sched:
    ENGINES = ("tensor", "vector", "scalar", "gpsimd", "sync")

    def __init__(self):
        self.prog = {e: [] for e in self.ENGINES}
        self.ecount = {e: 0 for e in self.ENGINES}
        self.qcount = {e: 0 for e in self.ENGINES}
        self.ccount = 0
        self.bufs = {}
        self.waited = {}
        self.deferred = {e: [] for e in self.ENGINES}

    def _buf(self, k):
        if k not in self.bufs:
            self.bufs[k] = {"w": None, "r": []}
        return self.bufs[k]

    def _need(self, eng, tag):
        kind, src, val = tag
        if kind == "e" and src == eng:
            if eng == "tensor":
                return False
            return True
        return True

    def _collect_waits(self, eng, reads, writes):
        waits = {}
        for k in reads:
            b = self._buf(k)
            if b["w"] is not None and self._need(eng, b["w"]):
                kind, src, val = b["w"]
                waits[(kind, src)] = max(waits.get((kind, src), 0), val)
        for k in writes:
            b = self._buf(k)
            if b["w"] is not None and self._need(eng, b["w"]):
                kind, src, val = b["w"]
                waits[(kind, src)] = max(waits.get((kind, src), 0), val)
            for tag in b["r"]:
                if self._need(eng, tag):
                    kind, src, val = tag
                    waits[(kind, src)] = max(waits.get((kind, src), 0), val)
        out = []
        for key, val in waits.items():
            if val > self.waited.get((eng, key), 0):
                self.waited[(eng, key)] = val
                out.append((key, val))
        return out

    def _register(self, tag, reads, writes):
        for k in writes:
            b = self._buf(k)
            b["w"] = tag
            b["r"] = []
        for k in reads:
            self._buf(k)["r"].append(tag)

    def op(self, eng, fn, reads=(), writes=(), inc=True):
        reads, writes = list(reads), list(writes)
        for key, val in self._collect_waits(eng, reads, writes):
            self.prog[eng].append(("wait", key, val))
        if inc:
            self.ecount[eng] += 1
            tag = ("e", eng, self.ecount[eng])
            self.prog[eng].append(("op", fn, True, reads, writes))
            for r, w in self.deferred[eng]:
                self._register(tag, r, w)
            self.deferred[eng] = []
            self._register(tag, reads, writes)
        else:
            self.prog[eng].append(("op", fn, False, reads, writes))
            self.deferred[eng].append((reads, writes))

    def dma(self, qeng, fn, reads=(), writes=()):
        reads, writes = list(reads), list(writes)
        for key, val in self._collect_waits(qeng, reads, writes):
            self.prog[qeng].append(("wait", key, val))
        self.qcount[qeng] += 1
        val = 16 * self.qcount[qeng]
        tag = ("q", qeng, val)
        self.prog[qeng].append(("dma", fn, val, reads, writes))
        self._register(tag, reads, writes)

    def collective(self, fn, reads=(), writes=()):
        eng = "gpsimd"
        for key, val in self._collect_waits(eng, list(reads), list(writes)):
            self.prog[eng].append(("wait", key, val))
        self.ccount += 1
        tag = ("c", "cc", self.ccount)
        self.prog[eng].append(("cc", fn, self.ccount, list(reads), list(writes)))
        self._register(tag, list(reads), list(writes))

    def barrier(self):
        tags = [("e", e, self.ecount[e]) for e in self.ENGINES
                if self.ecount[e] > 0]
        tags += [("q", q, 16 * self.qcount[q]) for q in self.ENGINES
                 if self.qcount[q] > 0]
        if self.ccount:
            tags.append(("c", "cc", self.ccount))
        for eng in self.ENGINES:
            for kind, src_, val in tags:
                if kind == "e" and src_ == eng:
                    continue
                key = (kind, src_)
                if val > self.waited.get((eng, key), 0):
                    self.waited[(eng, key)] = val
                    self.prog[eng].append(("wait", key, val))

    def final_waits(self, eng):
        for q in self.ENGINES:
            if self.qcount[q]:
                key, val = ("q", q), 16 * self.qcount[q]
                if self.waited.get((eng, key), 0) < val:
                    self.waited[(eng, key)] = val
                    self.prog[eng].append(("wait", key, val))

    STRIPW = {"XF": 1025 * 4, "XN": 1025 * 2, "B1": 1025 * 2, "B2": 1025 * 2,
              "W2X": 768}
    PSMAP = {"psb0": "PS0", "psb1": "PS1", "psb2": "PS2", "psb3": "PS3",
             "psb4": "PS4", "psb5": "PS5", "psb6": "PS6", "pst": "PS7"}
    DRMAP = {"xspill": "dXSP", "agi": "dAGI", "ago": "dAGO"}
    IGNORE = {"xT", "w1T", "w2T", "Wk", "Wv", "Wr", "Wo", "gwT", "gb2", "gbr",
              "vecs", "b1r", "b2", "id128", "sel", "pv", "y"}

    def _keys_of(self, pap):
        if not hasattr(pap, "memref"):
            return []
        name = pap.memref
        if name in self.IGNORE or name.startswith("const-") or                 name.startswith("tap_"):
            return []
        import concourse.mybir as _mb
        sz = _mb.dt.size(pap.dtype)
        dims = list(pap.ap)
        span = 0
        for st, ct in dims[1:]:
            span += abs(int(st)) * (int(ct) - 1)
        b0 = int(pap.offset) * sz
        b1 = (int(pap.offset) + span + 1) * sz
        if name in self.PSMAP:
            return [self.PSMAP[name]]
        if name in self.DRMAP:
            return [self.DRMAP[name]]
        if name in self.STRIPW:
            w = self.STRIPW[name]
            return [f"{name}:{i}" for i in range(b0 // w, (b1 - 1) // w + 1)]
        return [name]

    def replay(self, block, esems, qsems, ccsem, verify=False):
        report = []

        def check(eng, idx, instr, reads, writes):
            i = instr.ins
            decl = set()
            for k in list(reads) + list(writes):
                decl.add(k)
            declw = set(writes)
            try:
                real_in = []
                for a in i.ins:
                    real_in += self._keys_of(a)
                real_out = []
                for a in i.outs:
                    real_out += self._keys_of(a)
            except Exception as ex:
                report.append((eng, idx, "EXC", str(ex)))
                return
            miss_r = [k for k in real_in if k not in decl]
            miss_w = [k for k in real_out if k not in declw]
            if miss_r or miss_w:
                report.append((eng, idx, i.opcode,
                               "miss_r=" + ",".join(sorted(set(miss_r))) +
                               " miss_w=" + ",".join(sorted(set(miss_w))) +
                               " decl_r=" + ",".join(sorted(set(reads))) +
                               " decl_w=" + ",".join(sorted(set(writes)))))

        def run_engine(eng, engobj):
            for idx, entry in enumerate(self.prog[eng]):
                if entry[0] == "wait":
                    (kind, src), val = entry[1], entry[2]
                    sem = esems[src] if kind == "e" else (
                        qsems[src] if kind == "q" else ccsem)
                    engobj.wait_ge(sem, val)
                elif entry[0] == "op":
                    instr = entry[1]()
                    if verify:
                        check(eng, idx, instr, entry[3], entry[4])
                    if entry[2]:
                        instr.then_inc(esems[eng], 1)
                elif entry[0] == "dma":
                    instr = entry[1]()
                    if verify:
                        check(eng, idx, instr, entry[3], entry[4])
                    instr.then_inc(qsems[eng], 16)
                else:
                    entry[1]().then_inc(ccsem, 1)
        self._verify_report = report

        @block.tensor
        def _(t):
            run_engine("tensor", t)

        @block.vector
        def _(v):
            run_engine("vector", v)

        @block.scalar
        def _(s):
            run_engine("scalar", s)

        @block.gpsimd
        def _(g):
            run_engine("gpsimd", g)

        @block.sync
        def _(s):
            run_engine("sync", s)


def strips(region, b0, b1, strip_bytes):
    lo = b0 // strip_bytes
    hi = (b1 - 1) // strip_bytes
    return [f"{region}:{i}" for i in range(lo, hi + 1)]


def build_program(verify=False):
    nc = bass.Bass("TRN2", target_bir_lowering=False)

    d_x = nc.dram_tensor("xT", [D, M1], F32, kind="ExternalInput").ap()
    d_w1T = nc.dram_tensor("w1T", [E, D, HFF], F32, kind="ExternalInput").ap()
    d_w2T = nc.dram_tensor("w2T", [E, HFF, D], F32, kind="ExternalInput").ap()
    d_wk = nc.dram_tensor("Wk", [D, D], F32, kind="ExternalInput").ap()
    d_wv = nc.dram_tensor("Wv", [D, D], F32, kind="ExternalInput").ap()
    d_wr = nc.dram_tensor("Wr", [D, D], F32, kind="ExternalInput").ap()
    d_wo = nc.dram_tensor("Wo", [D, D], F32, kind="ExternalInput").ap()
    d_gwT = nc.dram_tensor("gwT", [D, E], F32, kind="ExternalInput").ap()
    d_gb = nc.dram_tensor("gb2", [E, 2], F32, kind="ExternalInput").ap()
    d_gbr = nc.dram_tensor("gbr", [1, 2 * E], F32, kind="ExternalInput").ap()
    d_vecs = nc.dram_tensor("vecs", [128, 7 * KB], F32, kind="ExternalInput").ap()
    d_b1r = nc.dram_tensor("b1r", [128, E * NB], F32, kind="ExternalInput").ap()
    d_b2 = nc.dram_tensor("b2", [E, D], F32, kind="ExternalInput").ap()
    d_id = nc.dram_tensor("id128", [128, 128], F32, kind="ExternalInput").ap()
    d_sel = nc.dram_tensor("sel", [128, NCORES], F32, kind="ExternalInput").ap()
    d_pv = nc.dram_tensor("pv", [128, 1], F32, kind="ExternalInput").ap()

    d_y = nc.dram_tensor("y", [D, M], F32, kind="ExternalOutput").ap()
    taps = {}
    TAP_SHAPES = {"x0": ([D, M1], F32), "x1": ([D, M1], F32),
                  "x2": ([D, M1], F32),
                  "xn1": ([128, KB * M1], BF16), "comb": ([8, M1], BF16),
                  "hs0": ([128, NB * M1], BF16), "sc1": ([128, 9 * 8], F32),
                  "sc0": ([128, 9 * 8], F32),
                  "xn2": ([128, KB * M1], BF16),
                  "xh": ([128, KB * M1], BF16),
                  "ekz": ([128, 2 * KB * M], BF16),
                  "wkv": ([128, 2 * KB * M], BF16),
                  "scn": ([128, 2 * KB * M1], BF16)}
    for tname in _TAPS:
        shp, dt = TAP_SHAPES[tname]
        taps[tname] = nc.dram_tensor("tap_" + tname, shp, dt,
                                     kind="ExternalOutput").ap()

    d_xsp = nc.dram_tensor("xspill", [128, KB * M1], F32)
    d_agi = nc.dram_tensor("agi", [128, 2 * KB], F32)
    d_ago = nc.dram_tensor("ago", [NCORES * 128, 2 * KB], F32)

    with contextlib.ExitStack() as ctx:
        sb = lambda name, shape, dt: ctx.enter_context(nc.sbuf_tensor(name, shape, dt))
        pst_ = lambda name, shape, dt: ctx.enter_context(nc.psum_tensor(name, shape, dt))

        XF = sb("XF", [128, KB * M1], F32)
        XN = sb("XN", [128, KB * M1], BF16)
        B1 = sb("B1", [128, KB * M1], BF16)
        B2 = sb("B2", [128, KB * M1], BF16)
        W2X = sb("W2X", [128, 3 * D], BF16)
        WGS = [sb(f"WG{i}", [128, 8 * 128], BF16) for i in range(4)]
        SMEAN = sb("SMEAN", [1, 512], F32)
        SRSTD = sb("SRSTD", [1, 512], F32)
        STMP = sb("STMP", [1, 512], F32)
        TH0 = sb("TH0", [128, 512], F32)
        TH1 = sb("TH1", [128, 512], F32)
        TSQ0 = sb("TSQ0", [128, 512], BF16)
        TSQ1 = sb("TSQ1", [128, 512], BF16)
        VEC = sb("VEC", [128, 7 * KB], F32)
        DD = sb("DD", [128, KB], F32)
        EU = sb("EU", [128, KB], F32)
        B1R = sb("B1R", [128, E * NB], F32)
        B2B = sb("B2B", [8, D], BF16)
        GWB = sb("GWB", [128, KB * E], BF16)
        GB8 = sb("GB8", [8, 1], F32)
        GBB2 = sb("GBB2", [128, 16], F32)
        GBB = sb("GBB", [128, 8], F32)
        GB2 = sb("GB2", [8, 2], F32)
        IDB = sb("IDB", [128, 128], BF16)
        ONE1B = sb("ONE1B", [1, 128], BF16)
        ONE1F = sb("ONE1F", [1, 128], F32)
        ONEKB = sb("ONEKB", [128, 1], BF16)
        EPSB = sb("EPSB", [1, 1], F32)
        SEL = sb("SEL", [128, NCORES], F32)
        PV = sb("PV", [128, 1], F32)
        AGP = sb("AGP", [128, 2 * KB], F32)
        AGALL = sb("AGALL", [128, NCORES * 2 * KB], BF16)
        AIN = sb("AIN", [128, 2 * KB], F32)
        IDF = sb("IDF", [128, 16], F32)
        SCR = sb("SCR", [128, 9 * 8], F32)
        CRB = sb("CRB", [128, 9 * 8], BF16)
        EQ1S = sb("EQ1S", [128, 9 * 8], BF16)
        EQ2S = sb("EQ2S", [128, 9 * 8], BF16)
        MSK8 = sb("MSK8", [128, 8], F32)
        TMP8 = sb("TMP8", [128, 8], F32)
        M1T = sb("M1T", [128, 9], F32)
        M2T = sb("M2T", [128, 9], F32)
        G1T = sb("G1T", [128, 9], F32)
        G2T = sb("G2T", [128, 9], F32)
        COMB05 = sb("COMB05", [8, M1], BF16)
        CROW = sb("CROW", [1, 512], F32)
        COMBB = sb("COMBB", [128, M1], BF16)

        PSB = [pst_(f"psb{i}", [128, 512], F32) for i in range(7)]
        PST = pst_("pst", [128, 1024], BF16)
        PS7F = PST[:].bitcast(F32)

        def psum(i):
            return PSB[i][:] if i < 7 else PS7F

        esems = {e: ctx.enter_context(nc.semaphore("se_" + e))
                 for e in Sched.ENGINES}
        qsems = {e: ctx.enter_context(nc.semaphore("sq_" + e))
                 for e in ("gpsimd", "sync", "scalar")}
        ccsem = ctx.enter_context(nc.semaphore("scc"))
        block = ctx.enter_context(nc.Block())

        sch = Sched()

        def restage_b1():
            for blk in range(KB):
                for h in range(2):
                    sch.op("scalar",
                           (lambda b=blk, o=h * 512: nc.scalar.copy(
                               B1[:, b * M + o:b * M + o + 512],
                               B1[:, b * M + o:b * M + o + 512])),
                           reads=kB1c(blk * M + h * 512, blk * M + h * 512 + 512),
                           writes=kB1c(blk * M + h * 512, blk * M + h * 512 + 512))

        def flush_xf():
            for blk in range(KB):
                sch.op("vector",
                       (lambda b=blk: nc.vector.tensor_copy(
                           XF[:, b * M1:b * M1 + 1], XF[:, b * M1:b * M1 + 1])),
                       reads=kXF(blk, 0, 1), writes=kXF(blk, 0, 1))

        def flush_b1():
            for blk in range(KB):
                sch.op("vector",
                       (lambda b=blk: nc.vector.tensor_copy(
                           B1[:, b * M1 + M1 - 1:b * M1 + M1],
                           B1[:, b * M1 + M1 - 1:b * M1 + M1])),
                       reads=kB1c(blk * M1 + M1 - 1, blk * M1 + M1),
                       writes=kB1c(blk * M1 + M1 - 1, blk * M1 + M1))

        def kXF(blk, c0=0, w=M1):
            return strips("XF", (blk * M1 + c0) * 4, (blk * M1 + c0 + w) * 4, M1 * 4)

        def kXFbf(bc0, bc1):
            return strips("XF", bc0 * 2, bc1 * 2, M1 * 4)

        def kEK(blk, c0=0, w=M):
            return kXFbf(blk * M + c0, blk * M + c0 + w)

        def kZ(blk, c0=0, w=M):
            return kXFbf(KB * M + blk * M + c0, KB * M + blk * M + c0 + w)

        def kXN(blk, c0=0, w=M1):
            return strips("XN", (blk * M1 + c0) * 2, (blk * M1 + c0 + w) * 2, M1 * 2)

        def kB1c(bc0, bc1):
            return strips("B1", bc0 * 2, bc1 * 2, M1 * 2)

        def kB2c(bc0, bc1):
            return strips("B2", bc0 * 2, bc1 * 2, M1 * 2)

        def kW2X(bc0, bc1):
            return strips("W2X", bc0 * 2, bc1 * 2, 768)

        XFbf = XF[:].bitcast(BF16)

        def vXF(blk, c0, w):
            return XF[:, blk * M1 + c0: blk * M1 + c0 + w]

        def vEK(blk, c0, w):
            return XFbf[:, blk * M + c0: blk * M + c0 + w]

        def vZ(blk, c0, w):
            return XFbf[:, KB * M + blk * M + c0: KB * M + blk * M + c0 + w]

        def vXN(blk, c0, w):
            return XN[:, blk * M1 + c0: blk * M1 + c0 + w]

        sch.dma("sync", lambda: nc.sync.dma_start(
            XF[:].rearrange("p (b t) -> p b t", b=KB),
            d_x.rearrange("(b p) t -> p b t", p=128)),
            writes=[f"XF:{i}" for i in range(KB)])
        sch.dma("sync", lambda: nc.sync.dma_start(VEC[:], d_vecs), writes=["VEC"])
        sch.dma("sync", lambda: nc.sync.dma_start(B1R[:], d_b1r), writes=["B1R"])
        sch.dma("sync", lambda: nc.sync.dma_start(SEL[:], d_sel), writes=["SEL"])
        sch.dma("sync", lambda: nc.sync.dma_start(PV[:], d_pv), writes=["PV"])
        sch.dma("sync", lambda: nc.sync.dma_start(GB2[:], d_gb), writes=["GB2"])
        sch.dma("gpsimd", lambda: nc.gpsimd.dma_start(
            GBB2[:], d_gbr.broadcast_to([128, 2 * E])), writes=["GBB2"])
        sch.dma("gpsimd", lambda: nc.gpsimd.dma_start(B2B[:], d_b2), writes=["B2B"])
        sch.dma("gpsimd", lambda: nc.gpsimd.dma_start(IDB[:], d_id), writes=["IDB"])
        sch.dma("sync", lambda: nc.sync.dma_start(IDF[:], d_id[:, 0:16]), writes=["IDF"])
        sch.dma("gpsimd", lambda: nc.gpsimd.dma_start(
            GWB[:].rearrange("p (b e) -> p b e", b=KB),
            d_gwT.rearrange("(b p) e -> p b e", p=128)), writes=["GWB"])

        if "x0" in _TAPS:
            sch.dma("sync", lambda: nc.sync.dma_start(
                taps["x0"].rearrange("(b p) t -> p b t", p=128),
                XF[:].rearrange("p (b t) -> p b t", b=KB)),
                reads=[f"XF:{i}" for i in range(KB)])
        sch.op("vector", lambda: nc.vector.memset(ONE1B[:], 1.0), writes=["ONE1B"])
        sch.op("vector", lambda: nc.vector.memset(ONE1F[:], 1.0), writes=["ONE1F"])
        sch.op("vector", lambda: nc.vector.memset(ONEKB[:], 1.0), writes=["ONEKB"])
        sch.op("vector", lambda: nc.vector.memset(EPSB[:], EPS), writes=["EPSB"])
        sch.op("vector", lambda: nc.vector.tensor_add(GB8[:], GB2[:, 0:1], GB2[:, 1:2]),
               reads=["GB2"], writes=["GB8"])
        sch.op("vector", lambda: nc.vector.tensor_add(
            GBB[:, 0:8], GBB2[:, 0:8], GBB2[:, 8:16]),
            reads=["GBB2"], writes=["GBB"])
        sch.op("scalar", lambda: nc.scalar.activation(
            EU[:], VEC[:, 5 * KB:6 * KB], AF.Exp), reads=["VEC"], writes=["EU"])
        sch.op("scalar", lambda: nc.scalar.activation(
            DD[:], VEC[:, 6 * KB:7 * KB], AF.Exp), reads=["VEC"], writes=["DD"])
        sch.op("scalar", lambda: nc.scalar.activation(
            DD[:], DD[:], AF.Exp, scale=-1.0), reads=["DD"], writes=["DD"])

        def layer_norm(chunks, src_xn, apply_fn):
            for (cc, cw) in chunks:
                for blk in range(KB):
                    p = blk % 2
                    tile = TSQ0 if p == 0 else TSQ1
                    if not src_xn:
                        sch.op("scalar",
                               (lambda b=blk, t=tile, c=cc, w=cw:
                                nc.scalar.copy(t[:, 0:w], vXF(b, c, w))),
                               reads=kXF(blk, cc, cw), writes=[f"TSQ{p}"])
                    else:
                        sch.op("scalar",
                               (lambda b=blk, t=tile, c=cc, w=cw:
                                nc.scalar.copy(t[:, 0:w], vXN(b, c, w))),
                               reads=kXN(blk, cc, cw), writes=[f"TSQ{p}"])
                    mmk = [f"TSQ{p}"]
                    mm_in = (lambda t=tile, w=cw: t[:, 0:w])
                    tq = TSQ1 if p == 0 else TSQ0
                    tqk = f"TSQ{1 - p}"
                    sch.op("vector",
                           (lambda f=mm_in, t=tq, w=cw:
                            nc.vector.tensor_mul(t[:, 0:w], f(), f())),
                           reads=mmk, writes=[tqk])
                    sch.op("tensor",
                           (lambda f=mm_in, w=cw, s=(blk == 0),
                                   e_=(blk == KB - 1):
                            nc.tensor.matmul(PSB[5][0:1, 0:w], ONEKB[:], f(),
                                             start=s, stop=e_)),
                           reads=mmk + ["ONEKB"], writes=["PS5"],
                           inc=(blk == KB - 1))
                    sch.op("tensor",
                           (lambda t=tq, w=cw, s=(blk == 0), e_=(blk == KB - 1):
                            nc.tensor.matmul(PSB[6][0:1, 0:w], ONEKB[:],
                                             t[:, 0:w], start=s, stop=e_)),
                           reads=[tqk, "ONEKB"], writes=["PS6"],
                           inc=(blk == KB - 1))
                sch.op("scalar",
                       (lambda w=cw: nc.scalar.activation(
                           SMEAN[:, 0:w], PSB[5][0:1, 0:w], AF.Copy,
                           scale=1.0 / D)),
                       reads=["PS5"], writes=["SMEAN"])
                sch.op("scalar",
                       (lambda w=cw: nc.scalar.activation(
                           STMP[:, 0:w], PSB[6][0:1, 0:w], AF.Copy,
                           scale=1.0 / D)),
                       reads=["PS6"], writes=["STMP"])
                sch.op("vector",
                       (lambda w=cw: nc.vector.tensor_mul(
                           SRSTD[:, 0:w], SMEAN[:, 0:w], SMEAN[:, 0:w])),
                       reads=["SMEAN"], writes=["SRSTD"])
                sch.op("vector",
                       (lambda w=cw: nc.vector.tensor_sub(
                           SRSTD[:, 0:w], STMP[:, 0:w], SRSTD[:, 0:w])),
                       reads=["STMP", "SRSTD"], writes=["SRSTD"])
                sch.op("scalar",
                       (lambda w=cw: nc.scalar.activation(
                           SRSTD[:, 0:w], SRSTD[:, 0:w], AF.Ln,
                           bias=EPSB[0:1, 0:1])),
                       reads=["SRSTD", "EPSB"], writes=["SRSTD"])
                sch.op("scalar",
                       (lambda w=cw: nc.scalar.activation(
                           SRSTD[:, 0:w], SRSTD[:, 0:w], AF.Exp, scale=-0.5)),
                       reads=["SRSTD"], writes=["SRSTD"])
                sch.op("tensor",
                       (lambda w=cw: nc.tensor.matmul(
                           PSB[5][:, 0:w], ONE1F[:], SRSTD[:, 0:w],
                           start=True, stop=True)),
                       reads=["SRSTD", "ONE1F"], writes=["PS5"])
                sch.op("tensor",
                       (lambda w=cw: nc.tensor.matmul(
                           PSB[6][:, 0:w], ONE1F[:], SMEAN[:, 0:w],
                           start=True, stop=True)),
                       reads=["SMEAN", "ONE1F"], writes=["PS6"])
                for blk in range(KB):
                    apply_fn(blk, cc, cw)

        def apply_plain(blk, cc, cw):
            sch.op("vector",
                   (lambda b=blk, c=cc, w=cw: nc.vector.tensor_sub(
                       TH0[:, 0:w], vXF(b, c, w), PSB[6][:, 0:w])),
                   reads=kXF(blk, cc, cw) + ["PS6"], writes=["TH0"])
            sch.op("vector",
                   (lambda b=blk, c=cc, w=cw: nc.vector.tensor_mul(
                       vXN(b, c, w), TH0[:, 0:w], PSB[5][:, 0:w])),
                   reads=["TH0", "PS5"], writes=kXN(blk, cc, cw))

        def apply_plain_s(blk, cc, cw):
            sch.op("vector",
                   (lambda b=blk, c=cc, w=cw: nc.vector.tensor_sub(
                       TH0[:, 0:w], vXF(b, c, w), PSB[6][:, 0:w])),
                   reads=kXF(blk, cc, cw) + ["PS6"], writes=["TH0"])
            sch.op("vector",
                   (lambda b=blk, c=cc, w=cw: nc.vector.tensor_mul(
                       TH1[:, 0:w], TH0[:, 0:w], PSB[5][:, 0:w])),
                   reads=["TH0", "PS5"], writes=["TH1"])
            sch.op("vector",
                   (lambda b=blk, c=cc, w=cw: nc.vector.tensor_copy(
                       vXN(b, c, w), TH1[:, 0:w])),
                   reads=["TH1"], writes=kXN(blk, cc, cw))
            sch.op("vector",
                   (lambda b=blk, c=cc, w=cw: nc.vector.tensor_add(
                       vXF(b, c, w), vXF(b, c, w), TH1[:, 0:w])),
                   reads=["TH1"] + kXF(blk, cc, cw), writes=kXF(blk, cc, cw))

        def apply_internal(blk, cc, cw):
            sch.op("vector",
                   (lambda b=blk, c=cc, w=cw: nc.vector.tensor_sub(
                       TH0[:, 0:w], vXN(b, c, w), PSB[6][:, 0:w])),
                   reads=kXN(blk, cc, cw) + ["PS6"], writes=["TH0"])
            sch.op("vector",
                   (lambda b=blk, c=cc, w=cw: nc.vector.tensor_mul(
                       TH1[:, 0:w], TH0[:, 0:w], PSB[5][:, 0:w])),
                   reads=["TH0", "PS5"], writes=["TH1"])
            sch.op("vector",
                   (lambda b=blk, c=cc, w=cw: nc.vector.tensor_scalar(
                       vXN(b, c, w), TH1[:, 0:w],
                       VEC[:, 0 * KB + b:0 * KB + b + 1],
                       VEC[:, 1 * KB + b:1 * KB + b + 1],
                       ALU.mult, ALU.add)),
                   reads=["TH1", "VEC"], writes=kXN(blk, cc, cw))

        def gating(chunks):
            c0 = chunks[0][0]
            cN = chunks[-1][0] + chunks[-1][1]
            glist_all = group_list(c0, cN)
            for gi, gc, gw in glist_all:
                for blk in range(KB):
                    sch.op("tensor",
                           (lambda b=blk, g=gc, w=gw, s=(blk == 0),
                                   e_=(blk == KB - 1):
                            nc.tensor.matmul(PSB[6][0:w, 0:8],
                                             vXN(b, g, w),
                                             GWB[:, b * E:(b + 1) * E],
                                             start=s, stop=e_)),
                           reads=kXN(blk, gc, gw) + ["GWB"], writes=["PS6"],
                           inc=(blk == KB - 1))
                sch.op("vector",
                       (lambda i=gi, w=gw: nc.vector.tensor_add(
                           SCR[0:w, i * 8:(i + 1) * 8], PSB[6][0:w, 0:8],
                           GBB[0:w, 0:8])),
                       reads=["PS6", "GBB"], writes=["SCR"])
            if "sc0" in _TAPS and chunks[0][0] == 0:
                sch.dma("sync", lambda: nc.sync.dma_start(
                    taps["sc0"], SCR[:]), reads=["SCR"])
            for gi, gc, gw in glist_all:
                sch.op("vector",
                       (lambda i=gi, w=gw: nc.vector.reduce_max(
                           M1T[0:w, i:i + 1], SCR[0:w, i * 8:(i + 1) * 8],
                           axis=AXX)),
                       reads=["SCR"], writes=["M1T"])
                sch.op("vector",
                       (lambda i=gi, w=gw: nc.vector.tensor_tensor(
                           EQ1S[0:w, i * 8:(i + 1) * 8],
                           SCR[0:w, i * 8:(i + 1) * 8],
                           M1T[0:w, i:i + 1].broadcast_to([w, 8]),
                           ALU.is_equal)),
                       reads=["SCR", "M1T"], writes=["EQ1S"])
                sch.op("vector",
                       (lambda i=gi, w=gw: nc.vector.scalar_tensor_tensor(
                           MSK8[0:w, 0:8], EQ1S[0:w, i * 8:(i + 1) * 8], -1e30,
                           SCR[0:w, i * 8:(i + 1) * 8], ALU.mult, ALU.add)),
                       reads=["EQ1S", "SCR"], writes=["MSK8"])
                sch.op("vector",
                       (lambda i=gi, w=gw: nc.vector.reduce_max(
                           M2T[0:w, i:i + 1], MSK8[0:w, 0:8], axis=AXX)),
                       reads=["MSK8"], writes=["M2T"])
                sch.op("vector",
                       (lambda i=gi, w=gw: nc.vector.tensor_tensor(
                           EQ2S[0:w, i * 8:(i + 1) * 8], MSK8[0:w, 0:8],
                           M2T[0:w, i:i + 1].broadcast_to([w, 8]),
                           ALU.is_equal)),
                       reads=["MSK8", "M2T"], writes=["EQ2S"])
            sch.op("vector", lambda: nc.vector.tensor_sub(
                G1T[:, 0:9], M1T[:, 0:9], M2T[:, 0:9]),
                reads=["M1T", "M2T"], writes=["G1T"])
            sch.op("scalar", lambda: nc.scalar.activation(
                G1T[:, 0:9], G1T[:, 0:9], AF.Sigmoid),
                reads=["G1T"], writes=["G1T"])
            sch.op("vector", lambda: nc.vector.tensor_scalar(
                G2T[:, 0:9], G1T[:, 0:9], -1.0, 1.0, ALU.mult, ALU.add),
                reads=["G1T"], writes=["G2T"])
            for gi, gc, gw in glist_all:
                sch.op("vector",
                       (lambda i=gi, w=gw: nc.vector.tensor_scalar(
                           TMP8[0:w, 0:8], EQ2S[0:w, i * 8:(i + 1) * 8],
                           G2T[0:w, i:i + 1], None, ALU.mult)),
                       reads=["EQ2S", "G2T"], writes=["TMP8"])
                sch.op("vector",
                       (lambda i=gi, w=gw: nc.vector.scalar_tensor_tensor(
                           SCR[0:w, i * 8:(i + 1) * 8],
                           EQ1S[0:w, i * 8:(i + 1) * 8],
                           G1T[0:w, i:i + 1], TMP8[0:w, 0:8],
                           ALU.mult, ALU.add)),
                       reads=["EQ1S", "G1T", "TMP8"], writes=["SCR"])
                sch.op("vector",
                       (lambda i=gi, w=gw: nc.vector.tensor_copy(
                           CRB[0:w, i * 8:(i + 1) * 8],
                           SCR[0:w, i * 8:(i + 1) * 8])),
                       reads=["SCR"], writes=["CRB"])
            for gi, gc, gw in glist_all:
                sch.op("tensor",
                       (lambda i=gi, w=gw: nc.tensor.transpose(
                           PST[0:8, 0:w], CRB[0:w, i * 8:(i + 1) * 8],
                           IDB[0:w, 0:w])),
                       reads=["CRB", "IDB"], writes=["PS7"])
                sch.op("scalar",
                       (lambda g=gc, w=gw: nc.scalar.activation(
                           COMB05[0:8, g:g + w], PST[0:8, 0:w], AF.Copy,
                           scale=0.5)),
                       reads=["PS7"], writes=["COMB05"])

        def comb_bcast(e, chunks):
            c0 = chunks[0][0]
            cN = chunks[-1][0] + chunks[-1][1]
            groups = group_list(c0, cN)
            for rnd in range(3):
                base = c0 + rnd * 512
                glist = [(gi, gc, gw) for gi, gc, gw in groups
                         if base <= gc < base + 512]
                if not glist:
                    continue
                for j, (gi, gc, gw) in enumerate(glist):
                    sch.op("tensor",
                           (lambda i=gi, g=gc, w=gw, bb=base:
                            nc.tensor.matmul(
                                PSB[5][0:1, g - bb:g - bb + w],
                                CRB[0:w, i * 8 + e:i * 8 + e + 1],
                                IDB[0:w, 0:w], start=True, stop=True)),
                           reads=["CRB", "IDB"], writes=["PS5"],
                           inc=(j == len(glist) - 1))
                lo = base
                hi = min(base + 512, cN)
                sch.op("scalar",
                       (lambda l=lo, h=hi, bb=base: nc.scalar.activation(
                           CROW[0:1, 0:h - l],
                           PSB[5][0:1, 0:h - l],
                           AF.Copy, scale=0.5)),
                       reads=["PS5"], writes=["CROW"])
                sch.op("tensor",
                       (lambda l=lo, h=hi: nc.tensor.matmul(
                           PSB[6][:, 0:h - l], ONE1F[:],
                           CROW[0:1, 0:h - l],
                           start=True, stop=True)),
                       reads=["CROW", "ONE1F"], writes=["PS6"])
                sch.op("scalar",
                       (lambda l=lo, h=hi: nc.scalar.copy(
                           COMBB[:, l:h], PSB[6][:, 0:h - l])),
                       reads=["PS6"], writes=["COMBB"])

        def w2_lhsT(nb, db):
            if nb < 8:
                return B2[:, nb * D + db * 128:nb * D + db * 128 + 128]
            return W2X[:, (nb - 8) * D + db * 128:(nb - 8) * D + db * 128 + 128]

        def kw2(nb, db):
            if nb < 8:
                return kB2c(nb * D + db * 128, nb * D + db * 128 + 128)
            return kW2X((nb - 8) * D + db * 128, (nb - 8) * D + db * 128 + 128)

        def ffn(chunks, mmch, tag):
            layer_norm(chunks, False, apply_plain)
            gating(chunks)
            if tag == "f1" and "xn1" in _TAPS:
                sch.dma("sync", lambda: nc.sync.dma_start(
                    taps["xn1"], XN[:]),
                    reads=[f"XN:{i}" for i in range(KB)])
            if tag == "f1" and "comb" in _TAPS:
                sch.dma("sync", lambda: nc.sync.dma_start(
                    taps["comb"], COMB05[0:8, :]), reads=["COMB05"])
            if tag == "f1" and "sc1" in _TAPS:
                sch.dma("sync", lambda: nc.sync.dma_start(
                    taps["sc1"], SCR[:]), reads=["SCR"])
            def emit_w2_load(e):
                for j in range(4):
                    sch.dma("gpsimd",
                            (lambda ee=e, jj=j: nc.gpsimd.dma_start(
                                B2[:, jj * 2 * D:(jj + 1) * 2 * D]
                                .rearrange("p (n d) -> p n d", n=2),
                                d_w2T[ee][jj * 256:(jj + 1) * 256, :]
                                .rearrange("(n p) d -> p n d", p=128))),
                            writes=kB2c(j * 2 * D, (j + 1) * 2 * D))
                sch.dma("gpsimd",
                        (lambda ee=e: nc.gpsimd.dma_start(
                            W2X[:].rearrange("p (n d) -> p n d", n=3),
                            d_w2T[ee][8 * 128:11 * 128, :]
                            .rearrange("(n p) d -> p n d", p=128))),
                        writes=kW2X(0, 3 * D))

            for e in range(E):
                comb_bcast(e, chunks)
                def emit_gran(nb):
                    for kh in range(2):
                        p = (nb * 2 + kh) % 4
                        sch.dma("gpsimd",
                                (lambda ee=e, n=nb, k=kh, w_=WGS[p]:
                                 nc.gpsimd.dma_start(
                                     w_[:].rearrange("p (b n) -> p b n", b=8),
                                     d_w1T[ee, k * 1024:(k + 1) * 1024,
                                           n * 128:(n + 1) * 128]
                                     .rearrange("(b p) n -> p b n", p=128))),
                                writes=[f"WG{p}"])

                emit_gran(0)
                emit_gran(1)
                emit_w2_load(e)
                for nb in range(NB):
                    for ci, (cc, cw) in enumerate(mmch):
                        pb = ci % 2
                        for kh in range(2):
                            p = (nb * 2 + kh) % 4
                            wg = WGS[p]
                            wk = f"WG{p}"
                            for b8 in range(8):
                                blk = kh * 8 + b8
                                sch.op("tensor",
                                       (lambda b=blk, bb=b8, c=cc, w=cw, pp=pb,
                                               w_=wg, s=(kh == 0 and b8 == 0),
                                               e_=(kh == 1 and b8 == 7):
                                        nc.tensor.matmul(
                                            psum(pp)[:, 0:w],
                                            w_[:, bb * 128:(bb + 1) * 128],
                                            vXN(b, c, w), start=s, stop=e_)),
                                       reads=kXN(blk, cc, cw) + [wk],
                                       writes=[f"PS{pb}"],
                                       inc=(kh == 1 and b8 == 7))
                        th = TH0 if (ci + nb) % 2 == 0 else TH1
                        thk = "TH0" if (ci + nb) % 2 == 0 else "TH1"
                        sch.op("scalar",
                               (lambda ee=e, n=nb, w=cw, pp=pb, t=th:
                                nc.scalar.activation(
                                    t[:, 0:w], psum(pp)[:, 0:w], AF.Silu,
                                    bias=B1R[:, ee * NB + n:ee * NB + n + 1])),
                               reads=[f"PS{pb}", "B1R"], writes=[thk])
                        sch.op("vector",
                               (lambda n=nb, c=cc, w=cw, t=th:
                                nc.vector.tensor_mul(
                                    B1[:, n * M1 + c:n * M1 + c + w],
                                    t[:, 0:w], COMBB[:, c:c + w])),
                               reads=[thk, "COMBB"],
                               writes=kB1c(nb * M1 + cc, nb * M1 + cc + cw))
                    if nb + 2 < NB:
                        emit_gran(nb + 2)
                b2c_here = (e == 0)
                for ci, (cc, cw) in enumerate(mmch):
                    for db in range(KB):
                        pi = 2 + ((ci * KB + db) % 4)
                        for nb in range(NB):
                            last = (nb == NB - 1) and not b2c_here
                            sch.op("tensor",
                                   (lambda n=nb, d=db, c=cc, w=cw, p=pi,
                                           s=(nb == 0), e_=last:
                                    nc.tensor.matmul(
                                        psum(p)[:, 0:w], w2_lhsT(n, d),
                                        B1[:, n * M1 + c:n * M1 + c + w],
                                        start=s, stop=e_)),
                                   reads=kB1c(nb * M1 + cc, nb * M1 + cc + cw)
                                   + kw2(nb, db),
                                   writes=[f"PS{pi}"], inc=last)
                        if b2c_here:
                            sch.op("tensor",
                                   (lambda d=db, c=cc, w=cw, p=pi:
                                    nc.tensor.matmul(
                                        psum(p)[:, 0:w],
                                        B2B[0:8, d * 128:d * 128 + 128],
                                        COMB05[0:8, c:c + w],
                                        start=False, stop=True)),
                                   reads=["B2B", "COMB05"], writes=[f"PS{pi}"],
                                   inc=True)
                        sch.op("vector",
                               (lambda d=db, c=cc, w=cw, p=pi:
                                nc.vector.tensor_add(
                                    vXF(d, c, w), vXF(d, c, w),
                                    psum(p)[:, 0:w])),
                               reads=[f"PS{pi}"] + kXF(db, cc, cw),
                               writes=kXF(db, cc, cw))

        ffn(CH_FULL, CH_FULL, "f1")

        if "x1" in _TAPS:
            sch.dma("sync", lambda: nc.sync.dma_start(
                taps["x1"].rearrange("(b p) t -> p b t", p=128),
                XF[:].rearrange("p (b t) -> p b t", b=KB)),
                reads=[f"XF:{i}" for i in range(KB)])

        layer_norm(CH_FULL, False, apply_plain_s)
        if "xh" in _TAPS:
            sch.dma("sync", lambda: nc.sync.dma_start(taps["xh"], XN[:]),
                    reads=[f"XN:{i}" for i in range(KB)])
        flush_xf()
        sch.dma("sync", lambda: nc.sync.dma_start(d_xsp[:, :], XF[:]),
                reads=[f"XF:{i}" for i in range(KB)], writes=["dXSP"])
        layer_norm(CH_FULL, True, apply_internal)
        for blk in range(KB):
            sch.op("vector",
                   (lambda b=blk: nc.vector.tensor_mul(
                       XN[:, b * M1:b * M1 + 1], XN[:, b * M1:b * M1 + 1],
                       PV[:, 0:1])),
                   reads=kXN(blk, 0, 1) + ["PV"], writes=kXN(blk, 0, 1))
        for blk in range(KB):
            for h in range(2):
                sch.op("vector",
                       (lambda b=blk, o=h * 512: nc.vector.tensor_sub(
                           TSQ0[:, 0:512], vXN(b, 1 + o, 512), vXN(b, o, 512))),
                       reads=kXN(blk, h * 512, 513), writes=["TSQ0"])
                sch.op("vector",
                       (lambda b=blk, o=h * 512: nc.vector.scalar_tensor_tensor(
                           B1[:, b * M + o:b * M + o + 512], TSQ0[:, 0:512],
                           VEC[:, 2 * KB + b:2 * KB + b + 1], vXN(b, o, 512),
                           ALU.mult, ALU.add)),
                       reads=["TSQ0", "VEC"] + kXN(blk, h * 512, 512),
                       writes=kB1c(blk * M + h * 512, blk * M + h * 512 + 512))
                sch.op("vector",
                       (lambda b=blk, o=h * 512: nc.vector.scalar_tensor_tensor(
                           B2[:, b * M + o:b * M + o + 512], TSQ0[:, 0:512],
                           VEC[:, 3 * KB + b:3 * KB + b + 1], vXN(b, o, 512),
                           ALU.mult, ALU.add)),
                       reads=["TSQ0", "VEC"] + kXN(blk, h * 512, 512),
                       writes=kB2c(blk * M + h * 512, blk * M + h * 512 + 512))

        def proj_phase(d_w, gslot, out_emit, rhs_B1):
            off = gslot * D
            for db in range(KB):
                sch.dma("gpsimd",
                        (lambda d=db, o=off: nc.gpsimd.dma_start(
                            W2X[:, o:o + D].rearrange("p (b n) -> p b n", b=KB),
                            d_w[:, d * 128:(d + 1) * 128]
                            .rearrange("(b p) n -> p b n", p=128))),
                        writes=kW2X(off, off + D))
                for mc in range(2):
                    p = (gslot * 4) + (db % 2) * 2 + mc
                    srcb = B1 if rhs_B1 else B2
                    ksrc = kB1c if rhs_B1 else kB2c
                    for blk in range(KB):
                        sch.op("tensor",
                               (lambda b=blk, m=mc, pp=p, o=off, s_=srcb,
                                       s=(blk == 0), e_=(blk == KB - 1):
                                nc.tensor.matmul(
                                    psum(pp)[:, 0:512],
                                    W2X[:, o + b * 128:o + b * 128 + 128],
                                    s_[:, b * M + m * 512:b * M + m * 512 + 512],
                                    start=s, stop=e_)),
                               reads=ksrc(blk * M + mc * 512,
                                          blk * M + mc * 512 + 512)
                               + kW2X(off, off + D),
                               writes=[f"PS{p}"], inc=(blk == KB - 1))
                    out_emit(db, mc, p)

        def k_out(db, mc, p):
            sch.op("scalar",
                   (lambda d=db, m=mc, pp=p: nc.scalar.activation(
                       vEK(d, m * 512, 512), psum(pp)[:, 0:512], AF.Exp)),
                   reads=[f"PS{p}"], writes=kEK(db, mc * 512, 512))

        def v_out(db, mc, p):
            sch.op("vector",
                   (lambda d=db, m=mc, pp=p: nc.vector.tensor_mul(
                       vZ(d, m * 512, 512), psum(pp)[:, 0:512],
                       vEK(d, m * 512, 512))),
                   reads=[f"PS{p}"] + kEK(db, mc * 512, 512),
                   writes=kZ(db, mc * 512, 512))

        if "xn2" in _TAPS:
            sch.dma("sync", lambda: nc.sync.dma_start(taps["xn2"], XN[:]),
                    reads=[f"XN:{i}" for i in range(KB)])
        proj_phase(d_wk, 0, k_out, rhs_B1=True)
        for blk in range(KB):
            for h in range(2):
                sch.op("vector",
                       (lambda b=blk, o=h * 512: nc.vector.tensor_sub(
                           TSQ0[:, 0:512], vXN(b, 1 + o, 512), vXN(b, o, 512))),
                       reads=kXN(blk, h * 512, 513), writes=["TSQ0"])
                sch.op("vector",
                       (lambda b=blk, o=h * 512: nc.vector.scalar_tensor_tensor(
                           B1[:, b * M + o:b * M + o + 512], TSQ0[:, 0:512],
                           VEC[:, 4 * KB + b:4 * KB + b + 1], vXN(b, o, 512),
                           ALU.mult, ALU.add)),
                       reads=["TSQ0", "VEC"] + kXN(blk, h * 512, 512),
                       writes=kB1c(blk * M + h * 512, blk * M + h * 512 + 512))
        flush_b1()
        proj_phase(d_wv, 1, v_out, rhs_B1=False)


        def scr_a(par):
            return B2[:, par * 2050:par * 2050 + 1025]

        def scr_b(par):
            return B2[:, par * 2050 + 1025:par * 2050 + 2050]

        def kscr(par):
            return kB2c(par * 2050, par * 2050 + 2050)

        for db in range(KB):
            par = db % 2
            sch.op("vector",
                   (lambda d=db, p=par: nc.vector.tensor_tensor_scan(
                       scr_a(p)[:, 0:M],
                       DD[:, d:d + 1].broadcast_to([128, M]),
                       vZ(d, 0, M), 0.0, ALU.mult, ALU.add)),
                   reads=kZ(db, 0, M) + ["DD"], writes=kscr(par)[:1])
            sch.op("vector",
                   (lambda d=db, p=par: nc.vector.tensor_tensor_scan(
                       scr_b(p)[:, 0:M],
                       DD[:, d:d + 1].broadcast_to([128, M]),
                       vEK(d, 0, M), 0.0, ALU.mult, ALU.add)),
                   reads=kEK(db, 0, M) + ["DD"], writes=kscr(par)[1:])
            sch.op("vector",
                   (lambda d=db, p=par: nc.vector.tensor_copy(
                       AGP[:, d:d + 1], scr_a(p)[:, M - 1:M])),
                   reads=kscr(par)[:1], writes=["AGP"])
            sch.op("vector",
                   (lambda d=db, p=par: nc.vector.tensor_copy(
                       AGP[:, KB + d:KB + d + 1], scr_b(p)[:, M - 1:M])),
                   reads=kscr(par)[1:], writes=["AGP"])
        sch.op("scalar", lambda: nc.scalar.copy(AGP[:], AGP[:]),
               reads=["AGP"], writes=["AGP"])
        sch.dma("gpsimd", lambda: nc.gpsimd.dma_start(d_agi[:, :], AGP[:]),
                reads=["AGP"], writes=["dAGI"])
        sch.collective(lambda: nc.gpsimd.collective_compute(
            "AllGather", ALU.bypass,
            replica_groups=[list(range(NCORES))],
            ins=[d_agi.ap().opt()],
            outs=[d_ago.ap().opt()]),
            reads=["dAGI"], writes=["dAGO"])
        sch.dma("gpsimd", lambda: nc.gpsimd.dma_start(
            AGALL[:].rearrange("p (r c) -> p r c", r=NCORES),
            d_ago.ap().rearrange("(r p) c -> p r c", p=128)),
            reads=["dAGO"], writes=["AGALL"])
        sch.op("vector", lambda: nc.vector.memset(AIN[:], 0.0), writes=["AIN"])
        for r in range(NCORES):
            sch.op("vector",
                   (lambda rr=r: nc.vector.scalar_tensor_tensor(
                       AIN[:], AGALL[:, rr * 2 * KB:(rr + 1) * 2 * KB],
                       SEL[:, rr:rr + 1], AIN[:], ALU.mult, ALU.add)),
                   reads=["AGALL", "SEL", "AIN"], writes=["AIN"])


        def r_out(db, mc, p):
            sch.op("scalar",
                   (lambda d=db, m=mc, pp=p: nc.scalar.activation(
                       XN[:, d * M1 + m * 512:d * M1 + m * 512 + 512],
                       psum(pp)[:, 0:512], AF.Sigmoid)),
                   reads=[f"PS{p}"],
                   writes=kXN(db, mc * 512, 512))

        restage_b1()
        proj_phase(d_wr, 0, r_out, rhs_B1=True)

        for db in range(KB):
            par = db % 2
            sch.op("vector",
                   (lambda d=db, p=par: nc.vector.tensor_copy(
                       scr_a(p)[:, 0:1], AIN[:, d:d + 1])),
                   reads=["AIN"], writes=kscr(par)[:1])
            sch.op("vector",
                   (lambda d=db, p=par: nc.vector.tensor_copy(
                       scr_b(p)[:, 0:1], AIN[:, KB + d:KB + d + 1])),
                   reads=["AIN"], writes=kscr(par)[1:])
            sch.op("vector",
                   (lambda d=db, p=par: nc.vector.tensor_tensor_scan(
                       scr_a(p)[:, 1:1 + M],
                       DD[:, d:d + 1].broadcast_to([128, M]),
                       vZ(d, 0, M), AIN[:, d:d + 1], ALU.mult, ALU.add)),
                   reads=kZ(db, 0, M) + ["DD", "AIN"], writes=kscr(par)[:1])
            sch.op("vector",
                   (lambda d=db, p=par: nc.vector.tensor_tensor_scan(
                       scr_b(p)[:, 1:1 + M],
                       DD[:, d:d + 1].broadcast_to([128, M]),
                       vEK(d, 0, M), AIN[:, KB + d:KB + d + 1],
                       ALU.mult, ALU.add)),
                   reads=kEK(db, 0, M) + ["DD", "AIN"], writes=kscr(par)[1:])
            for h in range(2):
                o = h * 512
                sch.op("vector",
                       (lambda d=db, oo=o, p=par: nc.vector.scalar_tensor_tensor(
                           TH0[:, 0:512], vZ(d, oo, 512), EU[:, d:d + 1],
                           scr_a(p)[:, oo:oo + 512], ALU.mult, ALU.add)),
                       reads=kZ(db, o, 512) + ["EU"] + kscr(par)[:1],
                       writes=["TH0"])
                sch.op("vector",
                       (lambda d=db, oo=o, p=par: nc.vector.scalar_tensor_tensor(
                           TH1[:, 0:512], vEK(d, oo, 512), EU[:, d:d + 1],
                           scr_b(p)[:, oo:oo + 512], ALU.mult, ALU.add)),
                       reads=kEK(db, o, 512) + ["EU"] + kscr(par)[1:],
                       writes=["TH1"])
                sch.op("scalar",
                       (lambda: nc.scalar.activation(
                           PSB[0][:, 0:512], TH1[:, 0:512], AF.Ln)),
                       reads=["TH1"], writes=["PS0"])
                sch.op("scalar",
                       (lambda: nc.scalar.activation(
                           TH1[:, 0:512], PSB[0][:, 0:512], AF.Exp,
                           scale=-1.0)),
                       reads=["PS0"], writes=["TH1"])
                sch.op("vector",
                       (lambda d=db, oo=o: nc.vector.tensor_mul(
                           vZ(d, oo, 512), TH0[:, 0:512], TH1[:, 0:512])),
                       reads=["TH0", "TH1"], writes=kZ(db, o, 512))
                sch.op("vector",
                       (lambda d=db, oo=o: nc.vector.tensor_mul(
                           B1[:, d * M + oo:d * M + oo + 512],
                           XN[:, d * M1 + oo:d * M1 + oo + 512],
                           vZ(d, oo, 512))),
                       reads=kXN(db, o, 512) + kZ(db, o, 512),
                       writes=kB1c(db * M + o, db * M + o + 512))

        sch.dma("sync", lambda: nc.sync.dma_start(XF[:], d_xsp[:, :]),
                reads=["dXSP"], writes=[f"XF:{i}" for i in range(KB)])

        restage_b1()

        def o_out(db, mc, p):
            cc = 1 + mc * 512
            sch.op("vector",
                   (lambda d=db, c=cc, pp=p: nc.vector.tensor_add(
                       vXF(d, c, 512), vXF(d, c, 512), psum(pp)[:, 0:512])),
                   reads=[f"PS{p}"] + kXF(db, cc, 512),
                   writes=kXF(db, cc, 512))

        proj_phase(d_wo, 1, o_out, rhs_B1=True)

        if "x2" in _TAPS:
            sch.dma("sync", lambda: nc.sync.dma_start(
                taps["x2"].rearrange("(b p) t -> p b t", p=128),
                XF[:].rearrange("p (b t) -> p b t", b=KB)),
                reads=[f"XF:{i}" for i in range(KB)])

        ffn(CH_OWN, CH_OWN, "f2")

        flush_xf()
        sch.dma("sync", lambda: nc.sync.dma_start(
            d_y.rearrange("(b p) t -> p b t", p=128),
            XF[:].rearrange("p (b t) -> p b t", b=KB)[:, :, 1:M1]),
            reads=[f"XF:{i}" for i in range(KB)])

        sch.final_waits("sync")
        sch.final_waits("gpsimd")
        sch.replay(block, esems, qsems, ccsem, verify=verify)
        build_program.last_sched = sch

    return nc


def _stage_vec(v):
    return np.ascontiguousarray(v.reshape(KB, 128).T)


def kernel(**inputs):
    x = np.asarray(inputs["L0_attn_x"], np.float32)
    Wk = np.ascontiguousarray(np.asarray(inputs["L0_attn_Wk"], np.float32))
    Wv = np.ascontiguousarray(np.asarray(inputs["L0_attn_Wv"], np.float32))
    Wr = np.ascontiguousarray(np.asarray(inputs["L0_attn_Wr"], np.float32))
    Wo = np.ascontiguousarray(np.asarray(inputs["L0_attn_Wo"], np.float32))
    w1 = np.asarray(inputs["L0_ffn_w1"], np.float32)
    w2 = np.asarray(inputs["L0_ffn_w2"], np.float32)
    gate_w = np.asarray(inputs["L0_ffn_gate_w"], np.float32)
    gate_b = np.asarray(inputs["L0_ffn_gate_b"], np.float32)
    bias = np.asarray(inputs["L0_ffn_bias"], np.float32)
    b1 = np.asarray(inputs["L0_ffn_b1"], np.float32)
    b2 = np.asarray(inputs["L0_ffn_b2"], np.float32)

    w1T = np.ascontiguousarray(w1.transpose(0, 2, 1))
    w2T = np.ascontiguousarray(w2.transpose(0, 2, 1))
    gwT = np.ascontiguousarray(gate_w.T)
    gb2 = np.ascontiguousarray(np.stack([gate_b, bias], axis=1))
    gbr = np.ascontiguousarray(
        np.concatenate([gate_b, bias]).reshape(1, 2 * E))
    b1r = np.ascontiguousarray(
        b1.reshape(E, NB, 128).transpose(2, 0, 1).reshape(128, E * NB))
    vecs = np.concatenate([
        _stage_vec(np.asarray(inputs["L0_attn_ln_scale"], np.float32)),
        _stage_vec(np.asarray(inputs["L0_attn_ln_bias"], np.float32)),
        _stage_vec(np.asarray(inputs["L0_attn_mix_k"], np.float32)),
        _stage_vec(np.asarray(inputs["L0_attn_mix_v"], np.float32)),
        _stage_vec(np.asarray(inputs["L0_attn_mix_r"], np.float32)),
        _stage_vec(np.asarray(inputs["L0_attn_time_first"], np.float32)),
        _stage_vec(np.asarray(inputs["L0_attn_time_decay"], np.float32)),
    ], axis=1)
    idm = np.eye(128, dtype=np.float32)

    xt = np.ascontiguousarray(x[0].T)

    in_maps = []
    for r in range(NCORES):
        lo = r * M
        if r == 0:
            xs = np.concatenate(
                [np.zeros((D, 1), np.float32), xt[:, 0:M]], axis=1)
        else:
            xs = xt[:, lo - 1:lo + M]
        sel = np.zeros((128, NCORES), np.float32)
        if r > 0:
            sel[:, r - 1] = 1.0
        pv = np.full((128, 1), 0.0 if r == 0 else 1.0, np.float32)
        in_maps.append(dict(
            xT=np.ascontiguousarray(xs), w1T=w1T, w2T=w2T,
            Wk=Wk, Wv=Wv, Wr=Wr, Wo=Wo, gwT=gwT, gb2=gb2, gbr=gbr, vecs=vecs,
            b1r=b1r, b2=np.ascontiguousarray(b2), id128=idm, sel=sel, pv=pv,
        ))

    nc = build_program()
    res = run_bass_kernel_spmd(nc, in_maps, core_ids=list(range(NCORES)))

    out = np.empty((1, T, D), np.float32)
    for r in range(NCORES):
        out[0, r * M:(r + 1) * M, :] = res.results[r]["y"].T
    if _TAPS:
        kernel.taps = [{k: res.results[r]["tap_" + k] for k in _TAPS}
                       for r in range(NCORES)]
    return out
